# version 02
# baseline (speedup 1.0000x reference)
import numpy as np
import ml_dtypes
from contextlib import ExitStack

import concourse.bass as bass
import concourse.mybir as mybir
from concourse.bass_utils import run_bass_kernel_spmd

F32 = mybir.dt.float32
BF16 = mybir.dt.bfloat16
AL = mybir.AluOpType
AF = mybir.ActivationFunctionType
AX = mybir.AxisListType

T = 4096
D = 1024
NTT = T // 128
BF = np.dtype(ml_dtypes.bfloat16)

SELU_L = 1.0507009873554804934193349852946
SELU_A = 1.6732632423543772848170429916717

BLOCKS = [0, 1, 2, 3]
FUSED = False
DO_ATTN = True
DO_FFN = True
L0_DENSE = True
L0_PEER = True
L0_PSTG = 99
L0_CONSV = True
LN_LIMIT = 2
DBG = False


def _packA(W, of_tile=128):
    IF, OF = W.shape
    return np.ascontiguousarray(
        W.reshape(IF // 128, 128, OF // of_tile, of_tile)
         .transpose(2, 1, 0, 3)
         .reshape(OF // of_tile, 128, (IF // 128) * of_tile))


def _col128(v):
    v = np.asarray(v, np.float32).reshape(-1)
    return np.ascontiguousarray(v.reshape(v.shape[0] // 128, 128).T)


def _row(v):
    return np.ascontiguousarray(np.asarray(v, np.float32).reshape(1, -1))


def host_prepare(inputs):
    g = {}
    f32 = lambda a: np.ascontiguousarray(np.asarray(a, dtype=np.float32))

    g["c_ident"] = np.eye(128, dtype=BF)
    g["c_ones1"] = np.ones((1, 128), dtype=BF)

    g["l0_cdA"] = _packA(f32(inputs["L0_attn_Wc_down"]))
    g["l0_quA"] = _packA(f32(inputs["L0_attn_Wq_up"]))
    g["l0_kA"] = _packA(np.ascontiguousarray(f32(inputs["L0_attn_Wkv_up"])[:, :64]), of_tile=64)
    g["l0_v"] = np.ascontiguousarray(f32(inputs["L0_attn_Wkv_up"])[:, 64:])
    g["l0_cp"] = f32(inputs["L0_attn_Wc_proj"])

    qswp = np.arange(1024).reshape(8, 2, 64)[:, ::-1].reshape(-1)
    g["l0_quAs"] = _packA(np.ascontiguousarray(f32(inputs["L0_attn_Wq_up"])[:, qswp]))
    g["l0_upA"] = _packA(f32(inputs["L0_ffn_W_up"]).T)
    g["l0_dn"] = np.ascontiguousarray(f32(inputs["L0_ffn_W_down"]).T)
    g["l0_qA"] = _packA(f32(inputs["L0_ffn_W_q"]).T)
    keys = f32(inputs["L0_ffn_keys"])
    karr = keys.transpose(2, 0, 3, 1).reshape(8, 64, 64)
    kdup = np.concatenate([karr, karr], axis=1)
    g["l0_keys"] = np.ascontiguousarray(
        kdup.transpose(1, 0, 2).reshape(128, 512).astype(BF))
    g["l0_deT"] = np.ascontiguousarray(f32(inputs["L0_ffn_down_embed"]).T)
    g["l0_ue"] = f32(inputs["L0_ffn_up_embed"])

    Wqk = f32(inputs["L1_attn_W_qk"])
    bqk = f32(inputs["L1_attn_b_qk"])
    qr = np.empty(1024, np.int64); kr = np.empty(1024, np.int64)
    for h in range(16):
        for d in range(64):
            qr[h * 64 + d] = h * 128 + 2 * d
            kr[h * 64 + d] = h * 128 + 2 * d + 1
    g["l1_qA"] = _packA(np.ascontiguousarray(Wqk[qr, :].T), of_tile=64)
    g["l1_kA"] = _packA(np.ascontiguousarray(Wqk[kr, :].T), of_tile=64)
    g["l1_bqc"] = _col128(np.pad(bqk[qr].reshape(16, 64), ((0, 0), (0, 64))).reshape(-1))
    g["l1_bkc"] = _col128(np.pad(bqk[kr].reshape(16, 64), ((0, 0), (0, 64))).reshape(-1))

    g["l1_wr"] = f32(inputs["L1_ffn_Wr"])
    g["l1_brr"] = _row(inputs["L1_ffn_br"])
    W1 = f32(inputs["L1_ffn_W1"]); W2 = f32(inputs["L1_ffn_W2"])
    g["l1_w1A"] = np.ascontiguousarray(np.stack([_packA(W1[e]) for e in range(8)]))
    g["l1_w2"] = W2
    b1 = f32(inputs["L1_ffn_b1"]); b2 = f32(inputs["L1_ffn_b2"])
    g["l1_b1c"] = np.ascontiguousarray(np.stack([_col128(b1[e]) for e in range(8)]))
    g["l1_b2r"] = np.ascontiguousarray(np.stack([_row(b2[e]) for e in range(8)]))

    g["l2_qA"] = _packA(f32(inputs["L2_attn_Wq"]))
    g["l2_kA"] = _packA(f32(inputs["L2_attn_Wk"]))
    qswp = np.arange(1024).reshape(8, 2, 64)[:, ::-1].reshape(-1)
    g["l2_qAs"] = _packA(np.ascontiguousarray(f32(inputs["L2_attn_Wq"])[:, qswp]))
    g["l2_bqcs"] = _col128(np.asarray(inputs["L2_attn_bq"], np.float32)[qswp])
    g["l2_v"] = f32(inputs["L2_attn_Wv"])
    perm = np.empty(1024, np.int64)
    for h in range(16):
        for d in range(64):
            perm[h * 64 + d] = (h % 4) * 256 + (h // 4) * 64 + d
    g["l2_wo"] = np.ascontiguousarray(f32(inputs["L2_attn_Wo"])[perm, :])
    g["l2_bqc"] = _col128(inputs["L2_attn_bq"])
    g["l2_bkc"] = _col128(inputs["L2_attn_bk"])
    g["l2_bvr"] = _row(inputs["L2_attn_bv"])
    g["l2_bor"] = _row(inputs["L2_attn_bo"])

    g["l3_qA"] = _packA(np.ascontiguousarray(f32(inputs["L3_attn_Wq"]).T))
    g["l3_kA"] = _packA(np.ascontiguousarray(f32(inputs["L3_attn_Wk"]).T))
    g["l3_v"] = np.ascontiguousarray(f32(inputs["L3_attn_Wv"]).T)
    g["l3_wu"] = np.ascontiguousarray(f32(inputs["L3_attn_Wu"]).T)
    g["l3_bur"] = _row(inputs["L3_attn_bu"])
    half = 512
    theta = 1.0 / (10000.0 ** (np.arange(half, dtype=np.float64) / half))
    ang = np.arange(T, dtype=np.float64)[:, None] * theta[None, :]
    cosT = np.cos(ang).astype(np.float32).T
    sinT = np.sin(ang).astype(np.float32).T
    ctab = np.empty((1024, T), np.float32); stab = np.empty((1024, T), np.float32)
    ctab[0::2] = cosT; ctab[1::2] = cosT
    stab[0::2] = -sinT; stab[1::2] = sinT
    g["l3_cos"] = ctab; g["l3_sin2"] = stab
    psw = np.zeros((128, 128), dtype=BF)
    for p in range(128):
        psw[p ^ 1, p] = 1.0
    g["c_psw"] = psw

    for L in (2, 3):
        p, o = f"L{L}_ffn_", f"t{L}_"
        g[o + "w1A"] = _packA(f32(inputs[p + "w1_w"]).T)
        g[o + "w2A"] = _packA(f32(inputs[p + "w2_w"]).T)
        g[o + "h1A"] = _packA(f32(inputs[p + "h1_w"]).T)
        g[o + "h2A"] = _packA(f32(inputs[p + "h2_w"]).T)
        g[o + "w3A"] = _packA(f32(inputs[p + "w3_w"]).T)
        g[o + "s1c"] = _col128(inputs[p + "w1_s"])
        g[o + "s2c"] = _col128(inputs[p + "w2_s"])
        g[o + "h1sc"] = _col128(inputs[p + "h1_s"])
        g[o + "h2sc"] = _col128(inputs[p + "h2_s"])
        g[o + "s3c"] = _col128(inputs[p + "w3_s"])
    return g



class _DummyInst:
    def then_inc(self, *a, **k):
        return self


_DI = _DummyInst()


class _DummyEng:
    def __getattr__(self, name):
        def f(*a, **k):
            return _DI
        return f


_DE = _DummyEng()

ROLE_ATTR = {"tensor": "te", "vector": "ve", "scalar": "se",
             "gpsimd": "gp", "sync": "sy"}


class Ctx:
    def __init__(self, nc, sems):
        self.nc = nc
        self.sems = sems

    def begin(self, role, eng):
        self.role = role
        self.cnt = dict((s, 0) for s in self.sems)
        for r, at in ROLE_ATTR.items():
            setattr(self, at, eng if r == role else _DE)

    def step(self, instr, sem, n=1):
        self.cnt[sem] += n
        if instr is not _DI and instr is not None:
            instr.then_inc(self.sems[sem], n)
        return self.cnt[sem]

    def wait(self, eng, sem, val=None):
        v = self.cnt[sem] if val is None else val
        if v > 0:
            eng.wait_ge(self.sems[sem], v)

    def dma(self, q, out, in_, sem):
        eng = self.gp if q == "gp" else self.sy
        ins = eng.dma_start(out=out, in_=in_)
        return self.step(ins, sem, 16)


def barrier(ctx, engs, sems):
    for en in engs:
        e = getattr(ctx, en)
        for sm in sems:
            ctx.wait(e, sm)


def chain(ctx, instr, sem, eng_at):
    v = ctx.step(instr, sem)
    ctx.wait(getattr(ctx, eng_at), sem, v)
    return v


class WStream:

    def __init__(self, ctx, buf, cols):
        self.ctx, self.buf, self.cols = ctx, buf, cols
        self.n = 0

    def load(self, src_ap, free_after=None):
        ctx = self.ctx
        if free_after is not None:
            ctx.wait(ctx.gp, "te", free_after)
        i = self.n
        b = (i % 2) * self.cols
        w = src_ap.shape[-1] if len(src_ap.shape) == 2 else self.cols
        wq = ctx.dma("gp", self.buf[:, b:b + w], src_ap, "wq")
        self.n += 1
        return i, wq

    def slot(self, i):
        b = (i % 2) * self.cols
        return self.buf[:, b:b + self.cols]


AR_COLS = 69 * 1024


def build_program(blocks, shared):
    nc = bass.Bass("TRN2", target_bir_lowering=False)
    A = {}
    A["x_in"] = nc.dram_tensor("x_in", [T, D], F32, kind="ExternalInput").ap()
    A["x_out"] = nc.dram_tensor("x_out", [T, D], F32, kind="ExternalOutput").ap()
    for name, arr in shared.items():
        A[name] = nc.dram_tensor(name, list(arr.shape),
                                 BF16 if arr.dtype == BF else F32,
                                 kind="ExternalInput").ap()
    A["xnsp"] = nc.dram_tensor("xnsp", [128, 8 * T], BF16, kind="Internal").ap()
    if 3 in blocks:
        A["l3qk"] = nc.dram_tensor("l3qk", [2, 128, 8 * T], BF16, kind="Internal").ap()
        A["l3v"] = nc.dram_tensor("l3v", [16, 32, 128, 65], BF16, kind="Internal").ap()
    if DBG:
        A["dbg"] = nc.dram_tensor("dbg", [128, 512], F32, kind="ExternalOutput").ap()
        A["dbg_xn"] = nc.dram_tensor("dbg_xn", [128, 32768], BF16, kind="ExternalOutput").ap()
        A["dbg_kt"] = nc.dram_tensor("dbg_kt", [128, 8192], BF16, kind="ExternalOutput").ap()

    with ExitStack() as st:
        S = {}
        S["X"] = st.enter_context(nc.sbuf_tensor("X", [128, NTT * 1024], BF16))
        S["AR"] = st.enter_context(nc.sbuf_tensor("AR", [128, AR_COLS - (1024 if DBG else 0)], BF16))
        S["AF"] = st.enter_context(nc.sbuf_tensor("AFX", [128, 1024], F32))
        S["CID"] = st.enter_context(nc.sbuf_tensor("CID", [128, 128], BF16))
        S["PSW"] = st.enter_context(nc.sbuf_tensor("PSW", [128, 128], BF16))
        S["ONE"] = st.enter_context(nc.sbuf_tensor("ONEX", [1, 128], BF16))
        S["STAT"] = st.enter_context(nc.sbuf_tensor("STAT", [128, 128], F32))
        if DBG:
            S["DBG"] = st.enter_context(nc.sbuf_tensor("DBGT", [128, 512], F32))
        P = {}
        P["A"] = st.enter_context(nc.psum_tensor("PSA", [128, 1024], F32))
        P["B"] = st.enter_context(nc.psum_tensor("PSB", [128, 1024], F32))
        P["C"] = st.enter_context(nc.psum_tensor("PSC", [128, 512], F32))
        P["D"] = st.enter_context(nc.psum_tensor("PSD", [128, 512], F32))
        P["T"] = st.enter_context(nc.psum_tensor("PST", [128, 2048], BF16))

        sems = dict((n, st.enter_context(nc.semaphore(n)))
                    for n in ("wq", "xq", "te", "ve", "se", "gq", "st", "so",
                              "w0", "w1", "w2", "w3"))
        block = st.enter_context(nc.Block())
        ctx = Ctx(nc, sems)

        def program(ctx):
            emit_prologue(ctx, A, S, P)
            for li in blocks:
                emit_ln_xn(ctx, A, S, P)
                if DO_ATTN:
                    ATTN[li](ctx, A, S, P)
                if DO_FFN:
                    FFN[li](ctx, A, S, P)
            emit_epilogue(ctx, A, S, P)

        @block.tensor
        def _(e):
            ctx.begin("tensor", e); program(ctx)

        @block.vector
        def _(e):
            ctx.begin("vector", e); program(ctx)

        @block.scalar
        def _(e):
            ctx.begin("scalar", e); program(ctx)

        @block.gpsimd
        def _(e):
            ctx.begin("gpsimd", e); program(ctx)

        @block.sync
        def _(e):
            ctx.begin("sync", e); program(ctx)

    return nc


def emit_prologue(ctx, A, S, P):
    if DBG:
        ctx.step(ctx.ve.memset(S["DBG"][:], 0.0), "ve")
    ctx.dma("sy", S["CID"][:], A["c_ident"], "xq")
    ctx.dma("sy", S["PSW"][:], A["c_psw"], "xq")
    ctx.dma("sy", S["ONE"][:], A["c_ones1"], "xq")
    src = A["x_in"].rearrange("(t p) d -> p t d", p=128)
    dst = S["X"][:].rearrange("p (t d) -> p t d", d=1024)
    ctx.dma("gp", dst, src, "wq")
    for en in ("te", "ve", "se"):
        e = getattr(ctx, en)
        ctx.wait(e, "xq")
        ctx.wait(e, "wq")


def emit_epilogue(ctx, A, S, P):
    dst = A["x_out"].rearrange("(t p) d -> p t d", p=128)
    src = S["X"][:].rearrange("p (t d) -> p t d", d=1024)
    ctx.wait(ctx.gp, "ve")
    ctx.wait(ctx.gp, "se")
    ctx.dma("gp", dst, src, "so")
    if DBG:
        ctx.dma("gp", A["dbg"], S["DBG"][:], "so")
    ctx.wait(ctx.gp, "so")



def emit_ln_xn(ctx, A, S, P):
    X, ST, AR = S["X"], S["STAT"], S["AR"]
    XN = AR[:, 0:NTT * 1024]
    jnk = AR[:, 64 * 1024:65 * 1024]
    barrier(ctx, engs=("ve", "se"), sems=("ve", "se", "te"))
    for t in range(NTT):
        xt = X[:, t * 1024:(t + 1) * 1024]
        ctx.step(ctx.ve.reduce_sum(ST[:, t:t + 1], xt, axis=AX.X), "ve")
        ctx.step(ctx.se.activation(jnk, xt, AF.Square,
                                   accum_out=ST[:, 32 + t:33 + t]), "se")
    if LN_LIMIT < 1:
        barrier(ctx, engs=("te", "gp", "sy"), sems=("ve", "se"))
        return
    ctx.wait(ctx.ve, "se")
    ctx.wait(ctx.ve, "ve")
    chain(ctx, ctx.ve.tensor_scalar(ST[:, 0:32], ST[:, 0:32], 1.0 / 1024.0, None, AL.mult), "ve", "ve")
    chain(ctx, ctx.ve.tensor_tensor(ST[:, 96:128], ST[:, 0:32], ST[:, 0:32], AL.mult), "ve", "ve")
    chain(ctx, ctx.ve.scalar_tensor_tensor(ST[:, 32:64], ST[:, 32:64], 1.0 / 1024.0,
                                           ST[:, 96:128], AL.mult, AL.subtract), "ve", "ve")
    chain(ctx, ctx.ve.tensor_scalar(ST[:, 32:64], ST[:, 32:64], 1e-5, None, AL.add), "ve", "ve")
    ctx.wait(ctx.se, "ve")
    chain(ctx, ctx.se.activation(ST[:, 64:96], ST[:, 32:64], AF.Sqrt), "se", "se")
    ctx.wait(ctx.ve, "se")
    chain(ctx, ctx.ve.reciprocal(ST[:, 64:96], ST[:, 64:96]), "ve", "ve")
    chain(ctx, ctx.ve.scalar_tensor_tensor(ST[:, 96:128], ST[:, 0:32], -1.0,
                                           ST[:, 64:96], AL.mult, AL.mult), "ve", "ve")
    ctx.wait(ctx.se, "ve")
    if LN_LIMIT < 2:
        barrier(ctx, engs=("te", "se", "gp", "sy"), sems=("ve",))
        return
    xp = [AR[:, 65 * 1024:66 * 1024], AR[:, 66 * 1024:67 * 1024]]
    te_tile = {}
    ve_tile = {}
    for t in range(NTT):
        b = t % 2
        xt = X[:, t * 1024:(t + 1) * 1024]
        if t >= 2:
            ctx.wait(ctx.se, "te", te_tile[t - 2])
        sev = ctx.step(ctx.se.activation(xp[b], xt, AF.Identity,
                                         bias=ST[:, 96 + t:97 + t],
                                         scale=ST[:, 64 + t:65 + t]), "se")
        ctx.wait(ctx.te, "se", sev)
        if t >= 2:
            ctx.wait(ctx.te, "ve", ve_tile[t - 2])
        for c in range(8):
            tr = ctx.te.transpose(P["T"][:, b * 1024 + c * 128: b * 1024 + (c + 1) * 128],
                                  xp[b][:, c * 128:(c + 1) * 128], S["CID"][:])
        te_tile[t] = ctx.step(tr, "te")
        ctx.wait(ctx.ve, "te", te_tile[t])
        for c in range(8):
            cp = ctx.ve.tensor_copy(
                XN[:, c * 4096 + t * 128: c * 4096 + (t + 1) * 128],
                P["T"][:, b * 1024 + c * 128: b * 1024 + (c + 1) * 128])
        ve_tile[t] = ctx.step(cp, "ve")
    barrier(ctx, engs=("te", "se", "gp", "sy"), sems=("ve",))



def make_ffn_ternary(li):
    o = f"t{li}_"

    def emit(ctx, A, S, P):
        X, AR, AFX = S["X"], S["AR"], S["AF"]
        XN = AR[:, 0:32 * 1024]
        aoff, boff = 8192, 40960
        woff = 65536
        joff = 65536 + 4096

        def a_chunk(c):
            return AR[:, aoff + c * 1024: aoff + (c + 1) * 1024]

        def b_chunk(c):
            if c < 24:
                return AR[:, boff + c * 1024: boff + (c + 1) * 1024]
            return AR[:, (c - 24) * 1024: (c - 23) * 1024]

        jnk = AR[:, joff:joff + 1024]
        wbuf1 = WStream(ctx, AR[:, woff:woff + 2048], 1024)
        wbuf = WStream(ctx, AR[:, woff:woff + 4096], 2048)

        barrier(ctx, engs=("sy",), sems=("te", "ve", "se"))
        ctx.dma("sy", A["xnsp"][:, :], XN, "st")
        spilled = ctx.cnt["st"]
        for en in ("te", "ve", "se", "gp"):
            ctx.wait(getattr(ctx, en), "st", spilled)

        s1c, s2c = AFX[:, 0:32], AFX[:, 32:64]
        h1c, h2c = AFX[:, 64:96], AFX[:, 96:128]
        s3c = AFX[:, 128:136]
        for nm, dst in ((o + "s1c", s1c), (o + "s2c", s2c), (o + "h1sc", h1c),
                        (o + "h2sc", h2c), (o + "s3c", s3c)):
            ctx.dma("sy", dst, A[nm], "xq")
        for en in ("se", "ve"):
            ctx.wait(getattr(ctx, en), "xq")

        te_free = None
        for q in range(4):
            xnq_src = A["xnsp"].rearrange("p (c t) -> p c t", t=T)[:, :, q * 1024:(q + 1) * 1024]
            xnq_dst = AR[:, 0:8192].rearrange("p (c t) -> p c t", t=1024)
            if q > 0:
                ctx.wait(ctx.sy, "te")
            xq_v = ctx.dma("sy", xnq_dst, xnq_src, "xq")
            ctx.wait(ctx.te, "xq", xq_v)

            def XNq(c, ts):
                return AR[:, c * 1024 + ts.start: c * 1024 + ts.stop]

            for oc in range(32):
                i1, v1 = wbuf1.load(A[o + "w1A"][oc], free_after=te_free)
                i2, v2 = wbuf1.load(A[o + "w2A"][oc], free_after=te_free)
                ctx.wait(ctx.te, "wq", v1)
                for hh in range(2):
                    ts = slice(hh * 512, (hh + 1) * 512)
                    for kc in range(8):
                        mm = ctx.te.matmul(P["A"][:, ts],
                                           wbuf1.slot(i1)[:, kc * 128:(kc + 1) * 128],
                                           XNq(kc, ts), start=(kc == 0), stop=(kc == 7))
                ctx.step(mm, "te")
                ctx.wait(ctx.te, "wq", v2)
                for hh in range(2):
                    ts = slice(hh * 512, (hh + 1) * 512)
                    for kc in range(8):
                        mm = ctx.te.matmul(P["B"][:, ts],
                                           wbuf1.slot(i2)[:, kc * 128:(kc + 1) * 128],
                                           XNq(kc, ts), start=(kc == 0), stop=(kc == 7))
                te_mm = ctx.step(mm, "te")
                te_free = te_mm
                ctx.wait(ctx.se, "te", te_mm)
                sev = ctx.step(ctx.se.activation(jnk, P["A"][:], AF.Silu,
                                                 scale=s1c[:, oc:oc + 1]), "se")
                ctx.wait(ctx.ve, "se", sev)
                vev = ctx.step(ctx.ve.scalar_tensor_tensor(
                    a_chunk(oc), P["B"][:], s2c[:, oc:oc + 1], jnk, AL.mult, AL.mult), "ve")
                ctx.wait(ctx.te, "ve", vev)

            for wname, scol, srcf, dstf in ((o + "h1A", h1c, a_chunk, b_chunk),
                                            (o + "h2A", h2c, b_chunk, a_chunk)):
                for oc in range(32):
                    ih, vh = wbuf.load(A[wname][oc][:, 0:2048], free_after=te_free)
                    ih2, vh2 = wbuf.load(A[wname][oc][:, 2048:4096], free_after=None)
                    ctx.wait(ctx.te, "wq", vh)
                    for hh in range(2):
                        ts = slice(hh * 512, (hh + 1) * 512)
                        for kc in range(16):
                            mm = ctx.te.matmul(P["A"][:, ts],
                                               wbuf.slot(ih)[:, kc * 128:(kc + 1) * 128],
                                               srcf(kc)[:, ts], start=(kc == 0), stop=False)
                    ctx.step(mm, "te")
                    ctx.wait(ctx.te, "wq", vh2)
                    for hh in range(2):
                        ts = slice(hh * 512, (hh + 1) * 512)
                        for kc in range(16):
                            mm = ctx.te.matmul(P["A"][:, ts],
                                               wbuf.slot(ih2)[:, kc * 128:(kc + 1) * 128],
                                               srcf(16 + kc)[:, ts], start=False, stop=(kc == 15))
                    te_mm = ctx.step(mm, "te")
                    te_free = te_mm
                    ctx.wait(ctx.se, "te", te_mm)
                    sev = ctx.step(ctx.se.activation(dstf(oc), P["A"][:], AF.Gelu,
                                                     scale=scol[:, oc:oc + 1]), "se")
                    ctx.wait(ctx.te, "se", sev)

            s4_ve = {}
            for oc in range(8):
                iw, vw = wbuf.load(A[o + "w3A"][oc][:, 0:2048], free_after=te_free)
                iw2, vw2 = wbuf.load(A[o + "w3A"][oc][:, 2048:4096], free_after=None)
                ctx.wait(ctx.te, "wq", vw)
                for hh in range(2):
                    ts = slice(hh * 512, (hh + 1) * 512)
                    for kc in range(16):
                        mm = ctx.te.matmul(P["B"][:, ts],
                                           wbuf.slot(iw)[:, kc * 128:(kc + 1) * 128],
                                           a_chunk(kc)[:, ts], start=(kc == 0), stop=False)
                ctx.step(mm, "te")
                ctx.wait(ctx.te, "wq", vw2)
                for hh in range(2):
                    ts = slice(hh * 512, (hh + 1) * 512)
                    for kc in range(16):
                        mm = ctx.te.matmul(P["B"][:, ts],
                                           wbuf.slot(iw2)[:, kc * 128:(kc + 1) * 128],
                                           a_chunk(16 + kc)[:, ts], start=False, stop=(kc == 15))
                te_mm = ctx.step(mm, "te")
                te_free = te_mm
                ctx.wait(ctx.se, "te", te_mm)
                sev = ctx.step(ctx.se.activation(jnk, P["B"][:], AF.Identity,
                                                 scale=s3c[:, oc:oc + 1]), "se")
                ctx.wait(ctx.te, "se", sev)
                bb = oc % 2
                if oc >= 2:
                    ctx.wait(ctx.te, "ve", s4_ve[oc - 2])
                for tb in range(8):
                    tr = ctx.te.transpose(
                        P["T"][:, bb * 1024 + tb * 128: bb * 1024 + (tb + 1) * 128],
                        jnk[:, tb * 128:(tb + 1) * 128], S["CID"][:])
                s4_te = ctx.step(tr, "te")
                ctx.wait(ctx.ve, "te", s4_te)
                for tb in range(8):
                    tt = q * 8 + tb
                    xsl = X[:, tt * 1024 + oc * 128: tt * 1024 + (oc + 1) * 128]
                    ad = ctx.ve.tensor_tensor(
                        xsl, xsl,
                        P["T"][:, bb * 1024 + tb * 128: bb * 1024 + (tb + 1) * 128],
                        AL.add)
                s4_ve[oc] = ctx.step(ad, "ve")
                ctx.wait(ctx.se, "ve", s4_ve[oc])
        barrier(ctx, engs=("te", "se", "gp", "sy"), sems=("ve",))

    return emit



def emit_attn_l2(ctx, A, S, P):
    X, AR, AFX = S["X"], S["AR"], S["AF"]
    XN = AR[:, 0:32 * 1024]
    kt_off = 32 * 1024
    vt_off = kt_off + 8192
    q_off = vt_off + 8320
    y_off = q_off + 1024
    p_off = y_off + 8192
    r_off = p_off + 2048
    wb_off = r_off + 1024
    rb_off = wb_off + 2048
    xb_off = rb_off + 512

    KT = AR[:, kt_off:kt_off + 8192]
    QT = AR[:, q_off:q_off + 512]
    QTs = AR[:, q_off + 512:q_off + 1024]
    YT = AR[0:64, y_off:y_off + 8192]
    PT2 = AR[:, p_off:p_off + 2048]
    RBC = AR[0:64, rb_off:rb_off + 512]
    wbuf = WStream(ctx, AR[:, wb_off:wb_off + 2048], 1024)

    kbc = AFX[:, 160:162]
    qbc = AFX[:, 162:170]
    qbcs = AFX[:, 146:154]
    km2 = AFX[:, 170:172]
    qmaxf = AFX[:, 172:173]
    negb = AFX[:, 184:186]
    recf = AFX[64:65, 192:192 + 512]
    bbf = AFX[0:1, 704:706]
    ctx.dma("sy", kbc, A["l2_bkc"], "xq")
    ctx.dma("sy", qbc, A["l2_bqc"], "xq")
    ctx.dma("sy", qbcs, A["l2_bqcs"], "xq")
    bvr = AR[0:1, r_off:r_off + 256]
    rrow = AR[64:65, r_off + 256:r_off + 768]
    km2b = AR[:, r_off + 768:r_off + 770]
    qmb = AR[:, r_off + 770:r_off + 771]
    br2 = AR[0:1, r_off + 772:r_off + 774]
    qrow_t = AR[0:1, r_off + 774:r_off + 774 + 128]
    kr2 = AR[0:1, xb_off:xb_off + 256]
    oneb = AR[:, xb_off + 256:xb_off + 256 + 64]
    bor = AR[0:1, xb_off + 512:xb_off + 512 + 1024]
    vv = ctx.step(ctx.ve.memset(oneb, 1.0), "ve")
    ctx.dma("gp", bvr, A["l2_bvr"], "wq")
    ctx.dma("gp", bor, A["l2_bor"], "wq")
    ctx.wait(ctx.se, "xq")

    te_free = None
    for oc in range(2):
        iw, vw = wbuf.load(A["l2_kA"][oc], free_after=te_free)
        ctx.wait(ctx.te, "wq", vw)
        for tt in range(8):
            ts = slice(tt * 512, (tt + 1) * 512)
            ps = P["C"] if tt % 2 == 0 else P["D"]
            if tt >= 2:
                ctx.wait(ctx.te, "se", ctx.cnt["se"] - 1)
            for kc in range(8):
                mm = ctx.te.matmul(ps[:], wbuf.slot(iw)[:, kc * 128:(kc + 1) * 128],
                                   XN[:, kc * 4096 + ts.start: kc * 4096 + ts.stop],
                                   start=(kc == 0), stop=(kc == 7))
            te_mm = ctx.step(mm, "te")
            te_free = te_mm
            ctx.wait(ctx.se, "te", te_mm)
            ctx.step(ctx.se.activation(KT[:, oc * 4096 + ts.start: oc * 4096 + ts.stop],
                                       ps[:], AF.Identity, bias=kbc[:, oc:oc + 1]), "se")
    ctx.wait(ctx.ve, "se")
    for oc in range(2):
        chain(ctx, ctx.ve.reduce_max(km2[:, oc:oc + 1], KT[:, oc * 4096:(oc + 1) * 4096],
                                     axis=AX.X, apply_absolute_value=True), "ve", "ve")
    vv = ctx.step(ctx.ve.tensor_copy(km2b, km2), "ve")
    ctx.wait(ctx.ve, "ve", vv)
    ctx.wait(ctx.te, "ve", vv)
    for oc in range(2):
        tev = ctx.step(ctx.te.matmul(P["C"][0:1, 0:128], km2b[:, oc:oc + 1],
                                     S["CID"][:], start=True, stop=True), "te")
        ctx.wait(ctx.ve, "te", tev)
        vv = ctx.step(ctx.ve.tensor_copy(kr2[:, oc * 128:oc * 128 + 128],
                                         P["C"][0:1, 0:128]), "ve")
        ctx.wait(ctx.te, "ve", vv)

    ov = AR[:, vt_off:vt_off + 32 * 260].rearrange("p (t g u) -> p t g u", g=4, u=65)
    vev = ctx.step(ctx.ve.memset(ov[:, :, :, 64:65], 1.0), "ve")
    vsrc = A["l2_v"].rearrange("(c p) d -> p c d", p=128)
    vwbuf = PT2
    vw = ctx.dma("gp", vwbuf.rearrange("p (c d) -> p c d", d=256), vsrc, "wq")
    ctx.wait(ctx.te, "wq", vw)
    ctx.wait(ctx.te, "ve", vev)
    for tt2 in range(16):
        ps = P["C"] if tt2 % 2 == 0 else P["D"]
        if tt2 >= 2:
            ctx.wait(ctx.te, "se", ctx.cnt["se"] - 2)
        for half in range(2):
            tt = tt2 * 2 + half
            hs = slice(half * 256, (half + 1) * 256)
            for kc in range(8):
                mm = ctx.te.matmul(ps[:, hs],
                                   XN[:, kc * 4096 + tt * 128: kc * 4096 + (tt + 1) * 128],
                                   vwbuf[:, kc * 256:(kc + 1) * 256],
                                   start=(kc == 0), stop=False)
            mm = ctx.te.matmul(ps[:, hs], S["ONE"][:], bvr, start=False, stop=True)
        te_mm = ctx.step(mm, "te")
        te_free = te_mm
        ctx.wait(ctx.se, "te", te_mm)
        for half in range(2):
            tt = tt2 * 2 + half
            hs = slice(half * 256, (half + 1) * 256)
            dst = AR[:, vt_off + tt * 260: vt_off + (tt + 1) * 260] \
                .rearrange("p (g u) -> p g u", u=65)[:, :, 0:64]
            srcv = ps[:, hs].rearrange("p (g u) -> p g u", u=64)
            ctx.step(ctx.se.activation(dst, srcv, AF.Copy), "se")
    barrier(ctx, engs=("te", "ve", "se"), sems=("ve", "se", "te"))
    if DBG:
        ctx.wait(ctx.sy, "se")
        ctx.wait(ctx.sy, "ve")
        ctx.dma("sy", A["dbg_xn"], AR[:, 0:32768], "xq")
        ctx.dma("sy", A["dbg_kt"], KT, "xq")

    for qq in range(8):
        qs = slice(qq * 512, (qq + 1) * 512)
        for pr in range(8):
            g = pr // 2
            gb = 64 * (g % 2)
            iw, vw = wbuf.load(A["l2_qA"][pr], free_after=te_free)
            iws, vws = wbuf.load(A["l2_qAs"][pr], free_after=te_free)
            ctx.wait(ctx.te, "wq", vws)
            ctx.wait(ctx.te, "ve")
            for sw, (qt, bc) in enumerate(((QT, qbc), (QTs, qbcs))):
                for kc in range(8):
                    mm = ctx.te.matmul(P["D"][:],
                                       wbuf.slot(iw if sw == 0 else iws)[:, kc * 128:(kc + 1) * 128],
                                       XN[:, kc * 4096 + qs.start: kc * 4096 + qs.stop],
                                       start=(kc == 0), stop=(kc == 7))
                te_mm = ctx.step(mm, "te")
                te_free = te_mm
                ctx.wait(ctx.se, "te", te_mm)
                sev = ctx.step(ctx.se.activation(qt, P["D"][:], AF.Identity,
                                                 bias=bc[:, pr:pr + 1]), "se")
                if sw == 0:
                    ctx.wait(ctx.te, "se", sev)
            ctx.wait(ctx.ve, "se", sev)
            chain(ctx, ctx.ve.reduce_max(qmaxf, QT, axis=AX.X,
                                         apply_absolute_value=True), "ve", "ve")
            vv = ctx.step(ctx.ve.tensor_copy(qmb, qmaxf), "ve")
            ctx.wait(ctx.ve, "ve", vv)
            ctx.wait(ctx.te, "ve", vv)
            tev = ctx.step(ctx.te.matmul(P["C"][0:1, 128:256], qmb, S["CID"][:],
                                         start=True, stop=True), "te")
            ctx.wait(ctx.ve, "te", tev)
            chain(ctx, ctx.ve.tensor_copy(qrow_t, P["C"][0:1, 128:256]), "ve", "ve")
            for hh in range(2):
                chain(ctx, ctx.ve.tensor_tensor(
                    RBC[0:1, hh * 64:hh * 64 + 64],
                    qrow_t[:, hh * 64:hh * 64 + 64],
                    kr2[:, (g // 2) * 128 + gb: (g // 2) * 128 + gb + 64], AL.mult), "ve", "ve")
                chain(ctx, ctx.ve.reduce_sum(bbf[:, hh:hh + 1],
                                             RBC[0:1, hh * 64:hh * 64 + 64],
                                             axis=AX.X), "ve", "ve")
            vv = ctx.step(ctx.ve.tensor_scalar(br2, bbf, -1.0 / 16.0, None, AL.mult), "ve")
            ctx.wait(ctx.ve, "ve", vv)
            ctx.wait(ctx.te, "ve", vv)
            mm = ctx.te.matmul(P["C"][:, 0:2], S["ONE"][:], br2, start=True, stop=True)
            te_mm = ctx.step(mm, "te")
            ctx.wait(ctx.ve, "te", te_mm)
            vv_negb = ctx.step(ctx.ve.tensor_copy(negb, P["C"][:, 0:2]), "ve")
            if DBG and qq == 0:
                dv = S["DBG"]
                if pr == 0:
                    ctx.step(ctx.ve.tensor_copy(dv[:, 440:450],
                                                AR[:, 0:10]), "ve")
                    ctx.step(ctx.ve.tensor_copy(dv[:, 450:460],
                                                S["STAT"][:, 64:74]), "ve")
                    ctx.step(ctx.ve.tensor_copy(dv[:, 460:470],
                                                S["X"][:, 0:10]), "ve")
                ctx.step(ctx.ve.tensor_copy(dv[:, pr * 2:pr * 2 + 2], negb), "ve")
                ctx.step(ctx.ve.tensor_copy(dv[0:1, 16 + pr * 2:16 + pr * 2 + 2], bbf), "ve")
                ctx.step(ctx.ve.tensor_copy(dv[:, 32 + pr:33 + pr], qmaxf), "ve")
                if pr == 0:
                    ctx.step(ctx.ve.tensor_copy(dv[0:1, 40:40 + 256], kr2), "ve")
                    ctx.step(ctx.ve.tensor_copy(dv[:, 296:298], km2), "ve")
                    ctx.step(ctx.ve.tensor_copy(dv[0:1, 300:428], qrow_t), "ve")
            for hh in range(2):
                h = 2 * pr + hh
                qt = QT if (h % 2) == (g % 2) else QTs
                qrow = qt[gb:gb + 64, :]
                ysl = YT[:, h * 512:(h + 1) * 512]
                pv_done = [None, None]
                exp_done = None
                for kc2 in range(16):
                    sl = kc2 % 2
                    pslot = PT2[:, sl * 1024:(sl + 1) * 1024]
                    if exp_done is not None:
                        ctx.wait(ctx.te, "se", exp_done)
                    for j in range(2):
                        kc = kc2 * 2 + j
                        mm = ctx.te.matmul(
                            P["A"][:, j * 512:(j + 1) * 512],
                            KT[gb:gb + 64,
                               (g // 2) * 4096 + kc * 128: (g // 2) * 4096 + (kc + 1) * 128],
                            qrow, start=True, stop=True)
                    te_mm = ctx.step(mm, "te")
                    ctx.wait(ctx.se, "te", te_mm)
                    if pv_done[sl] is not None:
                        ctx.wait(ctx.se, "te", pv_done[sl])
                    if kc2 == 0:
                        ctx.wait(ctx.se, "ve", vv_negb)
                    exp_done = ctx.step(ctx.se.activation(pslot, P["A"][:], AF.Exp,
                                                          scale=0.125,
                                                          bias=negb[:, hh:hh + 1]), "se")
                    ctx.wait(ctx.te, "se", exp_done)
                    for j in range(2):
                        kc = kc2 * 2 + j
                        mm = ctx.te.matmul(
                            P["D"][0:65, :],
                            AR[:, vt_off + kc * 260 + g * 65: vt_off + kc * 260 + g * 65 + 65],
                            pslot[:, j * 512:(j + 1) * 512],
                            start=(kc == 0), stop=(kc == 31))
                    te_mm = ctx.step(mm, "te")
                    pv_done[sl] = te_mm
                ctx.wait(ctx.ve, "te", te_mm)
                chain(ctx, ctx.ve.reciprocal(recf, P["D"][64:65, :]), "ve", "ve")
                vv = ctx.step(ctx.ve.tensor_copy(rrow, recf), "ve")
                ctx.wait(ctx.ve, "ve", vv)
                ctx.wait(ctx.te, "ve", vv)
                mm = ctx.te.matmul(P["C"][0:64, :], oneb[64:65, :], rrow,
                                   start=True, stop=True)
                te_mm = ctx.step(mm, "te")
                ctx.wait(ctx.se, "te", te_mm)
                sev = ctx.step(ctx.se.activation(RBC, P["C"][0:64, :], AF.Copy), "se")
                ctx.wait(ctx.ve, "se", sev)
                vv = ctx.step(ctx.ve.tensor_tensor(ysl, P["D"][0:64, :], RBC,
                                                   AL.mult), "ve")
                ctx.wait(ctx.te, "ve", vv)
        ctx.wait(ctx.te, "ve")
        wo_r = A["l2_wo"].rearrange("(c p) d -> c p d", p=64)
        for t2 in range(2):
            psl = [(P["A"], 0), (P["A"], 512), (P["B"], 0), (P["B"], 512)]
            for yc in range(16):
                if te_free is not None:
                    ctx.wait(ctx.gp, "te", te_free)
                wsl = wbuf.slot(yc)[0:64, :]
                vw = ctx.dma("gp", wsl, wo_r[yc], "wq")
                ctx.wait(ctx.te, "wq", vw)
                for u in range(2):
                    t4 = t2 * 2 + u
                    for dh in range(2):
                        ps, off = psl[u * 2 + dh]
                        mm = ctx.te.matmul(
                            ps[:, off:off + 512],
                            YT[:, yc * 512 + t4 * 128: yc * 512 + (t4 + 1) * 128],
                            wsl[:, dh * 512:(dh + 1) * 512],
                            start=(yc == 0), stop=False)
                te_mm = ctx.step(mm, "te")
                te_free = te_mm
            for u in range(2):
                t4 = t2 * 2 + u
                tt = qq * 4 + t4
                for dh in range(2):
                    ps, off = psl[u * 2 + dh]
                    mm = ctx.te.matmul(ps[:, off:off + 512], S["ONE"][:],
                                       bor[:, dh * 512:(dh + 1) * 512],
                                       start=False, stop=True)
                    te_mm = ctx.step(mm, "te")
                    ctx.wait(ctx.ve, "te", te_mm)
                    xsl = X[:, tt * 1024 + dh * 512: tt * 1024 + (dh + 1) * 512]
                    ctx.step(ctx.ve.tensor_tensor(xsl, xsl, ps[:, off:off + 512],
                                                  AL.add), "ve")
            ctx.wait(ctx.te, "ve")
    barrier(ctx, engs=("te", "se", "gp", "sy"), sems=("ve",))


def emit_attn_l3(ctx, A, S, P):
    X, AR, AFX = S["X"], S["AR"], S["AF"]
    XN = AR[:, 0:32 * 1024]
    csb_off = 32 * 1024
    qsb_off = csb_off + 8192
    t1_off = qsb_off + 1024
    ro_off = t1_off + 1024
    ktm_off = ro_off + 1024
    ksq_off = ktm_off + 1024
    gsb_off = ksq_off + 1024
    vs_off = gsb_off + 1024
    wb_off = vs_off + 2080
    kb_off = csb_off
    vb_off = kb_off + 4096
    qb_off = vb_off + 2080
    ub_off = qb_off + 512
    yt_off = ub_off + 1024
    pt_off = yt_off + 8192
    r2_off = pt_off + 2048

    wbuf = WStream(ctx, AR[:, wb_off:wb_off + 2048], 1024)
    CSB = AR[:, csb_off:csb_off + 8192]
    PSW = S["PSW"]

    recf = AFX[64:65, 192:192 + 512]
    trh = AFX[0:1, 140:156]
    mkh = AFX[0:1, 156:172]
    sc1 = AFX[0:1, 172:174]
    km_col = AFX[:, 176:177]

    QSB = AR[:, qsb_off:qsb_off + 1024]
    T1 = AR[:, t1_off:t1_off + 1024]
    RO = AR[:, ro_off:ro_off + 1024]
    KTM = AR[:, ktm_off:ktm_off + 1024]
    KSQ = AR[:, ksq_off:ksq_off + 1024]
    GSB = AR[0:64, gsb_off:gsb_off + 1024]
    onec = AR[:, r2_off + 3072:r2_off + 3073]
    vv = ctx.step(ctx.ve.memset(onec, 1.0), "ve")
    vv0 = ctx.step(ctx.ve.memset(trh, 0.0), "ve")
    vv0 = ctx.step(ctx.ve.memset(mkh, 0.0), "ve")
    ctx.wait(ctx.ve, "ve", vv0)
    KSQS = AFX[0:1, 180:181]
    qsb_used = {}
    ro_used = {}
    te_free = None
    for sel, wname in ((0, "l3_qA"), (1, "l3_kA")):
        for oc in range(8):
            iw, vw = wbuf.load(A[wname][oc], free_after=te_free)
            ctx.wait(ctx.gp, "ve")
            cw = ctx.dma("gp", CSB[:, 0:4096], A["l3_cos"][oc * 128:(oc + 1) * 128, :], "wq")
            cw = ctx.dma("gp", CSB[:, 4096:8192], A["l3_sin2"][oc * 128:(oc + 1) * 128, :], "wq")
            ctx.wait(ctx.te, "wq", cw)
            ctx.wait(ctx.ve, "wq", cw)
            for tt in range(8):
                b = tt % 2
                ts = slice(tt * 512, (tt + 1) * 512)
                psm = P["A"][:, b * 512:(b + 1) * 512]
                psw_out = P["B"][:, b * 512:(b + 1) * 512]
                qsb = QSB[:, b * 512:(b + 1) * 512]
                t1 = T1[:, b * 512:(b + 1) * 512]
                ro = RO[:, b * 512:(b + 1) * 512]
                if qsb_used.get(b) is not None:
                    ctx.wait(ctx.se, "ve", qsb_used[b])
                for kc in range(8):
                    mm = ctx.te.matmul(psm, wbuf.slot(iw)[:, kc * 128:(kc + 1) * 128],
                                       XN[:, kc * 4096 + ts.start:kc * 4096 + ts.stop],
                                       start=(kc == 0), stop=(kc == 7))
                te_mm = ctx.step(mm, "te")
                te_free = te_mm
                ctx.wait(ctx.se, "te", te_mm)
                sev = ctx.step(ctx.se.activation(qsb, psm, AF.Copy), "se")
                ctx.wait(ctx.te, "se", sev)
                mm = ctx.te.matmul(psw_out, PSW[:], qsb, start=True, stop=True)
                te_sw = ctx.step(mm, "te")
                ctx.wait(ctx.ve, "se", sev)
                if ro_used.get(b) is not None:
                    ctx.wait(ctx.ve, "xq", ro_used[b])
                qsb_used[b] = chain(ctx, ctx.ve.tensor_tensor(t1, qsb, CSB[:, ts], AL.mult), "ve", "ve")
                ctx.wait(ctx.ve, "te", te_sw)
                chain(ctx, ctx.ve.tensor_tensor(ro, psw_out,
                                                CSB[:, 4096 + ts.start:4096 + ts.stop],
                                                AL.mult), "ve", "ve")
                vv = ctx.step(ctx.ve.tensor_tensor(ro, ro, t1, AL.add), "ve")
                ctx.wait(ctx.ve, "ve", vv)
                ctx.wait(ctx.te, "ve", vv)
                ctx.wait(ctx.sy, "ve", vv)
                dst = A["l3qk"][sel, :, oc * 4096 + ts.start: oc * 4096 + ts.stop]
                ro_used[b] = ctx.dma("sy", dst, ro, "xq")
                if sel == 1:
                    ktm = KTM[:, b * 512:(b + 1) * 512]
                    for blk in range(4):
                        tr = ctx.te.transpose(
                            P["T"][:, b * 1024 + blk * 128: b * 1024 + (blk + 1) * 128],
                            ro[:, blk * 128:(blk + 1) * 128], S["CID"][:])
                    tr_v = ctx.step(tr, "te")
                    ctx.wait(ctx.ve, "te", tr_v)
                    for blk in range(4):
                        cp = ctx.ve.tensor_copy(
                            ktm[:, blk * 128:(blk + 1) * 128],
                            P["T"][:, b * 1024 + blk * 128: b * 1024 + (blk + 1) * 128])
                    ktm_v = ctx.step(cp, "ve")
                    ctx.wait(ctx.te, "ve", ktm_v)
                    for blk in range(4):
                        mm = ctx.te.matmul(P["C"][0:64, 0:64],
                                           ktm[:, blk * 128:blk * 128 + 64],
                                           ktm[:, blk * 128:blk * 128 + 64],
                                           start=(tt == 0 and blk == 0), stop=(tt == 7 and blk == 3))
                        mm = ctx.te.matmul(P["D"][0:64, 0:64],
                                           ktm[:, blk * 128 + 64:blk * 128 + 128],
                                           ktm[:, blk * 128 + 64:blk * 128 + 128],
                                           start=(tt == 0 and blk == 0), stop=(tt == 7 and blk == 3))
                    gm_v = ctx.step(mm, "te")
                    ksq = KSQ[:, b * 512:(b + 1) * 512]
                    sev = ctx.step(ctx.se.activation(ksq, ro, AF.Square), "se")
                    ctx.wait(ctx.te, "se", sev)
                    for hh in range(2):
                        mm = ctx.te.matmul(psm[0:1, 0:512],
                                           onec[64 * hh:64 * hh + 64, :],
                                           ksq[64 * hh:64 * hh + 64, :],
                                           start=True, stop=True)
                        knv = ctx.step(mm, "te")
                        ctx.wait(ctx.ve, "te", knv)
                        h = oc * 2 + hh
                        chain(ctx, ctx.ve.reduce_max(sc1[:, 1:2], psm[0:1, 0:512],
                                                     axis=AX.X), "ve", "ve")
                        vv = ctx.step(ctx.ve.tensor_tensor(mkh[:, h:h + 1], mkh[:, h:h + 1],
                                                           sc1[:, 1:2], AL.max), "ve")
                        ctx.wait(ctx.ve, "ve", vv)
                        ctx.wait(ctx.te, "ve", vv)
            if sel == 1:
                ctx.wait(ctx.se, "te", gm_v)
                sev = ctx.step(ctx.se.activation(GSB[:, (2 * oc) * 64:(2 * oc) * 64 + 64],
                                                 P["C"][0:64, 0:64], AF.Copy), "se")
                sev = ctx.step(ctx.se.activation(GSB[:, (2 * oc + 1) * 64:(2 * oc + 1) * 64 + 64],
                                                 P["D"][0:64, 0:64], AF.Copy), "se")
                ctx.wait(ctx.te, "se", sev)
    VW = CSB
    ctx.wait(ctx.gp, "ve")
    vw = ctx.dma("gp", VW.rearrange("p (c d) -> p c d", d=1024),
                 A["l3_v"].rearrange("(c p) d -> p c d", p=128), "wq")
    VS = AR[:, vs_off:vs_off + 2080]
    ov = VS.rearrange("p (b g u) -> p b g u", b=2, u=65)
    vv_one = ctx.step(ctx.ve.memset(ov[:, :, :, 64:65], 1.0), "ve")
    ctx.wait(ctx.te, "wq", vw)
    vdma = {}
    for tt in range(32):
        b = tt % 2
        psm = P["A"] if b == 0 else P["B"]
        if tt >= 2:
            ctx.wait(ctx.te, "se", ctx.cnt["se"] - 1)
        for dh in range(2):
            hs = slice(dh * 512, (dh + 1) * 512)
            for kc in range(8):
                mm = ctx.te.matmul(psm[:, hs],
                                   XN[:, kc * 4096 + tt * 128: kc * 4096 + (tt + 1) * 128],
                                   VW[:, kc * 1024 + hs.start: kc * 1024 + hs.stop],
                                   start=(kc == 0), stop=(kc == 7))
        te_mm = ctx.step(mm, "te")
        ctx.wait(ctx.se, "te", te_mm)
        if tt >= 2:
            ctx.wait(ctx.se, "xq", vdma[tt - 2])
        if tt == 0:
            ctx.wait(ctx.se, "ve", vv_one)
        dstv = VS[:, b * 1040:(b + 1) * 1040].rearrange("p (g u) -> p g u", u=65)[:, :, 0:64]
        sev = ctx.step(ctx.se.activation(dstv, psm[:].rearrange("p (g u) -> p g u", u=64),
                                         AF.Copy), "se")
        ctx.wait(ctx.sy, "se", sev)
        for h in range(16):
            last = ctx.dma("sy", A["l3v"][h, tt, :, :],
                           VS[:, b * 1040 + h * 65: b * 1040 + (h + 1) * 65], "xq")
        vdma[tt] = last
    barrier(ctx, engs=("te", "ve", "se", "gp", "sy"), sems=("ve", "se", "te", "xq", "wq"))

    KB = AR[:, kb_off:kb_off + 4096]
    VB = AR[:, vb_off:vb_off + 2080]
    QB = AR[:, qb_off:qb_off + 512]
    UB = AR[0:64, ub_off:ub_off + 512]
    UB2 = AR[0:64, ub_off + 512:ub_off + 1024]
    YT = AR[0:64, yt_off:yt_off + 8192]
    PT2 = AR[:, pt_off:pt_off + 2048]
    RBC = AR[0:64, r2_off:r2_off + 512]
    rrow = AR[64:65, r2_off + 512:r2_off + 1024]
    bur = AR[0:1, r2_off + 1024:r2_off + 2048]
    kmb = AR[:, r2_off + 2048:r2_off + 2049]
    oneb = AR[:, r2_off + 2052:r2_off + 2052 + 64]
    R_MU = AR[0:1, r2_off + 3200:r2_off + 3712]
    R_ES = AR[0:1, r2_off + 3712:r2_off + 4224]
    R_M2 = AR[0:1, r2_off + 4224:r2_off + 4736]
    R_SG = AR[0:1, r2_off + 4736:r2_off + 5248]
    R_U = AR[0:1, r2_off + 5248:r2_off + 5760]
    R_C = AR[0:1, r2_off + 5760:r2_off + 6272]
    vv = ctx.step(ctx.ve.memset(oneb, 1.0), "ve")
    vv = ctx.step(ctx.ve.memset(AR[64:65, kb_off:kb_off + 4096], 1.0), "ve")
    ctx.dma("gp", bur, A["l3_bur"], "wq")
    barrier(ctx, engs=("ve",), sems=("ve",))

    for qq in range(8):
        for h in range(16):
            ctx.wait(ctx.sy, "te")
            ctx.wait(ctx.sy, "ve")
            ctx.wait(ctx.sy, "se")
            kv1 = ctx.dma("sy", KB[0:64, :], A["l3qk"][1, 64 * (h % 2):64 * (h % 2) + 64,
                                                       (h // 2) * 4096:(h // 2 + 1) * 4096], "xq")
            kv2 = ctx.dma("sy", VB.rearrange("p (c u) -> p c u", u=65),
                          A["l3v"][h].rearrange("c p u -> p c u"), "xq")
            kv3 = ctx.dma("sy", QB[0:64, :],
                          A["l3qk"][0, 64 * (h % 2):64 * (h % 2) + 64,
                                    (h // 2) * 4096 + qq * 512: (h // 2) * 4096 + (qq + 1) * 512], "xq")
            ctx.wait(ctx.ve, "xq", kv3)
            chain(ctx, ctx.ve.reduce_sum(km_col[0:64, :], KB[0:64, :], axis=AX.X), "ve", "ve")
            chain(ctx, ctx.ve.tensor_scalar(kmb[0:64, :], km_col[0:64, :],
                                            1.0 / 4096.0, None, AL.mult), "ve", "ve")
            ctx.wait(ctx.te, "ve")
            mm = ctx.te.matmul(P["C"][0:1, 0:512], kmb[0:64, :], QB[0:64, :],
                               start=True, stop=True)
            mu_v = ctx.step(mm, "te")
            mm = ctx.te.matmul(P["D"][0:64, :], GSB[:, h * 64:(h + 1) * 64],
                               QB[0:64, :], start=True, stop=True)
            gq_v = ctx.step(mm, "te")
            ctx.wait(ctx.ve, "te", gq_v)
            chain(ctx, ctx.ve.tensor_tensor(UB, QB[0:64, :], P["D"][0:64, :], AL.mult),
                  "ve", "ve")
            chain(ctx, ctx.ve.tensor_tensor(UB2, QB[0:64, :], QB[0:64, :], AL.mult),
                  "ve", "ve")
            ctx.wait(ctx.te, "ve")
            mm = ctx.te.matmul(P["A"][0:1, 0:512], onec[0:64, :], UB,
                               start=True, stop=True)
            ss_v = ctx.step(mm, "te")
            mm = ctx.te.matmul(P["A"][0:1, 512:1024], onec[0:64, :], UB2,
                               start=True, stop=True)
            qn_v = ctx.step(mm, "te")
            ctx.wait(ctx.ve, "te", qn_v)
            chain(ctx, ctx.ve.tensor_scalar(R_MU, P["C"][0:1, 0:512], 0.125, None,
                                            AL.mult), "ve", "ve")
            chain(ctx, ctx.ve.tensor_scalar(R_ES, P["A"][0:1, 0:512],
                                            1.0 / (4096.0 * 64.0), None,
                                            AL.mult), "ve", "ve")
            chain(ctx, ctx.ve.tensor_tensor(R_M2, R_MU, R_MU, AL.mult), "ve", "ve")
            chain(ctx, ctx.ve.tensor_tensor(R_SG, R_ES, R_M2, AL.subtract), "ve", "ve")
            chain(ctx, ctx.ve.tensor_scalar(R_SG, R_SG, 1e-3, None, AL.max), "ve", "ve")
            ctx.wait(ctx.se, "ve")
            chain(ctx, ctx.se.activation(R_SG, R_SG, AF.Ln), "se", "se")
            chain(ctx, ctx.se.activation(R_SG, R_SG, AF.Exp, scale=0.5), "se", "se")
            ctx.wait(ctx.ve, "se")
            chain(ctx, ctx.ve.tensor_scalar(R_U, P["A"][0:1, 512:1024],
                                            mkh[:, h:h + 1], None, AL.mult), "ve", "ve")
            chain(ctx, ctx.ve.tensor_scalar(R_U, R_U, 1e-12, None, AL.max), "ve", "ve")
            ctx.wait(ctx.se, "ve")
            chain(ctx, ctx.se.activation(R_U, R_U, AF.Ln), "se", "se")
            chain(ctx, ctx.se.activation(R_U, R_U, AF.Exp, scale=0.5), "se", "se")
            ctx.wait(ctx.ve, "se")
            chain(ctx, ctx.ve.tensor_scalar(R_U, R_U, 0.125, -70.0, AL.mult, AL.add),
                  "ve", "ve")
            chain(ctx, ctx.ve.scalar_tensor_tensor(R_C, R_SG, 0.875, R_MU,
                                                   AL.mult, AL.add), "ve", "ve")
            chain(ctx, ctx.ve.tensor_tensor(R_C, R_C, R_U, AL.max), "ve", "ve")
            vv = ctx.step(ctx.ve.tensor_scalar(QB[64:65, :], R_C, -8.0, None, AL.mult), "ve")
            ctx.wait(ctx.ve, "ve", vv)
            ctx.wait(ctx.te, "ve", vv)
            ctx.wait(ctx.te, "xq", kv2)
            ysl = YT[:, h * 512:(h + 1) * 512]
            pv_done = [None, None]
            exp_done = None
            for kc2 in range(16):
                sl = kc2 % 2
                pslot = PT2[:, sl * 1024:(sl + 1) * 1024]
                if exp_done is not None:
                    ctx.wait(ctx.te, "se", exp_done)
                for j in range(2):
                    kc = kc2 * 2 + j
                    mm = ctx.te.matmul(
                        P["B"][:, j * 512:(j + 1) * 512],
                        KB[0:65, kc * 128:(kc + 1) * 128],
                        QB[0:65, :], start=True, stop=True)
                te_mm = ctx.step(mm, "te")
                ctx.wait(ctx.se, "te", te_mm)
                if pv_done[sl] is not None:
                    ctx.wait(ctx.se, "te", pv_done[sl])
                exp_done = ctx.step(ctx.se.activation(pslot, P["B"][:], AF.Exp,
                                                      scale=0.125), "se")
                ctx.wait(ctx.te, "se", exp_done)
                for j in range(2):
                    kc = kc2 * 2 + j
                    mm = ctx.te.matmul(
                        P["D"][0:65, :],
                        VB[:, kc * 65:(kc + 1) * 65],
                        pslot[:, j * 512:(j + 1) * 512],
                        start=(kc == 0), stop=(kc == 31))
                te_mm = ctx.step(mm, "te")
                pv_done[sl] = te_mm
            ctx.wait(ctx.ve, "te", te_mm)
            chain(ctx, ctx.ve.tensor_scalar(P["D"][64:65, :], P["D"][64:65, :], 1e-30,
                                            None, AL.add), "ve", "ve")
            chain(ctx, ctx.ve.reciprocal(recf, P["D"][64:65, :]), "ve", "ve")
            vv = ctx.step(ctx.ve.tensor_copy(rrow, recf), "ve")
            ctx.wait(ctx.ve, "ve", vv)
            ctx.wait(ctx.te, "ve", vv)
            mm = ctx.te.matmul(P["C"][0:64, :], oneb[64:65, :], rrow,
                               start=True, stop=True)
            te_mm = ctx.step(mm, "te")
            ctx.wait(ctx.se, "te", te_mm)
            sev = ctx.step(ctx.se.activation(RBC, P["C"][0:64, :], AF.Copy), "se")
            ctx.wait(ctx.ve, "se", sev)
            vv = ctx.step(ctx.ve.tensor_tensor(ysl, P["D"][0:64, :], RBC, AL.mult), "ve")
            ctx.wait(ctx.te, "ve", vv)
        ctx.wait(ctx.te, "ve")
        wu_r = A["l3_wu"].rearrange("(c p) d -> c p d", p=64)
        te_free = ctx.cnt["te"]
        for t2 in range(2):
            psl = [(P["A"], 0), (P["A"], 512), (P["B"], 0), (P["B"], 512)]
            for yc in range(16):
                ctx.wait(ctx.gp, "te", te_free)
                wsl = wbuf.slot(yc)[0:64, :]
                vw = ctx.dma("gp", wsl, wu_r[yc], "wq")
                ctx.wait(ctx.te, "wq", vw)
                for u in range(2):
                    t4 = t2 * 2 + u
                    for dh in range(2):
                        ps, off = psl[u * 2 + dh]
                        mm = ctx.te.matmul(
                            ps[:, off:off + 512],
                            YT[:, yc * 512 + t4 * 128: yc * 512 + (t4 + 1) * 128],
                            wsl[:, dh * 512:(dh + 1) * 512],
                            start=(yc == 0), stop=False)
                te_mm = ctx.step(mm, "te")
                te_free = te_mm
            for u in range(2):
                t4 = t2 * 2 + u
                tt = qq * 4 + t4
                for dh in range(2):
                    ps, off = psl[u * 2 + dh]
                    mm = ctx.te.matmul(ps[:, off:off + 512], S["ONE"][:],
                                       bur[:, dh * 512:(dh + 1) * 512],
                                       start=False, stop=True)
                    te_mm = ctx.step(mm, "te")
                    ctx.wait(ctx.ve, "te", te_mm)
                    xsl = X[:, tt * 1024 + dh * 512: tt * 1024 + (dh + 1) * 512]
                    ctx.step(ctx.ve.tensor_tensor(xsl, xsl, ps[:, off:off + 512],
                                                  AL.add), "ve")
            ctx.wait(ctx.te, "ve")
    barrier(ctx, engs=("te", "se", "gp", "sy"), sems=("ve",))


SELU_LA = SELU_L * SELU_A


def emit_attn_l1(ctx, A, S, P):
    X, AR, AFX = S["X"], S["AR"], S["AF"]
    XN = AR[:, 0:32 * 1024]
    kb_off = 32 * 1024
    qb_off = kb_off + 4096
    vb_off = qb_off + 4096
    es_off = vb_off + 2048
    rs_off = es_off + 2048
    tp_off = rs_off + 2048
    wb_off = tp_off + 2048
    r2_off = wb_off + 1024

    wbuf = WStream(ctx, AR[:, wb_off:wb_off + 1024], 512)
    KB = AR[:, kb_off:kb_off + 4096]
    QB = AR[:, qb_off:qb_off + 4096]
    VB = AR[:, vb_off:vb_off + 2048]
    EB = AR[:, es_off:es_off + 2048]
    RB = AR[:, rs_off:rs_off + 2048]
    TB = AR[:, tp_off:tp_off + 2048]
    kmb = AR[:, r2_off:r2_off + 1]
    corr = AFX[:, 176:177]
    vsum = AFX[:, 177:178]
    qbc = AFX[:, 178:194]
    kbc = AFX[:, 194:210]
    km_col = AFX[:, 210:211]
    ctx.dma("sy", qbc, A["l1_bqc"], "xq")
    ctx.dma("sy", kbc, A["l1_bkc"], "xq")
    vv = ctx.step(ctx.ve.memset(AR[64:65, kb_off:kb_off + 4096], 1.0), "ve")
    ctx.wait(ctx.se, "xq")
    barrier(ctx, engs=("ve", "te"), sems=("ve",))

    te_free = None
    for h in range(16):
        hb = 64 * (h % 2)
        for sel, wname, dst, bc in ((1, "l1_kA", KB, kbc), (0, "l1_qA", QB, qbc)):
            iw, vw = wbuf.load(A[wname][h], free_after=te_free)
            ctx.wait(ctx.te, "wq", vw)
            for tt in range(8):
                ts = slice(tt * 512, (tt + 1) * 512)
                psm = P["A"][:, (tt % 2) * 512:(tt % 2) * 512 + 512]
                if tt >= 2:
                    ctx.wait(ctx.te, "se", ctx.cnt["se"] - 1)
                for kc in range(8):
                    mm = ctx.te.matmul(psm[0:64, :], wbuf.slot(iw)[:, kc * 64:(kc + 1) * 64],
                                       XN[:, kc * 4096 + ts.start:kc * 4096 + ts.stop],
                                       start=(kc == 0), stop=(kc == 7))
                te_mm = ctx.step(mm, "te")
                te_free = te_mm
                ctx.wait(ctx.se, "te", te_mm)
                sev = ctx.step(ctx.se.activation(dst[0:64, ts], psm[0:64, :], AF.Identity,
                                                 bias=bc[0:64, h:h + 1]), "se")
        ctx.wait(ctx.ve, "se")
        chain(ctx, ctx.ve.reduce_sum(km_col[0:64, :], KB[0:64, :], axis=AX.X), "ve", "ve")
        vv = ctx.step(ctx.ve.tensor_copy(kmb[0:64, :], km_col[0:64, :]), "ve")
        ctx.wait(ctx.ve, "ve", vv)
        chain(ctx, ctx.ve.reduce_sum(vsum[0:64, :],
                                     XN[hb:hb + 64, (h // 2) * 4096:(h // 2 + 1) * 4096],
                                     axis=AX.X), "ve", "ve")
        vv = ctx.step(ctx.ve.tensor_scalar(corr[0:64, :], vsum[0:64, :], -SELU_LA, None,
                                           AL.mult), "ve")
        ctx.wait(ctx.ve, "ve", vv)
        ctx.wait(ctx.te, "ve", vv)
        for tt in range(8):
            ts = slice(tt * 512, (tt + 1) * 512)
            mm = ctx.te.matmul(P["C"][0:1, 0:512], kmb[0:64, :], QB[0:64, ts],
                               start=True, stop=True)
            te_mm = ctx.step(mm, "te")
            ctx.wait(ctx.ve, "te", te_mm)
            vv = ctx.step(ctx.ve.tensor_scalar(QB[64:65, ts], P["C"][0:1, 0:512],
                                               -1.0 / 4096.0, None, AL.mult), "ve")
            ctx.wait(ctx.ve, "ve", vv)
            ctx.wait(ctx.te, "ve", vv)
        for kc in range(32):
            bb = kc % 2
            if kc >= 2:
                ctx.wait(ctx.te, "ve", vb_done[kc - 2])
            tr = ctx.te.transpose(P["T"][0:128, bb * 1024:bb * 1024 + 64],
                                  XN[hb:hb + 64,
                                     (h // 2) * 4096 + kc * 128:(h // 2) * 4096 + (kc + 1) * 128],
                                  S["CID"][hb:hb + 64, hb:hb + 64])
            tr_v = ctx.step(tr, "te")
            ctx.wait(ctx.ve, "te", tr_v)
            if kc == 0:
                vb_done = {}
            vb_done[kc] = ctx.step(ctx.ve.tensor_copy(VB[:, kc * 64:(kc + 1) * 64],
                                                      P["T"][0:128, bb * 1024:bb * 1024 + 64]),
                                   "ve")
        ctx.wait(ctx.te, "ve")
        for qq in range(8):
            qs = slice(qq * 512, (qq + 1) * 512)
            ctx.wait(ctx.ve, "te")
            ctx.wait(ctx.se, "te")
            ctx.wait(ctx.gp, "te")
            pv_done = [None, None]
            e_done = None
            p_done = None
            for kc2 in range(16):
                sl = kc2 % 2
                ebuf = EB[:, sl * 1024:(sl + 1) * 1024]
                rbuf = RB[:, sl * 1024:(sl + 1) * 1024]
                tbuf = TB[:, sl * 1024:(sl + 1) * 1024]
                if e_done is not None:
                    ctx.wait(ctx.te, "se", e_done)
                    ctx.wait(ctx.te, "ve", r_done)
                for j in range(2):
                    kc = kc2 * 2 + j
                    mm = ctx.te.matmul(
                        P["B"][:, j * 512:(j + 1) * 512],
                        KB[0:65, kc * 128:(kc + 1) * 128],
                        QB[0:65, qs], start=True, stop=True)
                te_mm = ctx.step(mm, "te")
                ctx.wait(ctx.se, "te", te_mm)
                if pv_done[sl] is not None:
                    ctx.wait(ctx.se, "te", pv_done[sl])
                e_done = ctx.step(ctx.se.activation(ebuf, P["B"][:], AF.Exp,
                                                    scale=0.125), "se")
                ctx.wait(ctx.ve, "te", te_mm)
                ctx.wait(ctx.ve, "se", e_done)
                if pv_done[sl] is not None:
                    ctx.wait(ctx.ve, "te", pv_done[sl])
                r_done = chain(ctx, ctx.ve.tensor_scalar(rbuf, P["B"][:], 0.125 * SELU_L,
                                                         0.0, AL.mult, AL.max), "ve", "ve")
                ctx.wait(ctx.gp, "se", e_done)
                ctx.wait(ctx.gp, "ve", r_done)
                gv = ctx.step(ctx.gp.tensor_scalar(tbuf, ebuf, SELU_LA, SELU_LA,
                                                   AL.mult, AL.min), "gq")
                ctx.wait(ctx.ve, "gq", gv)
                p_done = chain(ctx, ctx.ve.tensor_tensor(tbuf, tbuf, rbuf, AL.add),
                               "ve", "ve")
                ctx.wait(ctx.te, "ve", p_done)
                for j in range(2):
                    kc = kc2 * 2 + j
                    mm = ctx.te.matmul(
                        P["D"][0:64, :],
                        VB[:, kc * 64:(kc + 1) * 64],
                        tbuf[:, j * 512:(j + 1) * 512],
                        start=(kc == 0), stop=(kc == 31))
                te_mm = ctx.step(mm, "te")
                pv_done[sl] = te_mm
            ctx.wait(ctx.ve, "te", te_mm)
            vv = ctx.step(ctx.ve.tensor_scalar(RB[0:64, 0:512], P["D"][0:64, :],
                                               corr[0:64, :], 0.015625, AL.add, AL.mult),
                          "ve")
            ctx.wait(ctx.ve, "ve", vv)
            ctx.wait(ctx.te, "ve", vv)
            for tb in range(4):
                tt = qq * 4 + tb
                tr = ctx.te.transpose(P["T"][0:128, tb * 128:tb * 128 + 64],
                                      RB[0:64, tb * 128:(tb + 1) * 128],
                                      S["CID"][0:64, 0:64])
                tr_v = ctx.step(tr, "te")
                ctx.wait(ctx.ve, "te", tr_v)
                xsl = X[:, tt * 1024 + h * 64: tt * 1024 + (h + 1) * 64]
                vv = ctx.step(ctx.ve.tensor_tensor(xsl, xsl,
                                                   P["T"][0:128, tb * 128:tb * 128 + 64],
                                                   AL.add), "ve")
            ctx.wait(ctx.te, "ve")
    barrier(ctx, engs=("te", "se", "gp", "sy"), sems=("ve", "gq"))


def emit_ffn_moe(ctx, A, S, P):
    X, AR, AFX = S["X"], S["AR"], S["AF"]
    XN = AR[:, 0:32 * 1024]
    w1_off = 32 * 1024
    w2_off = w1_off + 16384
    h_off = w2_off + 16384
    W1B = AR[:, w1_off:w1_off + 16384]
    W2B = AR[:, w2_off:w2_off + 16384]
    HB = AR[:, h_off:h_off + 4096]
    b2r = AR[0:1, h_off + 4096:h_off + 4096 + 1024]
    GTS = AFX[:, 256:512]
    LG = AFX[:, 212:220]
    m8 = AFX[:, 220:221]
    z8 = AFX[:, 221:222]
    b1c = AFX[:, 224:240]

    wr_s = AR[:, w1_off:w1_off + 64].rearrange("p (c e) -> p c e", e=8)
    brr = AR[0:1, w1_off + 64:w1_off + 72]
    jnk8 = HB[:, 0:8]
    vw = ctx.dma("gp", wr_s, A["l1_wr"].rearrange("(c p) e -> p c e", p=128), "wq")
    ctx.dma("gp", brr, A["l1_brr"], "wq")
    ctx.wait(ctx.te, "wq")
    barrier(ctx, engs=("te",), sems=("ve", "se"))
    for tt in range(32):
        psm = P["C"][:, 0:8] if tt % 2 == 0 else P["D"][:, 0:8]
        if tt >= 2:
            ctx.wait(ctx.te, "ve")
        for kc in range(8):
            mm = ctx.te.matmul(psm, XN[:, kc * 4096 + tt * 128:kc * 4096 + (tt + 1) * 128],
                               wr_s[:, kc, :], start=(kc == 0), stop=False)
        mm = ctx.te.matmul(psm, S["ONE"][:], brr, start=False, stop=True)
        te_mm = ctx.step(mm, "te")
        ctx.wait(ctx.ve, "te", te_mm)
        chain(ctx, ctx.ve.tensor_copy(LG, psm), "ve", "ve")
        chain(ctx, ctx.ve.reduce_max(m8, LG, axis=AX.X, negate=True), "ve", "ve")
        ctx.wait(ctx.se, "ve")
        sev = ctx.step(ctx.se.activation(jnk8, LG, AF.Exp,
                                         bias=m8, accum_out=z8), "se")
        ctx.wait(ctx.ve, "se", sev)
        chain(ctx, ctx.ve.reciprocal(z8, z8), "ve", "ve")
        chain(ctx, ctx.ve.tensor_scalar(m8, m8, -1.0, None, AL.mult), "ve", "ve")
        vv = ctx.step(ctx.ve.tensor_scalar(GTS[:, tt * 8:(tt + 1) * 8], LG, m8, z8,
                                           AL.is_ge, AL.mult), "ve")
        ctx.wait(ctx.ve, "ve", vv)
        ctx.wait(ctx.te, "ve", vv)

    w1r = A["l1_w1A"]
    w2r = A["l1_w2"].rearrange("e (c p) d -> e c p d", p=128)
    te_free = None
    ctx.wait(ctx.gp, "te")
    for e in range(8):
        if te_free is not None:
            ctx.wait(ctx.gp, "te", te_free)
        for hf in range(16):
            ctx.dma("gp", W1B[:, hf * 1024:(hf + 1) * 1024], w1r[e, hf], "wq")
        for kc in range(16):
            ctx.dma("gp", W2B[:, kc * 1024:(kc + 1) * 1024], w2r[e, kc], "wq")
        w_v = ctx.dma("gp", b2r, A["l1_b2r"][e], "wq")
        bv = ctx.dma("sy", b1c, A["l1_b1c"][e], "xq")
        ctx.wait(ctx.te, "wq", w_v)
        ctx.wait(ctx.se, "xq", bv)
        for t2 in range(16):
            for hf in range(16):
                psm = P["A"][:, (hf % 2) * 512:(hf % 2) * 512 + 256]
                if hf >= 2 or t2 > 0:
                    ctx.wait(ctx.te, "se", ctx.cnt["se"] - 1)
                for kc in range(8):
                    mm = ctx.te.matmul(psm,
                                       W1B[:, hf * 1024 + kc * 128: hf * 1024 + (kc + 1) * 128],
                                       XN[:, kc * 4096 + t2 * 256:kc * 4096 + (t2 + 1) * 256],
                                       start=(kc == 0), stop=(kc == 7))
                te_mm = ctx.step(mm, "te")
                te_free = te_mm
                ctx.wait(ctx.se, "te", te_mm)
                sev = ctx.step(ctx.se.activation(HB[:, hf * 256:(hf + 1) * 256], psm,
                                                 AF.Relu, bias=b1c[:, hf:hf + 1]), "se")
            ctx.wait(ctx.te, "se")
            for u in range(2):
                t128 = t2 * 2 + u
                for dh in range(2):
                    psm = P["B"][:, dh * 512:(dh + 1) * 512]
                    for kc in range(16):
                        mm = ctx.te.matmul(psm,
                                           HB[:, kc * 256 + u * 128: kc * 256 + u * 128 + 128],
                                           W2B[:, kc * 1024 + dh * 512: kc * 1024 + (dh + 1) * 512],
                                           start=(kc == 0), stop=False)
                    mm = ctx.te.matmul(psm, S["ONE"][:], b2r[:, dh * 512:(dh + 1) * 512],
                                       start=False, stop=True)
                    te_mm = ctx.step(mm, "te")
                    ctx.wait(ctx.ve, "te", te_mm)
                    xsl = X[:, t128 * 1024 + dh * 512: t128 * 1024 + (dh + 1) * 512]
                    vv = ctx.step(ctx.ve.scalar_tensor_tensor(
                        xsl, psm, GTS[:, t128 * 8 + e: t128 * 8 + e + 1], xsl,
                        AL.mult, AL.add), "ve")
                    ctx.wait(ctx.ve, "ve", vv)
                ctx.wait(ctx.te, "ve")
        te_free = ctx.cnt["te"]
    barrier(ctx, engs=("te", "se", "gp", "sy"), sems=("ve",))


def emit_attn_l0(ctx, A, S, P):
    X, AR, AFX = S["X"], S["AR"], S["AF"]
    XN = AR[:, 0:32 * 1024]
    lt_off = 32 * 1024
    kb_off = lt_off + 16384
    vb_off = kb_off + 4096
    qb_off = vb_off + 2080
    yt_off = qb_off + 1024
    pt_off = yt_off + 8192
    r2_off = pt_off + 2048
    wb_off = r2_off + 1408

    wbuf = WStream(ctx, AR[:, wb_off:wb_off + 2048], 1024)
    LT = AR[:, lt_off:lt_off + 16384]
    KB = AR[:, kb_off:kb_off + 4096]
    VB = AR[:, vb_off:vb_off + 2080]
    QT = AR[:, qb_off:qb_off + 512]
    QTs = AR[:, qb_off + 512:qb_off + 1024]
    YT = AR[0:64, yt_off:yt_off + 8192]
    PT2 = AR[:, pt_off:pt_off + 2048]
    RBC = AR[0:64, r2_off:r2_off + 512]
    rrow = AR[64:65, r2_off + 512:r2_off + 1024]
    kmb = AR[:, r2_off + 1024:r2_off + 1025]
    qmb = AR[:, r2_off + 1026:r2_off + 1027]
    br2 = AR[0:1, r2_off + 1028:r2_off + 1030]
    qrow_t = AR[0:1, r2_off + 1030:r2_off + 1158]
    kr1 = AR[0:1, r2_off + 1158:r2_off + 1286]
    oneb = AR[:, r2_off + 1288:r2_off + 1352]
    kmaxf = AFX[:, 170:171]
    qmaxf = AFX[:, 172:173]
    negb = AFX[:, 184:186]
    recf = AFX[64:65, 192:192 + 512]
    bbf = AFX[0:1, 704:706]
    vv = ctx.step(ctx.ve.memset(oneb, 1.0), "ve")

    te_free = None
    for oc in range(4):
        iw, vw = wbuf.load(A["l0_cdA"][oc], free_after=te_free)
        ctx.wait(ctx.te, "wq", vw)
        for tt in range(8):
            ts = slice(tt * 512, (tt + 1) * 512)
            psm = P["A"][:, (tt % 2) * 512:(tt % 2) * 512 + 512]
            if tt >= 2:
                ctx.wait(ctx.te, "se", ctx.cnt["se"] - 1)
            for kc in range(8):
                sl = wbuf.slot(iw)
                mm = ctx.te.matmul(psm, sl[:, kc * 128:kc * 128 + 128],
                                   XN[:, kc * 4096 + ts.start:kc * 4096 + ts.stop],
                                   start=(kc == 0), stop=(kc == 7))
            te_mm = ctx.step(mm, "te")
            te_free = te_mm
            ctx.wait(ctx.se, "te", te_mm)
            ctx.step(ctx.se.activation(LT[:, oc * 4096 + ts.start:oc * 4096 + ts.stop],
                                       psm, AF.Copy), "se")
    barrier(ctx, engs=("te", "ve"), sems=("se",))

    if te_free is not None:
        ctx.wait(ctx.gp, "te", te_free)
    kwsl = AR[:, wb_off:wb_off + 256]
    vw = ctx.dma("gp", kwsl, A["l0_kA"][0][:, 0:256], "wq")
    vwsl = AR[:, wb_off + 256:wb_off + 512]
    vvw = ctx.dma("gp", vwsl.rearrange("p (c d) -> p c d", d=64),
                  A["l0_v"].rearrange("(c p) d -> p c d", p=128), "wq")
    ov = VB.rearrange("p (g u) -> p g u", u=65)
    vv_one = ctx.step(ctx.ve.memset(ov[:, :, 64:65], 1.0), "ve")
    ctx.wait(ctx.te, "wq", vvw)
    ctx.wait(ctx.te, "ve", vv_one)
    for tt in range(8):
        ts = slice(tt * 512, (tt + 1) * 512)
        psm = P["A"] if tt % 2 == 0 else P["B"]
        if tt >= 2:
            ctx.wait(ctx.te, "se", ctx.cnt["se"] - 2)
        for kc in range(4):
            mm = ctx.te.matmul(psm[0:64, 0:512], kwsl[:, kc * 64:(kc + 1) * 64],
                               LT[:, kc * 4096 + ts.start:kc * 4096 + ts.stop],
                               start=(kc == 0), stop=(kc == 3))
        ctx.step(mm, "te")
        for kc in range(4):
            mm = ctx.te.matmul(psm[64:128, 0:512], kwsl[:, kc * 64:(kc + 1) * 64],
                               LT[:, kc * 4096 + ts.start:kc * 4096 + ts.stop],
                               start=(kc == 0), stop=(kc == 3))
        te_mm = ctx.step(mm, "te")
        te_free = te_mm
        ctx.wait(ctx.se, "te", te_mm)
        ctx.step(ctx.se.activation(KB[0:64, ts], psm[0:64, 0:512], AF.Copy), "se")
        ctx.step(ctx.se.activation(KB[64:128, ts], psm[64:128, 0:512], AF.Copy), "se")
    for tt2 in range(8):
        psm = P["C"] if tt2 % 2 == 0 else P["D"]
        if tt2 >= 2:
            ctx.wait(ctx.te, "se", ctx.cnt["se"] - 4)
        for u in range(4):
            t128 = tt2 * 4 + u
            us = slice(u * 64, (u + 1) * 64)
            for kc in range(4):
                mm = ctx.te.matmul(psm[:, us],
                                   LT[:, kc * 4096 + t128 * 128:kc * 4096 + (t128 + 1) * 128],
                                   vwsl[:, kc * 64:(kc + 1) * 64],
                                   start=(kc == 0), stop=(kc == 3))
        te_mm = ctx.step(mm, "te")
        te_free = te_mm
        ctx.wait(ctx.se, "te", te_mm)
        for u in range(4):
            t128 = tt2 * 4 + u
            us = slice(u * 64, (u + 1) * 64)
            ctx.step(ctx.se.activation(VB[:, t128 * 65:t128 * 65 + 64], psm[:, us],
                                       AF.Copy), "se")
    ctx.wait(ctx.ve, "se")
    chain(ctx, ctx.ve.reduce_max(kmaxf, KB[:, :], axis=AX.X,
                                 apply_absolute_value=True), "ve", "ve")
    vv = ctx.step(ctx.ve.tensor_copy(kmb, kmaxf), "ve")
    ctx.wait(ctx.ve, "ve", vv)
    ctx.wait(ctx.te, "ve", vv)
    tev = ctx.step(ctx.te.matmul(P["C"][0:1, 0:128], kmb, S["CID"][:],
                                 start=True, stop=True), "te")
    ctx.wait(ctx.ve, "te", tev)
    vv = ctx.step(ctx.ve.tensor_copy(kr1, P["C"][0:1, 0:128]), "ve")
    ctx.wait(ctx.ve, "ve", vv)
    ctx.wait(ctx.te, "ve", vv)
    barrier(ctx, engs=("te", "ve", "se"), sems=("ve", "se", "te"))

    for qq in range(8):
        qs = slice(qq * 512, (qq + 1) * 512)
        nkc = 4 * qq + 4
        for pr in range(8):
            iw, vw = wbuf.load(A["l0_quA"][pr], free_after=te_free)
            ctx.wait(ctx.te, "wq", vw)
            ctx.wait(ctx.te, "ve")
            for kc in range(4):
                mm = ctx.te.matmul(P["D"][:],
                                   wbuf.slot(iw)[:, kc * 128:(kc + 1) * 128],
                                   LT[:, kc * 4096 + qs.start:kc * 4096 + qs.stop],
                                   start=(kc == 0), stop=(kc == 3))
            te_mm = ctx.step(mm, "te")
            te_free = te_mm
            ctx.wait(ctx.se, "te", te_mm)
            sev = ctx.step(ctx.se.activation(QT, P["D"][:], AF.Copy), "se")
            ctx.wait(ctx.ve, "se", sev)
            chain(ctx, ctx.ve.reduce_max(qmaxf, QT, axis=AX.X,
                                         apply_absolute_value=True), "ve", "ve")
            vv = ctx.step(ctx.ve.tensor_copy(qmb, qmaxf), "ve")
            ctx.wait(ctx.ve, "ve", vv)
            ctx.wait(ctx.te, "ve", vv)
            tev = ctx.step(ctx.te.matmul(P["C"][0:1, 128:256], qmb, S["CID"][:],
                                         start=True, stop=True), "te")
            ctx.wait(ctx.ve, "te", tev)
            chain(ctx, ctx.ve.tensor_copy(qrow_t, P["C"][0:1, 128:256]), "ve", "ve")
            for hh in range(2):
                chain(ctx, ctx.ve.tensor_tensor(
                    RBC[0:1, hh * 64:hh * 64 + 64],
                    qrow_t[:, hh * 64:hh * 64 + 64],
                    kr1[:, 0:64], AL.mult), "ve", "ve")
                chain(ctx, ctx.ve.reduce_sum(bbf[:, hh:hh + 1],
                                             RBC[0:1, hh * 64:hh * 64 + 64],
                                             axis=AX.X), "ve", "ve")
            vv = ctx.step(ctx.ve.tensor_scalar(br2, bbf, -1.0 / 16.0, None, AL.mult), "ve")
            ctx.wait(ctx.ve, "ve", vv)
            ctx.wait(ctx.te, "ve", vv)
            mm = ctx.te.matmul(P["C"][:, 0:2], S["ONE"][:], br2, start=True, stop=True)
            te_mm = ctx.step(mm, "te")
            ctx.wait(ctx.ve, "te", te_mm)
            vv_negb = ctx.step(ctx.ve.tensor_copy(negb, P["C"][:, 0:2]), "ve")
            for hh in range(2):
                h = 2 * pr + hh
                qrow = QT[64 * hh:64 * hh + 64, :]
                ysl = YT[:, h * 512:(h + 1) * 512]
                pv_done = [None, None]
                exp_done = None
                gp_done = None
                for kc2 in range((nkc + 1) // 2):
                    sl = kc2 % 2
                    pslot = PT2[:, sl * 1024:(sl + 1) * 1024]
                    njj = min(2, nkc - kc2 * 2)
                    if exp_done is not None:
                        ctx.wait(ctx.te, "se", exp_done)
                    for j in range(njj):
                        kc = kc2 * 2 + j
                        mm = ctx.te.matmul(
                            P["A"][:, j * 512:(j + 1) * 512],
                            KB[64 * hh:64 * hh + 64, kc * 128:(kc + 1) * 128],
                            qrow, start=True, stop=True)
                    te_mm = ctx.step(mm, "te")
                    ctx.wait(ctx.se, "te", te_mm)
                    if pv_done[sl] is not None:
                        ctx.wait(ctx.se, "te", pv_done[sl])
                    if gp_done is not None:
                        ctx.wait(ctx.se, "gq", gp_done)
                    if kc2 == 0:
                        ctx.wait(ctx.se, "ve", vv_negb)
                    exp_done = ctx.step(ctx.se.activation(
                        pslot[:, 0:512 * njj], P["A"][:, 0:512 * njj], AF.Exp,
                        scale=0.125, bias=negb[:, hh:hh + 1]), "se")
                    need_mask = [j for j in range(njj) if (kc2 * 2 + j) >= 4 * qq]
                    if need_mask:
                        ctx.wait(ctx.gp, "se", exp_done)
                        for j in need_mask:
                            kc = kc2 * 2 + j
                            gp_done = ctx.step(ctx.gp.affine_select(
                                pslot[:, j * 512:(j + 1) * 512],
                                pslot[:, j * 512:(j + 1) * 512],
                                pattern=[[1, 512]], compare_op=AL.is_ge,
                                fill=0.0, base=qq * 512 - kc * 128,
                                channel_multiplier=-1), "gq")
                        ctx.wait(ctx.te, "gq", gp_done)
                    ctx.wait(ctx.te, "se", exp_done)
                    for j in range(njj):
                        kc = kc2 * 2 + j
                        mm = ctx.te.matmul(
                            P["D"][0:65, :],
                            VB[:, kc * 65:(kc + 1) * 65],
                            pslot[:, j * 512:(j + 1) * 512],
                            start=(kc == 0), stop=(kc == nkc - 1))
                    te_mm = ctx.step(mm, "te")
                    pv_done[sl] = te_mm
                ctx.wait(ctx.ve, "te", te_mm)
                chain(ctx, ctx.ve.tensor_scalar(P["D"][64:65, :], P["D"][64:65, :],
                                                1e-30, None, AL.add), "ve", "ve")
                chain(ctx, ctx.ve.reciprocal(recf, P["D"][64:65, :]), "ve", "ve")
                vv = ctx.step(ctx.ve.tensor_copy(rrow, recf), "ve")
                ctx.wait(ctx.ve, "ve", vv)
                ctx.wait(ctx.te, "ve", vv)
                mm = ctx.te.matmul(P["C"][0:64, :], oneb[64:65, :], rrow,
                                   start=True, stop=True)
                te_mm = ctx.step(mm, "te")
                ctx.wait(ctx.se, "te", te_mm)
                sev = ctx.step(ctx.se.activation(RBC, P["C"][0:64, :], AF.Copy), "se")
                ctx.wait(ctx.ve, "se", sev)
                vv = ctx.step(ctx.ve.tensor_tensor(ysl, P["D"][0:64, :], RBC, AL.mult),
                              "ve")
                ctx.wait(ctx.te, "ve", vv)
        ctx.wait(ctx.te, "ve")
        cp_r = A["l0_cp"].rearrange("(c p) d -> c p d", p=64)
        te_free = ctx.cnt["te"]
        for t2 in range(2):
            psl = [(P["A"], 0), (P["A"], 512), (P["B"], 0), (P["B"], 512)]
            for yc in range(16):
                ctx.wait(ctx.gp, "te", te_free)
                wsl = wbuf.slot(yc)[0:64, :]
                vw = ctx.dma("gp", wsl, cp_r[yc], "wq")
                ctx.wait(ctx.te, "wq", vw)
                for u in range(2):
                    t4 = t2 * 2 + u
                    for dh in range(2):
                        ps, off = psl[u * 2 + dh]
                        mm = ctx.te.matmul(
                            ps[:, off:off + 512],
                            YT[:, yc * 512 + t4 * 128: yc * 512 + (t4 + 1) * 128],
                            wsl[:, dh * 512:(dh + 1) * 512],
                            start=(yc == 0), stop=(yc == 15))
                te_mm = ctx.step(mm, "te")
                te_free = te_mm
            for u in range(2):
                t4 = t2 * 2 + u
                tt = qq * 4 + t4
                for dh in range(2):
                    ps, off = psl[u * 2 + dh]
                    ctx.wait(ctx.ve, "te", te_free)
                    xsl = X[:, tt * 1024 + dh * 512: tt * 1024 + (dh + 1) * 512]
                    ctx.step(ctx.ve.tensor_tensor(xsl, xsl, ps[:, off:off + 512],
                                                  AL.add), "ve")
            ctx.wait(ctx.te, "ve")
    barrier(ctx, engs=("te", "se", "gp", "sy"), sems=("ve", "gq"))


def emit_l0_down(ctx, A, S, P, q4, UP, wbuf, te_free):
    X, AR = S["X"], S["AR"]
    dn_r = A["l0_dn"].rearrange("(c p) d -> c p d", p=128)
    wb_off2 = 64 * 1024
    slab_use = {}
    for s in range(4):
        slab_use[s] = (None, ctx.cnt["te"])
    for dh in range(2):
        for grp in range(2):
            for yc in range(32):
                s4 = yc % 4
                ctx.wait(ctx.gp, "te", slab_use[s4][1])
                wsl = AR[:, wb_off2 + s4 * 512: wb_off2 + (s4 + 1) * 512]
                vw = ctx.dma("gp", wsl, dn_r[yc][:, dh * 512:(dh + 1) * 512], f"w{s4}")
                ctx.wait(ctx.te, f"w{s4}", vw)
                for u in range(4):
                    tloc = grp * 4 + u
                    ps, off = [(P["A"], 0), (P["A"], 512), (P["B"], 0), (P["B"], 512)][u]
                    mm = ctx.te.matmul(
                        ps[:, off:off + 512],
                        UP[:, yc * 1024 + tloc * 128: yc * 1024 + (tloc + 1) * 128],
                        wsl, start=(yc == 0), stop=(yc == 31))
                te_mm = ctx.step(mm, "te")
                te_free = te_mm
                slab_use[s4] = (vw, te_mm)
            for u in range(4):
                tloc = grp * 4 + u
                tt = q4 * 8 + tloc
                ps, off = [(P["A"], 0), (P["A"], 512), (P["B"], 0), (P["B"], 512)][u]
                ctx.wait(ctx.ve, "te", te_free)
                xsl = X[:, tt * 1024 + dh * 512: tt * 1024 + (dh + 1) * 512]
                ctx.step(ctx.ve.tensor_tensor(xsl, xsl, ps[:, off:off + 512], AL.add),
                         "ve")
            ctx.wait(ctx.te, "ve")
    return te_free


def emit_ffn_peer(ctx, A, S, P):
    X, AR, AFX = S["X"], S["AR"], S["AF"]
    XN = AR[:, 0:32 * 1024]
    up_off = 32 * 1024
    wb_off = 66 * 1024
    wbuf = WStream(ctx, AR[:, wb_off:wb_off + 2048], 1024)
    UP = AR[:, up_off:up_off + 32768]

    barrier(ctx, engs=("te", "ve", "se", "gp"), sems=("ve", "se", "te"))
    te_free = None
    for q4 in range(4 if L0_DENSE else 0):
        tq = q4 * 1024
        for oc in range(32):
            iw, vw = wbuf.load(A["l0_upA"][oc], free_after=te_free)
            ctx.wait(ctx.te, "wq", vw)
            for hh in range(2):
                hs = slice(hh * 512, (hh + 1) * 512)
                for kc in range(8):
                    mm = ctx.te.matmul(P["C"][:] if hh == 0 else P["D"][:],
                                       wbuf.slot(iw)[:, kc * 128:(kc + 1) * 128],
                                       XN[:, kc * 4096 + tq + hs.start:
                                          kc * 4096 + tq + hs.stop],
                                       start=(kc == 0), stop=(kc == 7))
            te_mm = ctx.step(mm, "te")
            te_free = te_mm
            ctx.wait(ctx.se, "te", te_mm)
            sev = ctx.step(ctx.se.activation(UP[:, oc * 1024:oc * 1024 + 512], P["C"][:],
                                             AF.Silu), "se")
            sev = ctx.step(ctx.se.activation(UP[:, oc * 1024 + 512:(oc + 1) * 1024],
                                             P["D"][:], AF.Silu), "se")
            ctx.wait(ctx.te, "se", sev)
        if DBG and q4 == 0:
            ctx.wait(ctx.gp, "se")
            ctx.dma("gp", A["dbg_xn"], UP, "so")
            ctx.wait(ctx.gp, "so")
        te_free = emit_l0_down(ctx, A, S, P, q4, UP, wbuf, te_free)
    barrier(ctx, engs=("te", "ve", "se", "gp", "sy"), sems=("ve", "se", "te", "wq"))

    dt_off = 32 * 1024
    gt_off = dt_off + 8192
    g_off = gt_off + 8192
    b1_off = g_off + 8192
    b2_off = b1_off + 1024
    b3_off = b2_off + 1024
    qp_off = b3_off + 1024
    ky_off = qp_off + 512
    ex_off = ky_off + 512
    DOTS = AR[:, dt_off:dt_off + 8192]
    GT = AR[:, gt_off:gt_off + 8192]
    G = AR[:, g_off:g_off + 4096]
    B1 = AR[:, b1_off:b1_off + 1024]
    B2 = AR[:, b2_off:b2_off + 1024]
    B3 = AR[:, b3_off:b3_off + 1024]
    QP = AR[:, qp_off:qp_off + 512]
    KEYS = AR[:, ky_off:ky_off + 512]
    EXH = AR[:, ex_off:ex_off + 64]
    SIM = AFX[:, 0:512]
    T8X = AFX[:, 512:520]
    T8Y = AFX[:, 520:528]
    T8G = AFX[:, 528:536]
    GRD = AFX[:, 544:608]
    E8 = AFX[:, 608:616]
    ZS = AFX[:, 616:617]
    negmx = AFX[:, 617:618]
    ctx.dma("sy", KEYS, A["l0_keys"], "xq")
    ctx.wait(ctx.te, "xq")
    deT_r = A["l0_deT"]
    ue_r = A["l0_ue"]
    qw_r = A["l0_qA"]

    te_free = None
    slab_use = {}
    for t128 in range(32 if L0_PEER else 0):
        tcol = t128 * 128
        for s in range(4):
            slab_use[s] = (None, ctx.cnt["te"])
        for ec in range(8):
            psm = P["C"] if ec % 2 == 0 else P["D"]
            for kc in range(8):
                i = ec * 8 + kc
                s4 = i % 4
                if s4 in slab_use:
                    ctx.wait(ctx.gp, "te", slab_use[s4][1])
                wsl = AR[:, wb_off + s4 * 512: wb_off + (s4 + 1) * 512]
                vw = ctx.dma("gp", wsl,
                             deT_r[kc * 128:(kc + 1) * 128, ec * 512:(ec + 1) * 512],
                             f"w{s4}")
                ctx.wait(ctx.te, f"w{s4}", vw)
                mm = ctx.te.matmul(psm[:, 0:512],
                                   XN[:, kc * 4096 + tcol: kc * 4096 + tcol + 128],
                                   wsl, start=(kc == 0), stop=(kc == 7))
                te_mm = ctx.step(mm, "te")
                te_free = te_mm
                slab_use[s4] = (vw, te_mm)
            ctx.wait(ctx.se, "te", te_free)
            sev = ctx.step(ctx.se.activation(DOTS[:, ec * 512:(ec + 1) * 512],
                                             psm[:, 0:512], AF.Copy), "se")
            ctx.wait(ctx.te, "se", sev)
        for oc in range(4 if L0_PSTG >= 2 else 0):
            iw, vw = wbuf.load(qw_r[oc], free_after=te_free)
            ctx.wait(ctx.te, "wq", vw)
            psm = P["A"][:, (oc % 2) * 512:(oc % 2) * 512 + 128]
            if oc >= 2:
                ctx.wait(ctx.te, "se", ctx.cnt["se"] - 1)
            for kc in range(8):
                mm = ctx.te.matmul(psm, wbuf.slot(iw)[:, kc * 128:(kc + 1) * 128],
                                   XN[:, kc * 4096 + tcol: kc * 4096 + tcol + 128],
                                   start=(kc == 0), stop=(kc == 7))
            te_mm = ctx.step(mm, "te")
            te_free = te_mm
            ctx.wait(ctx.se, "te", te_mm)
            sev = ctx.step(ctx.se.activation(QP[:, oc * 128:(oc + 1) * 128], psm,
                                             AF.Copy), "se")
        if L0_PSTG >= 2:
            ctx.wait(ctx.te, "se")
            slots = [(P["A"], 0), (P["A"], 512), (P["B"], 0), (P["B"], 512)]
            for w in range(2):
                for i in range(4):
                    blk = w * 4 + i
                    p, hq = blk // 4, blk % 4
                    hb = 64 * (hq % 2)
                    cc = 2 * p + hq // 2
                    ps, off = slots[i]
                    mm = ctx.te.matmul(ps[:, off:off + 64],
                                       QP[hb:hb + 64, cc * 128:(cc + 1) * 128],
                                       KEYS[hb:hb + 64, blk * 64:(blk + 1) * 64],
                                       start=True, stop=True)
                te_mm = ctx.step(mm, "te")
                ctx.wait(ctx.se, "te", te_mm)
                for i in range(4):
                    blk = w * 4 + i
                    ps, off = slots[i]
                    sev = ctx.se.activation(SIM[:, blk * 64:(blk + 1) * 64],
                                            ps[:, off:off + 64], AF.Copy)
                sev = ctx.step(sev, "se")
                ctx.wait(ctx.te, "se", sev)
            ctx.wait(ctx.ve, "se", sev)
        for hq in range(4 if L0_PSTG >= 3 else 0):
            xs = slice(hq * 64, (hq + 1) * 64)
            ys = slice(256 + hq * 64, 256 + (hq + 1) * 64)
            chain(ctx, ctx.ve.max(out=T8X, in_=SIM[:, xs]), "ve", "ve")
            chain(ctx, ctx.ve.max(out=T8Y, in_=SIM[:, ys]), "ve", "ve")
            g3 = GRD.rearrange("p (i j) -> p i j", i=8)
            chain(ctx, ctx.ve.tensor_copy(g3, T8Y.unsqueeze(1).to_broadcast([128, 8, 8])),
                  "ve", "ve")
            chain(ctx, ctx.ve.tensor_tensor(g3, g3,
                                            T8X.unsqueeze(2).to_broadcast([128, 8, 8]),
                                            AL.add), "ve", "ve")
            chain(ctx, ctx.ve.max(out=T8G, in_=GRD), "ve", "ve")
            vv = ctx.step(ctx.ve.tensor_scalar(negmx, T8G[:, 0:1], -1.0, None, AL.mult),
                          "ve")
            ctx.wait(ctx.ve, "ve", vv)
            ctx.wait(ctx.se, "ve", vv)
            chain(ctx, ctx.se.activation(E8, T8G, AF.Exp, bias=negmx, accum_out=ZS),
                  "se", "se")
            sev = ctx.step(ctx.se.activation(EXH, SIM[:, xs], AF.Exp, bias=negmx), "se")
            ctx.wait(ctx.ve, "se", sev)
            chain(ctx, ctx.ve.reciprocal(ZS, ZS), "ve", "ve")
            for ech in range(4):
                es = slice(ech * 1024, (ech + 1) * 1024)
                sev = ctx.step(ctx.se.activation(
                    B1.rearrange("p (i j) -> p i j", j=64),
                    SIM[:, ys].unsqueeze(1).to_broadcast([128, 16, 64]), AF.Exp), "se")
                ctx.wait(ctx.ve, "se", sev)
                exv = EXH[:, ech * 16:(ech + 1) * 16]
                chain(ctx, ctx.ve.tensor_tensor(
                    B2.rearrange("p (i j) -> p i j", j=64),
                    B1.rearrange("p (i j) -> p i j", j=64),
                    exv.unsqueeze(2).to_broadcast([128, 16, 64]), AL.mult), "ve", "ve")
                chain(ctx, ctx.ve.scalar_tensor_tensor(B3, B2, E8[:, 7:8], B2,
                                                       AL.is_ge, AL.mult), "ve", "ve")
                chain(ctx, ctx.ve.scalar_tensor_tensor(B1, DOTS[:, es], ZS, B3,
                                                       AL.mult, AL.mult), "ve", "ve")
                ctx.wait(ctx.se, "ve")
                sev = ctx.step(ctx.se.activation(B2, B1, AF.Tanh, scale=0.5), "se")
                ctx.wait(ctx.ve, "se", sev)
                chain(ctx, ctx.ve.scalar_tensor_tensor(B3, B2, 1.0, B1, AL.add, AL.mult),
                      "ve", "ve")
                if hq == 0:
                    vv = ctx.step(ctx.ve.tensor_scalar(G[:, es], B3, 0.5, None, AL.mult),
                                  "ve")
                else:
                    vv = ctx.step(ctx.ve.scalar_tensor_tensor(G[:, es], B3, 0.5,
                                                              G[:, es], AL.mult, AL.add),
                                  "ve")
                ctx.wait(ctx.ve, "ve", vv)
                ctx.wait(ctx.se, "ve", vv)
        ctx.wait(ctx.te, "ve")
        for half in range(4 if L0_PSTG >= 4 else 0):
            bb = half % 2
            if half >= 2:
                ctx.wait(ctx.te, "ve", gt_ve[half - 2])
            for blk in range(8):
                c = half * 8 + blk
                tr = ctx.te.transpose(P["T"][:, bb * 1024 + blk * 128: bb * 1024 + (blk + 1) * 128],
                                      G[:, c * 128:(c + 1) * 128], S["CID"][:])
            tr_v = ctx.step(tr, "te")
            ctx.wait(ctx.ve, "te", tr_v)
            for blk in range(8):
                c = half * 8 + blk
                cp = ctx.ve.tensor_copy(GT[:, c * 128:(c + 1) * 128],
                                        P["T"][:, bb * 1024 + blk * 128: bb * 1024 + (blk + 1) * 128])
            if half == 0:
                gt_ve = {}
            gt_ve[half] = ctx.step(cp, "ve")
        ctx.wait(ctx.te, "ve")
        for s in range(4):
            slab_use[s] = (None, ctx.cnt["te"])
        for dc in range(2 if L0_PSTG >= 5 else 0):
            psm = P["A"] if dc == 0 else P["B"]
            for kc in range(32):
                i = dc * 32 + kc
                s4 = i % 4
                if s4 in slab_use:
                    ctx.wait(ctx.gp, "te", slab_use[s4][1])
                wsl = AR[:, wb_off + s4 * 512: wb_off + (s4 + 1) * 512]
                vw = ctx.dma("gp", wsl,
                             ue_r[kc * 128:(kc + 1) * 128, dc * 512:(dc + 1) * 512], f"w{s4}")
                ctx.wait(ctx.te, f"w{s4}", vw)
                mm = ctx.te.matmul(psm[:, 0:512], GT[:, kc * 128:(kc + 1) * 128],
                                   wsl, start=(kc == 0), stop=(kc == 31))
                te_mm = ctx.step(mm, "te")
                te_free = te_mm
                slab_use[s4] = (vw, te_mm)
            ctx.wait(ctx.ve, "te", te_free)
            xsl = X[:, t128 * 1024 + dc * 512: t128 * 1024 + (dc + 1) * 512]
            vv = ctx.step(ctx.ve.tensor_tensor(xsl, xsl, psm[:, 0:512], AL.add), "ve")
            ctx.wait(ctx.te, "ve", vv)
    barrier(ctx, engs=("te", "se", "gp", "sy"), sems=("ve", "se"))


ATTN = {0: emit_attn_l0, 1: emit_attn_l1, 2: emit_attn_l2, 3: emit_attn_l3}
FFN = {0: emit_ffn_peer, 1: emit_ffn_moe, 2: make_ffn_ternary(2), 3: make_ffn_ternary(3)}



def kernel(**inputs):
    shared = host_prepare(inputs)
    x = np.ascontiguousarray(np.asarray(inputs["L0_attn_x"], dtype=np.float32))
    B = x.shape[0]
    cur = [np.ascontiguousarray(x[b]) for b in range(B)]
    if FUSED:
        groups = [list(BLOCKS)]
    else:
        groups = [[li] for li in BLOCKS]
    for gblocks in groups:
        nc = build_program(gblocks, shared)
        in_maps = [dict(shared, x_in=cur[b]) for b in range(B)]
        res = run_bass_kernel_spmd(nc, in_maps, core_ids=list(range(B)))
        cur = [np.asarray(res.results[b]["x_out"], dtype=np.float32) for b in range(B)]
    return np.stack(cur).reshape(B, T, D)


# version 03 (final)
# speedup vs baseline: 1.0016x; 1.0016x over previous
import numpy as np
import ml_dtypes
from contextlib import ExitStack

import concourse.bass as bass
import concourse.mybir as mybir
from concourse.bass_utils import run_bass_kernel_spmd

F32 = mybir.dt.float32
BF16 = mybir.dt.bfloat16
AL = mybir.AluOpType
AF = mybir.ActivationFunctionType
AX = mybir.AxisListType

T = 4096
D = 1024
NTT = T // 128
BF = np.dtype(ml_dtypes.bfloat16)

SELU_L = 1.0507009873554804934193349852946
SELU_A = 1.6732632423543772848170429916717

BLOCKS = [0, 1, 2, 3]
FUSED = True
DO_ATTN = True
DO_FFN = True
L0_DENSE = True
L0_PEER = True
L0_PSTG = 99
L0_CONSV = True
LN_LIMIT = 2
DBG = False


def _packA(W, of_tile=128):
    IF, OF = W.shape
    return np.ascontiguousarray(
        W.reshape(IF // 128, 128, OF // of_tile, of_tile)
         .transpose(2, 1, 0, 3)
         .reshape(OF // of_tile, 128, (IF // 128) * of_tile))


def _col128(v):
    v = np.asarray(v, np.float32).reshape(-1)
    return np.ascontiguousarray(v.reshape(v.shape[0] // 128, 128).T)


def _row(v):
    return np.ascontiguousarray(np.asarray(v, np.float32).reshape(1, -1))


def host_prepare(inputs):
    g = {}
    f32 = lambda a: np.ascontiguousarray(np.asarray(a, dtype=np.float32))

    g["c_ident"] = np.eye(128, dtype=BF)
    g["c_ones1"] = np.ones((1, 128), dtype=BF)

    g["l0_cdA"] = _packA(f32(inputs["L0_attn_Wc_down"]))
    g["l0_quA"] = _packA(f32(inputs["L0_attn_Wq_up"]))
    g["l0_kA"] = _packA(np.ascontiguousarray(f32(inputs["L0_attn_Wkv_up"])[:, :64]), of_tile=64)
    g["l0_v"] = np.ascontiguousarray(f32(inputs["L0_attn_Wkv_up"])[:, 64:])
    g["l0_cp"] = f32(inputs["L0_attn_Wc_proj"])

    qswp = np.arange(1024).reshape(8, 2, 64)[:, ::-1].reshape(-1)
    g["l0_quAs"] = _packA(np.ascontiguousarray(f32(inputs["L0_attn_Wq_up"])[:, qswp]))
    g["l0_upA"] = _packA(f32(inputs["L0_ffn_W_up"]).T)
    g["l0_dn"] = np.ascontiguousarray(f32(inputs["L0_ffn_W_down"]).T)
    g["l0_qA"] = _packA(f32(inputs["L0_ffn_W_q"]).T)
    keys = f32(inputs["L0_ffn_keys"])
    karr = keys.transpose(2, 0, 3, 1).reshape(8, 64, 64)
    kdup = np.concatenate([karr, karr], axis=1)
    g["l0_keys"] = np.ascontiguousarray(
        kdup.transpose(1, 0, 2).reshape(128, 512).astype(BF))
    g["l0_deT"] = np.ascontiguousarray(f32(inputs["L0_ffn_down_embed"]).T)
    g["l0_ue"] = f32(inputs["L0_ffn_up_embed"])

    Wqk = f32(inputs["L1_attn_W_qk"])
    bqk = f32(inputs["L1_attn_b_qk"])
    qr = np.empty(1024, np.int64); kr = np.empty(1024, np.int64)
    for h in range(16):
        for d in range(64):
            qr[h * 64 + d] = h * 128 + 2 * d
            kr[h * 64 + d] = h * 128 + 2 * d + 1
    g["l1_qA"] = _packA(np.ascontiguousarray(Wqk[qr, :].T), of_tile=64)
    g["l1_kA"] = _packA(np.ascontiguousarray(Wqk[kr, :].T), of_tile=64)
    g["l1_bqc"] = _col128(np.pad(bqk[qr].reshape(16, 64), ((0, 0), (0, 64))).reshape(-1))
    g["l1_bkc"] = _col128(np.pad(bqk[kr].reshape(16, 64), ((0, 0), (0, 64))).reshape(-1))

    g["l1_wr"] = f32(inputs["L1_ffn_Wr"])
    g["l1_brr"] = _row(inputs["L1_ffn_br"])
    W1 = f32(inputs["L1_ffn_W1"]); W2 = f32(inputs["L1_ffn_W2"])
    g["l1_w1A"] = np.ascontiguousarray(np.stack([_packA(W1[e]) for e in range(8)]))
    g["l1_w2"] = W2
    b1 = f32(inputs["L1_ffn_b1"]); b2 = f32(inputs["L1_ffn_b2"])
    g["l1_b1c"] = np.ascontiguousarray(np.stack([_col128(b1[e]) for e in range(8)]))
    g["l1_b2r"] = np.ascontiguousarray(np.stack([_row(b2[e]) for e in range(8)]))

    g["l2_qA"] = _packA(f32(inputs["L2_attn_Wq"]))
    g["l2_kA"] = _packA(f32(inputs["L2_attn_Wk"]))
    qswp = np.arange(1024).reshape(8, 2, 64)[:, ::-1].reshape(-1)
    g["l2_qAs"] = _packA(np.ascontiguousarray(f32(inputs["L2_attn_Wq"])[:, qswp]))
    g["l2_bqcs"] = _col128(np.asarray(inputs["L2_attn_bq"], np.float32)[qswp])
    g["l2_v"] = f32(inputs["L2_attn_Wv"])
    perm = np.empty(1024, np.int64)
    for h in range(16):
        for d in range(64):
            perm[h * 64 + d] = (h % 4) * 256 + (h // 4) * 64 + d
    g["l2_wo"] = np.ascontiguousarray(f32(inputs["L2_attn_Wo"])[perm, :])
    g["l2_bqc"] = _col128(inputs["L2_attn_bq"])
    g["l2_bkc"] = _col128(inputs["L2_attn_bk"])
    g["l2_bvr"] = _row(inputs["L2_attn_bv"])
    g["l2_bor"] = _row(inputs["L2_attn_bo"])

    g["l3_qA"] = _packA(np.ascontiguousarray(f32(inputs["L3_attn_Wq"]).T))
    g["l3_kA"] = _packA(np.ascontiguousarray(f32(inputs["L3_attn_Wk"]).T))
    g["l3_v"] = np.ascontiguousarray(f32(inputs["L3_attn_Wv"]).T)
    g["l3_wu"] = np.ascontiguousarray(f32(inputs["L3_attn_Wu"]).T)
    g["l3_bur"] = _row(inputs["L3_attn_bu"])
    half = 512
    theta = 1.0 / (10000.0 ** (np.arange(half, dtype=np.float64) / half))
    ang = np.arange(T, dtype=np.float64)[:, None] * theta[None, :]
    cosT = np.cos(ang).astype(np.float32).T
    sinT = np.sin(ang).astype(np.float32).T
    ctab = np.empty((1024, T), np.float32); stab = np.empty((1024, T), np.float32)
    ctab[0::2] = cosT; ctab[1::2] = cosT
    stab[0::2] = -sinT; stab[1::2] = sinT
    g["l3_cos"] = ctab; g["l3_sin2"] = stab
    psw = np.zeros((128, 128), dtype=BF)
    for p in range(128):
        psw[p ^ 1, p] = 1.0
    g["c_psw"] = psw

    for L in (2, 3):
        p, o = f"L{L}_ffn_", f"t{L}_"
        g[o + "w1A"] = _packA(f32(inputs[p + "w1_w"]).T)
        g[o + "w2A"] = _packA(f32(inputs[p + "w2_w"]).T)
        g[o + "h1A"] = _packA(f32(inputs[p + "h1_w"]).T)
        g[o + "h2A"] = _packA(f32(inputs[p + "h2_w"]).T)
        g[o + "w3A"] = _packA(f32(inputs[p + "w3_w"]).T)
        g[o + "s1c"] = _col128(inputs[p + "w1_s"])
        g[o + "s2c"] = _col128(inputs[p + "w2_s"])
        g[o + "h1sc"] = _col128(inputs[p + "h1_s"])
        g[o + "h2sc"] = _col128(inputs[p + "h2_s"])
        g[o + "s3c"] = _col128(inputs[p + "w3_s"])
    return g



class _DummyInst:
    def then_inc(self, *a, **k):
        return self


_DI = _DummyInst()


class _DummyEng:
    def __getattr__(self, name):
        def f(*a, **k):
            return _DI
        return f


_DE = _DummyEng()

ROLE_ATTR = {"tensor": "te", "vector": "ve", "scalar": "se",
             "gpsimd": "gp", "sync": "sy"}


class Ctx:
    def __init__(self, nc, sems):
        self.nc = nc
        self.sems = sems

    def begin(self, role, eng):
        self.role = role
        self.cnt = dict((s, 0) for s in self.sems)
        for r, at in ROLE_ATTR.items():
            setattr(self, at, eng if r == role else _DE)

    def step(self, instr, sem, n=1):
        self.cnt[sem] += n
        if instr is not _DI and instr is not None:
            instr.then_inc(self.sems[sem], n)
        return self.cnt[sem]

    def wait(self, eng, sem, val=None):
        v = self.cnt[sem] if val is None else val
        if v > 0:
            eng.wait_ge(self.sems[sem], v)

    def dma(self, q, out, in_, sem):
        eng = self.gp if q == "gp" else self.sy
        ins = eng.dma_start(out=out, in_=in_)
        return self.step(ins, sem, 16)


def barrier(ctx, engs, sems):
    for en in engs:
        e = getattr(ctx, en)
        for sm in sems:
            ctx.wait(e, sm)


def chain(ctx, instr, sem, eng_at):
    v = ctx.step(instr, sem)
    ctx.wait(getattr(ctx, eng_at), sem, v)
    return v


class WStream:

    def __init__(self, ctx, buf, cols):
        self.ctx, self.buf, self.cols = ctx, buf, cols
        self.n = 0

    def load(self, src_ap, free_after=None):
        ctx = self.ctx
        if free_after is not None:
            ctx.wait(ctx.gp, "te", free_after)
        i = self.n
        b = (i % 2) * self.cols
        w = src_ap.shape[-1] if len(src_ap.shape) == 2 else self.cols
        wq = ctx.dma("gp", self.buf[:, b:b + w], src_ap, "wq")
        self.n += 1
        return i, wq

    def slot(self, i):
        b = (i % 2) * self.cols
        return self.buf[:, b:b + self.cols]


AR_COLS = 69 * 1024


def build_program(blocks, shared):
    nc = bass.Bass("TRN2", target_bir_lowering=False)
    A = {}
    A["x_in"] = nc.dram_tensor("x_in", [T, D], F32, kind="ExternalInput").ap()
    A["x_out"] = nc.dram_tensor("x_out", [T, D], F32, kind="ExternalOutput").ap()
    for name, arr in shared.items():
        A[name] = nc.dram_tensor(name, list(arr.shape),
                                 BF16 if arr.dtype == BF else F32,
                                 kind="ExternalInput").ap()
    A["xnsp"] = nc.dram_tensor("xnsp", [128, 8 * T], BF16, kind="Internal").ap()
    if 3 in blocks:
        A["l3qk"] = nc.dram_tensor("l3qk", [2, 128, 8 * T], BF16, kind="Internal").ap()
        A["l3v"] = nc.dram_tensor("l3v", [16, 32, 128, 65], BF16, kind="Internal").ap()
    if DBG:
        A["dbg"] = nc.dram_tensor("dbg", [128, 512], F32, kind="ExternalOutput").ap()
        A["dbg_xn"] = nc.dram_tensor("dbg_xn", [128, 32768], BF16, kind="ExternalOutput").ap()
        A["dbg_kt"] = nc.dram_tensor("dbg_kt", [128, 8192], BF16, kind="ExternalOutput").ap()

    with ExitStack() as st:
        S = {}
        S["X"] = st.enter_context(nc.sbuf_tensor("X", [128, NTT * 1024], BF16))
        S["AR"] = st.enter_context(nc.sbuf_tensor("AR", [128, AR_COLS - (1024 if DBG else 0)], BF16))
        S["AF"] = st.enter_context(nc.sbuf_tensor("AFX", [128, 1024], F32))
        S["CID"] = st.enter_context(nc.sbuf_tensor("CID", [128, 128], BF16))
        S["PSW"] = st.enter_context(nc.sbuf_tensor("PSW", [128, 128], BF16))
        S["ONE"] = st.enter_context(nc.sbuf_tensor("ONEX", [1, 128], BF16))
        S["STAT"] = st.enter_context(nc.sbuf_tensor("STAT", [128, 128], F32))
        if DBG:
            S["DBG"] = st.enter_context(nc.sbuf_tensor("DBGT", [128, 512], F32))
        P = {}
        P["A"] = st.enter_context(nc.psum_tensor("PSA", [128, 1024], F32))
        P["B"] = st.enter_context(nc.psum_tensor("PSB", [128, 1024], F32))
        P["C"] = st.enter_context(nc.psum_tensor("PSC", [128, 512], F32))
        P["D"] = st.enter_context(nc.psum_tensor("PSD", [128, 512], F32))
        P["T"] = st.enter_context(nc.psum_tensor("PST", [128, 2048], BF16))

        sems = dict((n, st.enter_context(nc.semaphore(n)))
                    for n in ("wq", "xq", "te", "ve", "se", "gq", "st", "so",
                              "w0", "w1", "w2", "w3"))
        block = st.enter_context(nc.Block())
        ctx = Ctx(nc, sems)

        def program(ctx):
            emit_prologue(ctx, A, S, P)
            for li in blocks:
                emit_ln_xn(ctx, A, S, P)
                if DO_ATTN:
                    ATTN[li](ctx, A, S, P)
                if DO_FFN:
                    FFN[li](ctx, A, S, P)
            emit_epilogue(ctx, A, S, P)

        @block.tensor
        def _(e):
            ctx.begin("tensor", e); program(ctx)

        @block.vector
        def _(e):
            ctx.begin("vector", e); program(ctx)

        @block.scalar
        def _(e):
            ctx.begin("scalar", e); program(ctx)

        @block.gpsimd
        def _(e):
            ctx.begin("gpsimd", e); program(ctx)

        @block.sync
        def _(e):
            ctx.begin("sync", e); program(ctx)

    return nc


def emit_prologue(ctx, A, S, P):
    if DBG:
        ctx.step(ctx.ve.memset(S["DBG"][:], 0.0), "ve")
    ctx.dma("sy", S["CID"][:], A["c_ident"], "xq")
    ctx.dma("sy", S["PSW"][:], A["c_psw"], "xq")
    ctx.dma("sy", S["ONE"][:], A["c_ones1"], "xq")
    src = A["x_in"].rearrange("(t p) d -> p t d", p=128)
    dst = S["X"][:].rearrange("p (t d) -> p t d", d=1024)
    ctx.dma("gp", dst, src, "wq")
    for en in ("te", "ve", "se"):
        e = getattr(ctx, en)
        ctx.wait(e, "xq")
        ctx.wait(e, "wq")


def emit_epilogue(ctx, A, S, P):
    dst = A["x_out"].rearrange("(t p) d -> p t d", p=128)
    src = S["X"][:].rearrange("p (t d) -> p t d", d=1024)
    ctx.wait(ctx.gp, "ve")
    ctx.wait(ctx.gp, "se")
    ctx.dma("gp", dst, src, "so")
    if DBG:
        ctx.dma("gp", A["dbg"], S["DBG"][:], "so")
    ctx.wait(ctx.gp, "so")



def emit_ln_xn(ctx, A, S, P):
    X, ST, AR = S["X"], S["STAT"], S["AR"]
    XN = AR[:, 0:NTT * 1024]
    jnk = AR[:, 64 * 1024:65 * 1024]
    barrier(ctx, engs=("ve", "se"), sems=("ve", "se", "te"))
    for t in range(NTT):
        xt = X[:, t * 1024:(t + 1) * 1024]
        ctx.step(ctx.ve.reduce_sum(ST[:, t:t + 1], xt, axis=AX.X), "ve")
        ctx.step(ctx.se.activation(jnk, xt, AF.Square,
                                   accum_out=ST[:, 32 + t:33 + t]), "se")
    if LN_LIMIT < 1:
        barrier(ctx, engs=("te", "gp", "sy"), sems=("ve", "se"))
        return
    ctx.wait(ctx.ve, "se")
    ctx.wait(ctx.ve, "ve")
    chain(ctx, ctx.ve.tensor_scalar(ST[:, 0:32], ST[:, 0:32], 1.0 / 1024.0, None, AL.mult), "ve", "ve")
    chain(ctx, ctx.ve.tensor_tensor(ST[:, 96:128], ST[:, 0:32], ST[:, 0:32], AL.mult), "ve", "ve")
    chain(ctx, ctx.ve.scalar_tensor_tensor(ST[:, 32:64], ST[:, 32:64], 1.0 / 1024.0,
                                           ST[:, 96:128], AL.mult, AL.subtract), "ve", "ve")
    chain(ctx, ctx.ve.tensor_scalar(ST[:, 32:64], ST[:, 32:64], 1e-5, None, AL.add), "ve", "ve")
    ctx.wait(ctx.se, "ve")
    chain(ctx, ctx.se.activation(ST[:, 64:96], ST[:, 32:64], AF.Sqrt), "se", "se")
    ctx.wait(ctx.ve, "se")
    chain(ctx, ctx.ve.reciprocal(ST[:, 64:96], ST[:, 64:96]), "ve", "ve")
    chain(ctx, ctx.ve.scalar_tensor_tensor(ST[:, 96:128], ST[:, 0:32], -1.0,
                                           ST[:, 64:96], AL.mult, AL.mult), "ve", "ve")
    ctx.wait(ctx.se, "ve")
    if LN_LIMIT < 2:
        barrier(ctx, engs=("te", "se", "gp", "sy"), sems=("ve",))
        return
    xp = [AR[:, 65 * 1024:66 * 1024], AR[:, 66 * 1024:67 * 1024]]
    te_tile = {}
    ve_tile = {}
    for t in range(NTT):
        b = t % 2
        xt = X[:, t * 1024:(t + 1) * 1024]
        if t >= 2:
            ctx.wait(ctx.se, "te", te_tile[t - 2])
        sev = ctx.step(ctx.se.activation(xp[b], xt, AF.Identity,
                                         bias=ST[:, 96 + t:97 + t],
                                         scale=ST[:, 64 + t:65 + t]), "se")
        ctx.wait(ctx.te, "se", sev)
        if t >= 2:
            ctx.wait(ctx.te, "ve", ve_tile[t - 2])
        for c in range(8):
            tr = ctx.te.transpose(P["T"][:, b * 1024 + c * 128: b * 1024 + (c + 1) * 128],
                                  xp[b][:, c * 128:(c + 1) * 128], S["CID"][:])
        te_tile[t] = ctx.step(tr, "te")
        ctx.wait(ctx.ve, "te", te_tile[t])
        for c in range(8):
            cp = ctx.ve.tensor_copy(
                XN[:, c * 4096 + t * 128: c * 4096 + (t + 1) * 128],
                P["T"][:, b * 1024 + c * 128: b * 1024 + (c + 1) * 128])
        ve_tile[t] = ctx.step(cp, "ve")
    barrier(ctx, engs=("te", "se", "gp", "sy"), sems=("ve",))



def make_ffn_ternary(li):
    o = f"t{li}_"

    def emit(ctx, A, S, P):
        X, AR, AFX = S["X"], S["AR"], S["AF"]
        XN = AR[:, 0:32 * 1024]
        aoff, boff = 8192, 40960
        woff = 65536
        joff = 65536 + 4096

        def a_chunk(c):
            return AR[:, aoff + c * 1024: aoff + (c + 1) * 1024]

        def b_chunk(c):
            if c < 24:
                return AR[:, boff + c * 1024: boff + (c + 1) * 1024]
            return AR[:, (c - 24) * 1024: (c - 23) * 1024]

        jnk = AR[:, joff:joff + 1024]
        wbuf1 = WStream(ctx, AR[:, woff:woff + 2048], 1024)
        wbuf = WStream(ctx, AR[:, woff:woff + 4096], 2048)

        barrier(ctx, engs=("sy",), sems=("te", "ve", "se"))
        ctx.dma("sy", A["xnsp"][:, :], XN, "st")
        spilled = ctx.cnt["st"]
        for en in ("te", "ve", "se", "gp"):
            ctx.wait(getattr(ctx, en), "st", spilled)

        s1c, s2c = AFX[:, 0:32], AFX[:, 32:64]
        h1c, h2c = AFX[:, 64:96], AFX[:, 96:128]
        s3c = AFX[:, 128:136]
        for nm, dst in ((o + "s1c", s1c), (o + "s2c", s2c), (o + "h1sc", h1c),
                        (o + "h2sc", h2c), (o + "s3c", s3c)):
            ctx.dma("sy", dst, A[nm], "xq")
        for en in ("se", "ve"):
            ctx.wait(getattr(ctx, en), "xq")

        te_free = None
        for q in range(4):
            xnq_src = A["xnsp"].rearrange("p (c t) -> p c t", t=T)[:, :, q * 1024:(q + 1) * 1024]
            xnq_dst = AR[:, 0:8192].rearrange("p (c t) -> p c t", t=1024)
            if q > 0:
                ctx.wait(ctx.sy, "te")
            xq_v = ctx.dma("sy", xnq_dst, xnq_src, "xq")
            ctx.wait(ctx.te, "xq", xq_v)

            def XNq(c, ts):
                return AR[:, c * 1024 + ts.start: c * 1024 + ts.stop]

            for oc in range(32):
                i1, v1 = wbuf1.load(A[o + "w1A"][oc], free_after=te_free)
                i2, v2 = wbuf1.load(A[o + "w2A"][oc], free_after=te_free)
                ctx.wait(ctx.te, "wq", v1)
                for hh in range(2):
                    ts = slice(hh * 512, (hh + 1) * 512)
                    for kc in range(8):
                        mm = ctx.te.matmul(P["A"][:, ts],
                                           wbuf1.slot(i1)[:, kc * 128:(kc + 1) * 128],
                                           XNq(kc, ts), start=(kc == 0), stop=(kc == 7))
                ctx.step(mm, "te")
                ctx.wait(ctx.te, "wq", v2)
                for hh in range(2):
                    ts = slice(hh * 512, (hh + 1) * 512)
                    for kc in range(8):
                        mm = ctx.te.matmul(P["B"][:, ts],
                                           wbuf1.slot(i2)[:, kc * 128:(kc + 1) * 128],
                                           XNq(kc, ts), start=(kc == 0), stop=(kc == 7))
                te_mm = ctx.step(mm, "te")
                te_free = te_mm
                ctx.wait(ctx.se, "te", te_mm)
                sev = ctx.step(ctx.se.activation(jnk, P["A"][:], AF.Silu,
                                                 scale=s1c[:, oc:oc + 1]), "se")
                ctx.wait(ctx.ve, "se", sev)
                vev = ctx.step(ctx.ve.scalar_tensor_tensor(
                    a_chunk(oc), P["B"][:], s2c[:, oc:oc + 1], jnk, AL.mult, AL.mult), "ve")
                ctx.wait(ctx.te, "ve", vev)

            for wname, scol, srcf, dstf in ((o + "h1A", h1c, a_chunk, b_chunk),
                                            (o + "h2A", h2c, b_chunk, a_chunk)):
                for oc in range(32):
                    ih, vh = wbuf.load(A[wname][oc][:, 0:2048], free_after=te_free)
                    ih2, vh2 = wbuf.load(A[wname][oc][:, 2048:4096], free_after=None)
                    ctx.wait(ctx.te, "wq", vh)
                    for hh in range(2):
                        ts = slice(hh * 512, (hh + 1) * 512)
                        for kc in range(16):
                            mm = ctx.te.matmul(P["A"][:, ts],
                                               wbuf.slot(ih)[:, kc * 128:(kc + 1) * 128],
                                               srcf(kc)[:, ts], start=(kc == 0), stop=False)
                    ctx.step(mm, "te")
                    ctx.wait(ctx.te, "wq", vh2)
                    for hh in range(2):
                        ts = slice(hh * 512, (hh + 1) * 512)
                        for kc in range(16):
                            mm = ctx.te.matmul(P["A"][:, ts],
                                               wbuf.slot(ih2)[:, kc * 128:(kc + 1) * 128],
                                               srcf(16 + kc)[:, ts], start=False, stop=(kc == 15))
                    te_mm = ctx.step(mm, "te")
                    te_free = te_mm
                    ctx.wait(ctx.se, "te", te_mm)
                    sev = ctx.step(ctx.se.activation(dstf(oc), P["A"][:], AF.Gelu,
                                                     scale=scol[:, oc:oc + 1]), "se")
                    ctx.wait(ctx.te, "se", sev)

            s4_ve = {}
            for oc in range(8):
                iw, vw = wbuf.load(A[o + "w3A"][oc][:, 0:2048], free_after=te_free)
                iw2, vw2 = wbuf.load(A[o + "w3A"][oc][:, 2048:4096], free_after=None)
                ctx.wait(ctx.te, "wq", vw)
                for hh in range(2):
                    ts = slice(hh * 512, (hh + 1) * 512)
                    for kc in range(16):
                        mm = ctx.te.matmul(P["B"][:, ts],
                                           wbuf.slot(iw)[:, kc * 128:(kc + 1) * 128],
                                           a_chunk(kc)[:, ts], start=(kc == 0), stop=False)
                ctx.step(mm, "te")
                ctx.wait(ctx.te, "wq", vw2)
                for hh in range(2):
                    ts = slice(hh * 512, (hh + 1) * 512)
                    for kc in range(16):
                        mm = ctx.te.matmul(P["B"][:, ts],
                                           wbuf.slot(iw2)[:, kc * 128:(kc + 1) * 128],
                                           a_chunk(16 + kc)[:, ts], start=False, stop=(kc == 15))
                te_mm = ctx.step(mm, "te")
                te_free = te_mm
                ctx.wait(ctx.se, "te", te_mm)
                sev = ctx.step(ctx.se.activation(jnk, P["B"][:], AF.Identity,
                                                 scale=s3c[:, oc:oc + 1]), "se")
                ctx.wait(ctx.te, "se", sev)
                bb = oc % 2
                if oc >= 2:
                    ctx.wait(ctx.te, "ve", s4_ve[oc - 2])
                for tb in range(8):
                    tr = ctx.te.transpose(
                        P["T"][:, bb * 1024 + tb * 128: bb * 1024 + (tb + 1) * 128],
                        jnk[:, tb * 128:(tb + 1) * 128], S["CID"][:])
                s4_te = ctx.step(tr, "te")
                ctx.wait(ctx.ve, "te", s4_te)
                for tb in range(8):
                    tt = q * 8 + tb
                    xsl = X[:, tt * 1024 + oc * 128: tt * 1024 + (oc + 1) * 128]
                    ad = ctx.ve.tensor_tensor(
                        xsl, xsl,
                        P["T"][:, bb * 1024 + tb * 128: bb * 1024 + (tb + 1) * 128],
                        AL.add)
                s4_ve[oc] = ctx.step(ad, "ve")
                ctx.wait(ctx.se, "ve", s4_ve[oc])
        barrier(ctx, engs=("te", "se", "gp", "sy"), sems=("ve",))

    return emit



def emit_attn_l2(ctx, A, S, P):
    X, AR, AFX = S["X"], S["AR"], S["AF"]
    XN = AR[:, 0:32 * 1024]
    kt_off = 32 * 1024
    vt_off = kt_off + 8192
    q_off = vt_off + 8320
    y_off = q_off + 1024
    p_off = y_off + 8192
    r_off = p_off + 2048
    wb_off = r_off + 1024
    rb_off = wb_off + 2048
    xb_off = rb_off + 512

    KT = AR[:, kt_off:kt_off + 8192]
    QT = AR[:, q_off:q_off + 512]
    QTs = AR[:, q_off + 512:q_off + 1024]
    YT = AR[0:64, y_off:y_off + 8192]
    PT2 = AR[:, p_off:p_off + 2048]
    RBC = AR[0:64, rb_off:rb_off + 512]
    wbuf = WStream(ctx, AR[:, wb_off:wb_off + 2048], 1024)

    kbc = AFX[:, 160:162]
    qbc = AFX[:, 162:170]
    qbcs = AFX[:, 146:154]
    km2 = AFX[:, 170:172]
    qmaxf = AFX[:, 172:173]
    negb = AFX[:, 184:186]
    recf = AFX[64:65, 192:192 + 512]
    bbf = AFX[0:1, 704:706]
    ctx.dma("sy", kbc, A["l2_bkc"], "xq")
    ctx.dma("sy", qbc, A["l2_bqc"], "xq")
    ctx.dma("sy", qbcs, A["l2_bqcs"], "xq")
    bvr = AR[0:1, r_off:r_off + 256]
    rrow = AR[64:65, r_off + 256:r_off + 768]
    km2b = AR[:, r_off + 768:r_off + 770]
    qmb = AR[:, r_off + 770:r_off + 771]
    br2 = AR[0:1, r_off + 772:r_off + 774]
    qrow_t = AR[0:1, r_off + 774:r_off + 774 + 128]
    kr2 = AR[0:1, xb_off:xb_off + 256]
    oneb = AR[:, xb_off + 256:xb_off + 256 + 64]
    bor = AR[0:1, xb_off + 512:xb_off + 512 + 1024]
    vv = ctx.step(ctx.ve.memset(oneb, 1.0), "ve")
    ctx.dma("gp", bvr, A["l2_bvr"], "wq")
    ctx.dma("gp", bor, A["l2_bor"], "wq")
    ctx.wait(ctx.se, "xq")

    te_free = None
    for oc in range(2):
        iw, vw = wbuf.load(A["l2_kA"][oc], free_after=te_free)
        ctx.wait(ctx.te, "wq", vw)
        for tt in range(8):
            ts = slice(tt * 512, (tt + 1) * 512)
            ps = P["C"] if tt % 2 == 0 else P["D"]
            if tt >= 2:
                ctx.wait(ctx.te, "se", ctx.cnt["se"] - 1)
            for kc in range(8):
                mm = ctx.te.matmul(ps[:], wbuf.slot(iw)[:, kc * 128:(kc + 1) * 128],
                                   XN[:, kc * 4096 + ts.start: kc * 4096 + ts.stop],
                                   start=(kc == 0), stop=(kc == 7))
            te_mm = ctx.step(mm, "te")
            te_free = te_mm
            ctx.wait(ctx.se, "te", te_mm)
            ctx.step(ctx.se.activation(KT[:, oc * 4096 + ts.start: oc * 4096 + ts.stop],
                                       ps[:], AF.Identity, bias=kbc[:, oc:oc + 1]), "se")
    ctx.wait(ctx.ve, "se")
    for oc in range(2):
        chain(ctx, ctx.ve.reduce_max(km2[:, oc:oc + 1], KT[:, oc * 4096:(oc + 1) * 4096],
                                     axis=AX.X, apply_absolute_value=True), "ve", "ve")
    vv = ctx.step(ctx.ve.tensor_copy(km2b, km2), "ve")
    ctx.wait(ctx.ve, "ve", vv)
    ctx.wait(ctx.te, "ve", vv)
    for oc in range(2):
        tev = ctx.step(ctx.te.matmul(P["C"][0:1, 0:128], km2b[:, oc:oc + 1],
                                     S["CID"][:], start=True, stop=True), "te")
        ctx.wait(ctx.ve, "te", tev)
        vv = ctx.step(ctx.ve.tensor_copy(kr2[:, oc * 128:oc * 128 + 128],
                                         P["C"][0:1, 0:128]), "ve")
        ctx.wait(ctx.te, "ve", vv)

    ov = AR[:, vt_off:vt_off + 32 * 260].rearrange("p (t g u) -> p t g u", g=4, u=65)
    vev = ctx.step(ctx.ve.memset(ov[:, :, :, 64:65], 1.0), "ve")
    vsrc = A["l2_v"].rearrange("(c p) d -> p c d", p=128)
    vwbuf = PT2
    vw = ctx.dma("gp", vwbuf.rearrange("p (c d) -> p c d", d=256), vsrc, "wq")
    ctx.wait(ctx.te, "wq", vw)
    ctx.wait(ctx.te, "ve", vev)
    for tt2 in range(16):
        ps = P["C"] if tt2 % 2 == 0 else P["D"]
        if tt2 >= 2:
            ctx.wait(ctx.te, "se", ctx.cnt["se"] - 2)
        for half in range(2):
            tt = tt2 * 2 + half
            hs = slice(half * 256, (half + 1) * 256)
            for kc in range(8):
                mm = ctx.te.matmul(ps[:, hs],
                                   XN[:, kc * 4096 + tt * 128: kc * 4096 + (tt + 1) * 128],
                                   vwbuf[:, kc * 256:(kc + 1) * 256],
                                   start=(kc == 0), stop=False)
            mm = ctx.te.matmul(ps[:, hs], S["ONE"][:], bvr, start=False, stop=True)
        te_mm = ctx.step(mm, "te")
        te_free = te_mm
        ctx.wait(ctx.se, "te", te_mm)
        for half in range(2):
            tt = tt2 * 2 + half
            hs = slice(half * 256, (half + 1) * 256)
            dst = AR[:, vt_off + tt * 260: vt_off + (tt + 1) * 260] \
                .rearrange("p (g u) -> p g u", u=65)[:, :, 0:64]
            srcv = ps[:, hs].rearrange("p (g u) -> p g u", u=64)
            ctx.step(ctx.se.activation(dst, srcv, AF.Copy), "se")
    barrier(ctx, engs=("te", "ve", "se"), sems=("ve", "se", "te"))
    if DBG:
        ctx.wait(ctx.sy, "se")
        ctx.wait(ctx.sy, "ve")
        ctx.dma("sy", A["dbg_xn"], AR[:, 0:32768], "xq")
        ctx.dma("sy", A["dbg_kt"], KT, "xq")

    for qq in range(8):
        qs = slice(qq * 512, (qq + 1) * 512)
        for pr in range(8):
            g = pr // 2
            gb = 64 * (g % 2)
            iw, vw = wbuf.load(A["l2_qA"][pr], free_after=te_free)
            iws, vws = wbuf.load(A["l2_qAs"][pr], free_after=te_free)
            ctx.wait(ctx.te, "wq", vws)
            ctx.wait(ctx.te, "ve")
            for sw, (qt, bc) in enumerate(((QT, qbc), (QTs, qbcs))):
                for kc in range(8):
                    mm = ctx.te.matmul(P["D"][:],
                                       wbuf.slot(iw if sw == 0 else iws)[:, kc * 128:(kc + 1) * 128],
                                       XN[:, kc * 4096 + qs.start: kc * 4096 + qs.stop],
                                       start=(kc == 0), stop=(kc == 7))
                te_mm = ctx.step(mm, "te")
                te_free = te_mm
                ctx.wait(ctx.se, "te", te_mm)
                sev = ctx.step(ctx.se.activation(qt, P["D"][:], AF.Identity,
                                                 bias=bc[:, pr:pr + 1]), "se")
                if sw == 0:
                    ctx.wait(ctx.te, "se", sev)
            ctx.wait(ctx.ve, "se", sev)
            chain(ctx, ctx.ve.reduce_max(qmaxf, QT, axis=AX.X,
                                         apply_absolute_value=True), "ve", "ve")
            vv = ctx.step(ctx.ve.tensor_copy(qmb, qmaxf), "ve")
            ctx.wait(ctx.ve, "ve", vv)
            ctx.wait(ctx.te, "ve", vv)
            tev = ctx.step(ctx.te.matmul(P["C"][0:1, 128:256], qmb, S["CID"][:],
                                         start=True, stop=True), "te")
            ctx.wait(ctx.ve, "te", tev)
            chain(ctx, ctx.ve.tensor_copy(qrow_t, P["C"][0:1, 128:256]), "ve", "ve")
            for hh in range(2):
                chain(ctx, ctx.ve.tensor_tensor(
                    RBC[0:1, hh * 64:hh * 64 + 64],
                    qrow_t[:, hh * 64:hh * 64 + 64],
                    kr2[:, (g // 2) * 128 + gb: (g // 2) * 128 + gb + 64], AL.mult), "ve", "ve")
                chain(ctx, ctx.ve.reduce_sum(bbf[:, hh:hh + 1],
                                             RBC[0:1, hh * 64:hh * 64 + 64],
                                             axis=AX.X), "ve", "ve")
            vv = ctx.step(ctx.ve.tensor_scalar(br2, bbf, -1.0 / 16.0, None, AL.mult), "ve")
            ctx.wait(ctx.ve, "ve", vv)
            ctx.wait(ctx.te, "ve", vv)
            mm = ctx.te.matmul(P["C"][:, 0:2], S["ONE"][:], br2, start=True, stop=True)
            te_mm = ctx.step(mm, "te")
            ctx.wait(ctx.ve, "te", te_mm)
            vv_negb = ctx.step(ctx.ve.tensor_copy(negb, P["C"][:, 0:2]), "ve")
            if DBG and qq == 0:
                dv = S["DBG"]
                if pr == 0:
                    ctx.step(ctx.ve.tensor_copy(dv[:, 440:450],
                                                AR[:, 0:10]), "ve")
                    ctx.step(ctx.ve.tensor_copy(dv[:, 450:460],
                                                S["STAT"][:, 64:74]), "ve")
                    ctx.step(ctx.ve.tensor_copy(dv[:, 460:470],
                                                S["X"][:, 0:10]), "ve")
                ctx.step(ctx.ve.tensor_copy(dv[:, pr * 2:pr * 2 + 2], negb), "ve")
                ctx.step(ctx.ve.tensor_copy(dv[0:1, 16 + pr * 2:16 + pr * 2 + 2], bbf), "ve")
                ctx.step(ctx.ve.tensor_copy(dv[:, 32 + pr:33 + pr], qmaxf), "ve")
                if pr == 0:
                    ctx.step(ctx.ve.tensor_copy(dv[0:1, 40:40 + 256], kr2), "ve")
                    ctx.step(ctx.ve.tensor_copy(dv[:, 296:298], km2), "ve")
                    ctx.step(ctx.ve.tensor_copy(dv[0:1, 300:428], qrow_t), "ve")
            for hh in range(2):
                h = 2 * pr + hh
                qt = QT if (h % 2) == (g % 2) else QTs
                qrow = qt[gb:gb + 64, :]
                ysl = YT[:, h * 512:(h + 1) * 512]
                pv_done = [None, None]
                exp_done = None
                for kc2 in range(16):
                    sl = kc2 % 2
                    pslot = PT2[:, sl * 1024:(sl + 1) * 1024]
                    if exp_done is not None:
                        ctx.wait(ctx.te, "se", exp_done)
                    for j in range(2):
                        kc = kc2 * 2 + j
                        mm = ctx.te.matmul(
                            P["A"][:, j * 512:(j + 1) * 512],
                            KT[gb:gb + 64,
                               (g // 2) * 4096 + kc * 128: (g // 2) * 4096 + (kc + 1) * 128],
                            qrow, start=True, stop=True)
                    te_mm = ctx.step(mm, "te")
                    ctx.wait(ctx.se, "te", te_mm)
                    if pv_done[sl] is not None:
                        ctx.wait(ctx.se, "te", pv_done[sl])
                    if kc2 == 0:
                        ctx.wait(ctx.se, "ve", vv_negb)
                    exp_done = ctx.step(ctx.se.activation(pslot, P["A"][:], AF.Exp,
                                                          scale=0.125,
                                                          bias=negb[:, hh:hh + 1]), "se")
                    ctx.wait(ctx.te, "se", exp_done)
                    for j in range(2):
                        kc = kc2 * 2 + j
                        mm = ctx.te.matmul(
                            P["D"][0:65, :],
                            AR[:, vt_off + kc * 260 + g * 65: vt_off + kc * 260 + g * 65 + 65],
                            pslot[:, j * 512:(j + 1) * 512],
                            start=(kc == 0), stop=(kc == 31))
                    te_mm = ctx.step(mm, "te")
                    pv_done[sl] = te_mm
                ctx.wait(ctx.ve, "te", te_mm)
                chain(ctx, ctx.ve.reciprocal(recf, P["D"][64:65, :]), "ve", "ve")
                vv = ctx.step(ctx.ve.tensor_copy(rrow, recf), "ve")
                ctx.wait(ctx.ve, "ve", vv)
                ctx.wait(ctx.te, "ve", vv)
                mm = ctx.te.matmul(P["C"][0:64, :], oneb[64:65, :], rrow,
                                   start=True, stop=True)
                te_mm = ctx.step(mm, "te")
                ctx.wait(ctx.se, "te", te_mm)
                sev = ctx.step(ctx.se.activation(RBC, P["C"][0:64, :], AF.Copy), "se")
                ctx.wait(ctx.ve, "se", sev)
                vv = ctx.step(ctx.ve.tensor_tensor(ysl, P["D"][0:64, :], RBC,
                                                   AL.mult), "ve")
                ctx.wait(ctx.te, "ve", vv)
        ctx.wait(ctx.te, "ve")
        wo_r = A["l2_wo"].rearrange("(c p) d -> c p d", p=64)
        for t2 in range(2):
            psl = [(P["A"], 0), (P["A"], 512), (P["B"], 0), (P["B"], 512)]
            for yc in range(16):
                if te_free is not None:
                    ctx.wait(ctx.gp, "te", te_free)
                wsl = wbuf.slot(yc)[0:64, :]
                vw = ctx.dma("gp", wsl, wo_r[yc], "wq")
                ctx.wait(ctx.te, "wq", vw)
                for u in range(2):
                    t4 = t2 * 2 + u
                    for dh in range(2):
                        ps, off = psl[u * 2 + dh]
                        mm = ctx.te.matmul(
                            ps[:, off:off + 512],
                            YT[:, yc * 512 + t4 * 128: yc * 512 + (t4 + 1) * 128],
                            wsl[:, dh * 512:(dh + 1) * 512],
                            start=(yc == 0), stop=False)
                te_mm = ctx.step(mm, "te")
                te_free = te_mm
            for u in range(2):
                t4 = t2 * 2 + u
                tt = qq * 4 + t4
                for dh in range(2):
                    ps, off = psl[u * 2 + dh]
                    mm = ctx.te.matmul(ps[:, off:off + 512], S["ONE"][:],
                                       bor[:, dh * 512:(dh + 1) * 512],
                                       start=False, stop=True)
                    te_mm = ctx.step(mm, "te")
                    ctx.wait(ctx.ve, "te", te_mm)
                    xsl = X[:, tt * 1024 + dh * 512: tt * 1024 + (dh + 1) * 512]
                    ctx.step(ctx.ve.tensor_tensor(xsl, xsl, ps[:, off:off + 512],
                                                  AL.add), "ve")
            ctx.wait(ctx.te, "ve")
    barrier(ctx, engs=("te", "se", "gp", "sy"), sems=("ve",))


def emit_attn_l3(ctx, A, S, P):
    X, AR, AFX = S["X"], S["AR"], S["AF"]
    XN = AR[:, 0:32 * 1024]
    csb_off = 32 * 1024
    qsb_off = csb_off + 8192
    t1_off = qsb_off + 1024
    ro_off = t1_off + 1024
    ktm_off = ro_off + 1024
    ksq_off = ktm_off + 1024
    gsb_off = ksq_off + 1024
    vs_off = gsb_off + 1024
    wb_off = vs_off + 2080
    kb_off = csb_off
    vb_off = kb_off + 4096
    qb_off = vb_off + 2080
    ub_off = qb_off + 512
    yt_off = ub_off + 1024
    pt_off = yt_off + 8192
    r2_off = pt_off + 2048

    wbuf = WStream(ctx, AR[:, wb_off:wb_off + 2048], 1024)
    CSB = AR[:, csb_off:csb_off + 8192]
    PSW = S["PSW"]

    recf = AFX[64:65, 192:192 + 512]
    trh = AFX[0:1, 140:156]
    mkh = AFX[0:1, 156:172]
    sc1 = AFX[0:1, 172:174]
    km_col = AFX[:, 176:177]

    QSB = AR[:, qsb_off:qsb_off + 1024]
    T1 = AR[:, t1_off:t1_off + 1024]
    RO = AR[:, ro_off:ro_off + 1024]
    KTM = AR[:, ktm_off:ktm_off + 1024]
    KSQ = AR[:, ksq_off:ksq_off + 1024]
    GSB = AR[0:64, gsb_off:gsb_off + 1024]
    onec = AR[:, r2_off + 3072:r2_off + 3073]
    vv = ctx.step(ctx.ve.memset(onec, 1.0), "ve")
    vv0 = ctx.step(ctx.ve.memset(trh, 0.0), "ve")
    vv0 = ctx.step(ctx.ve.memset(mkh, 0.0), "ve")
    ctx.wait(ctx.ve, "ve", vv0)
    KSQS = AFX[0:1, 180:181]
    qsb_used = {}
    ro_used = {}
    te_free = None
    for sel, wname in ((0, "l3_qA"), (1, "l3_kA")):
        for oc in range(8):
            iw, vw = wbuf.load(A[wname][oc], free_after=te_free)
            ctx.wait(ctx.gp, "ve")
            cw = ctx.dma("gp", CSB[:, 0:4096], A["l3_cos"][oc * 128:(oc + 1) * 128, :], "wq")
            cw = ctx.dma("gp", CSB[:, 4096:8192], A["l3_sin2"][oc * 128:(oc + 1) * 128, :], "wq")
            ctx.wait(ctx.te, "wq", cw)
            ctx.wait(ctx.ve, "wq", cw)
            for tt in range(8):
                b = tt % 2
                ts = slice(tt * 512, (tt + 1) * 512)
                psm = P["A"][:, b * 512:(b + 1) * 512]
                psw_out = P["B"][:, b * 512:(b + 1) * 512]
                qsb = QSB[:, b * 512:(b + 1) * 512]
                t1 = T1[:, b * 512:(b + 1) * 512]
                ro = RO[:, b * 512:(b + 1) * 512]
                if qsb_used.get(b) is not None:
                    ctx.wait(ctx.se, "ve", qsb_used[b])
                for kc in range(8):
                    mm = ctx.te.matmul(psm, wbuf.slot(iw)[:, kc * 128:(kc + 1) * 128],
                                       XN[:, kc * 4096 + ts.start:kc * 4096 + ts.stop],
                                       start=(kc == 0), stop=(kc == 7))
                te_mm = ctx.step(mm, "te")
                te_free = te_mm
                ctx.wait(ctx.se, "te", te_mm)
                sev = ctx.step(ctx.se.activation(qsb, psm, AF.Copy), "se")
                ctx.wait(ctx.te, "se", sev)
                mm = ctx.te.matmul(psw_out, PSW[:], qsb, start=True, stop=True)
                te_sw = ctx.step(mm, "te")
                ctx.wait(ctx.ve, "se", sev)
                if ro_used.get(b) is not None:
                    ctx.wait(ctx.ve, "xq", ro_used[b])
                qsb_used[b] = chain(ctx, ctx.ve.tensor_tensor(t1, qsb, CSB[:, ts], AL.mult), "ve", "ve")
                ctx.wait(ctx.ve, "te", te_sw)
                chain(ctx, ctx.ve.tensor_tensor(ro, psw_out,
                                                CSB[:, 4096 + ts.start:4096 + ts.stop],
                                                AL.mult), "ve", "ve")
                vv = ctx.step(ctx.ve.tensor_tensor(ro, ro, t1, AL.add), "ve")
                ctx.wait(ctx.ve, "ve", vv)
                ctx.wait(ctx.te, "ve", vv)
                ctx.wait(ctx.sy, "ve", vv)
                dst = A["l3qk"][sel, :, oc * 4096 + ts.start: oc * 4096 + ts.stop]
                ro_used[b] = ctx.dma("sy", dst, ro, "xq")
                if sel == 1:
                    ktm = KTM[:, b * 512:(b + 1) * 512]
                    for blk in range(4):
                        tr = ctx.te.transpose(
                            P["T"][:, b * 1024 + blk * 128: b * 1024 + (blk + 1) * 128],
                            ro[:, blk * 128:(blk + 1) * 128], S["CID"][:])
                    tr_v = ctx.step(tr, "te")
                    ctx.wait(ctx.ve, "te", tr_v)
                    for blk in range(4):
                        cp = ctx.ve.tensor_copy(
                            ktm[:, blk * 128:(blk + 1) * 128],
                            P["T"][:, b * 1024 + blk * 128: b * 1024 + (blk + 1) * 128])
                    ktm_v = ctx.step(cp, "ve")
                    ctx.wait(ctx.te, "ve", ktm_v)
                    for blk in range(4):
                        mm = ctx.te.matmul(P["C"][0:64, 0:64],
                                           ktm[:, blk * 128:blk * 128 + 64],
                                           ktm[:, blk * 128:blk * 128 + 64],
                                           start=(tt == 0 and blk == 0), stop=(tt == 7 and blk == 3))
                        mm = ctx.te.matmul(P["D"][0:64, 0:64],
                                           ktm[:, blk * 128 + 64:blk * 128 + 128],
                                           ktm[:, blk * 128 + 64:blk * 128 + 128],
                                           start=(tt == 0 and blk == 0), stop=(tt == 7 and blk == 3))
                    gm_v = ctx.step(mm, "te")
                    ksq = KSQ[:, b * 512:(b + 1) * 512]
                    sev = ctx.step(ctx.se.activation(ksq, ro, AF.Square), "se")
                    ctx.wait(ctx.te, "se", sev)
                    for hh in range(2):
                        mm = ctx.te.matmul(psm[0:1, 0:512],
                                           onec[64 * hh:64 * hh + 64, :],
                                           ksq[64 * hh:64 * hh + 64, :],
                                           start=True, stop=True)
                        knv = ctx.step(mm, "te")
                        ctx.wait(ctx.ve, "te", knv)
                        h = oc * 2 + hh
                        chain(ctx, ctx.ve.reduce_max(sc1[:, 1:2], psm[0:1, 0:512],
                                                     axis=AX.X), "ve", "ve")
                        vv = ctx.step(ctx.ve.tensor_tensor(mkh[:, h:h + 1], mkh[:, h:h + 1],
                                                           sc1[:, 1:2], AL.max), "ve")
                        ctx.wait(ctx.ve, "ve", vv)
                        ctx.wait(ctx.te, "ve", vv)
            if sel == 1:
                ctx.wait(ctx.se, "te", gm_v)
                sev = ctx.step(ctx.se.activation(GSB[:, (2 * oc) * 64:(2 * oc) * 64 + 64],
                                                 P["C"][0:64, 0:64], AF.Copy), "se")
                sev = ctx.step(ctx.se.activation(GSB[:, (2 * oc + 1) * 64:(2 * oc + 1) * 64 + 64],
                                                 P["D"][0:64, 0:64], AF.Copy), "se")
                ctx.wait(ctx.te, "se", sev)
    VW = CSB
    ctx.wait(ctx.gp, "ve")
    vw = ctx.dma("gp", VW.rearrange("p (c d) -> p c d", d=1024),
                 A["l3_v"].rearrange("(c p) d -> p c d", p=128), "wq")
    VS = AR[:, vs_off:vs_off + 2080]
    ov = VS.rearrange("p (b g u) -> p b g u", b=2, u=65)
    vv_one = ctx.step(ctx.ve.memset(ov[:, :, :, 64:65], 1.0), "ve")
    ctx.wait(ctx.te, "wq", vw)
    vdma = {}
    for tt in range(32):
        b = tt % 2
        psm = P["A"] if b == 0 else P["B"]
        if tt >= 2:
            ctx.wait(ctx.te, "se", ctx.cnt["se"] - 1)
        for dh in range(2):
            hs = slice(dh * 512, (dh + 1) * 512)
            for kc in range(8):
                mm = ctx.te.matmul(psm[:, hs],
                                   XN[:, kc * 4096 + tt * 128: kc * 4096 + (tt + 1) * 128],
                                   VW[:, kc * 1024 + hs.start: kc * 1024 + hs.stop],
                                   start=(kc == 0), stop=(kc == 7))
        te_mm = ctx.step(mm, "te")
        ctx.wait(ctx.se, "te", te_mm)
        if tt >= 2:
            ctx.wait(ctx.se, "xq", vdma[tt - 2])
        if tt == 0:
            ctx.wait(ctx.se, "ve", vv_one)
        dstv = VS[:, b * 1040:(b + 1) * 1040].rearrange("p (g u) -> p g u", u=65)[:, :, 0:64]
        sev = ctx.step(ctx.se.activation(dstv, psm[:].rearrange("p (g u) -> p g u", u=64),
                                         AF.Copy), "se")
        ctx.wait(ctx.sy, "se", sev)
        for h in range(16):
            last = ctx.dma("sy", A["l3v"][h, tt, :, :],
                           VS[:, b * 1040 + h * 65: b * 1040 + (h + 1) * 65], "xq")
        vdma[tt] = last
    barrier(ctx, engs=("te", "ve", "se", "gp", "sy"), sems=("ve", "se", "te", "xq", "wq"))

    KB = AR[:, kb_off:kb_off + 4096]
    VB = AR[:, vb_off:vb_off + 2080]
    QB = AR[:, qb_off:qb_off + 512]
    UB = AR[0:64, ub_off:ub_off + 512]
    UB2 = AR[0:64, ub_off + 512:ub_off + 1024]
    YT = AR[0:64, yt_off:yt_off + 8192]
    PT2 = AR[:, pt_off:pt_off + 2048]
    RBC = AR[0:64, r2_off:r2_off + 512]
    rrow = AR[64:65, r2_off + 512:r2_off + 1024]
    bur = AR[0:1, r2_off + 1024:r2_off + 2048]
    kmb = AR[:, r2_off + 2048:r2_off + 2049]
    oneb = AR[:, r2_off + 2052:r2_off + 2052 + 64]
    R_MU = AR[0:1, r2_off + 3200:r2_off + 3712]
    R_ES = AR[0:1, r2_off + 3712:r2_off + 4224]
    R_M2 = AR[0:1, r2_off + 4224:r2_off + 4736]
    R_SG = AR[0:1, r2_off + 4736:r2_off + 5248]
    R_U = AR[0:1, r2_off + 5248:r2_off + 5760]
    R_C = AR[0:1, r2_off + 5760:r2_off + 6272]
    vv = ctx.step(ctx.ve.memset(oneb, 1.0), "ve")
    vv = ctx.step(ctx.ve.memset(AR[64:65, kb_off:kb_off + 4096], 1.0), "ve")
    ctx.dma("gp", bur, A["l3_bur"], "wq")
    barrier(ctx, engs=("ve",), sems=("ve",))

    for qq in range(8):
        for h in range(16):
            ctx.wait(ctx.sy, "te")
            ctx.wait(ctx.sy, "ve")
            ctx.wait(ctx.sy, "se")
            kv1 = ctx.dma("sy", KB[0:64, :], A["l3qk"][1, 64 * (h % 2):64 * (h % 2) + 64,
                                                       (h // 2) * 4096:(h // 2 + 1) * 4096], "xq")
            kv2 = ctx.dma("sy", VB.rearrange("p (c u) -> p c u", u=65),
                          A["l3v"][h].rearrange("c p u -> p c u"), "xq")
            kv3 = ctx.dma("sy", QB[0:64, :],
                          A["l3qk"][0, 64 * (h % 2):64 * (h % 2) + 64,
                                    (h // 2) * 4096 + qq * 512: (h // 2) * 4096 + (qq + 1) * 512], "xq")
            ctx.wait(ctx.ve, "xq", kv3)
            chain(ctx, ctx.ve.reduce_sum(km_col[0:64, :], KB[0:64, :], axis=AX.X), "ve", "ve")
            chain(ctx, ctx.ve.tensor_scalar(kmb[0:64, :], km_col[0:64, :],
                                            1.0 / 4096.0, None, AL.mult), "ve", "ve")
            ctx.wait(ctx.te, "ve")
            mm = ctx.te.matmul(P["C"][0:1, 0:512], kmb[0:64, :], QB[0:64, :],
                               start=True, stop=True)
            mu_v = ctx.step(mm, "te")
            mm = ctx.te.matmul(P["D"][0:64, :], GSB[:, h * 64:(h + 1) * 64],
                               QB[0:64, :], start=True, stop=True)
            gq_v = ctx.step(mm, "te")
            ctx.wait(ctx.ve, "te", gq_v)
            chain(ctx, ctx.ve.tensor_tensor(UB, QB[0:64, :], P["D"][0:64, :], AL.mult),
                  "ve", "ve")
            chain(ctx, ctx.ve.tensor_tensor(UB2, QB[0:64, :], QB[0:64, :], AL.mult),
                  "ve", "ve")
            ctx.wait(ctx.te, "ve")
            mm = ctx.te.matmul(P["A"][0:1, 0:512], onec[0:64, :], UB,
                               start=True, stop=True)
            ss_v = ctx.step(mm, "te")
            mm = ctx.te.matmul(P["A"][0:1, 512:1024], onec[0:64, :], UB2,
                               start=True, stop=True)
            qn_v = ctx.step(mm, "te")
            ctx.wait(ctx.ve, "te", qn_v)
            chain(ctx, ctx.ve.tensor_scalar(R_MU, P["C"][0:1, 0:512], 0.125, None,
                                            AL.mult), "ve", "ve")
            chain(ctx, ctx.ve.tensor_scalar(R_ES, P["A"][0:1, 0:512],
                                            1.0 / (4096.0 * 64.0), None,
                                            AL.mult), "ve", "ve")
            chain(ctx, ctx.ve.tensor_tensor(R_M2, R_MU, R_MU, AL.mult), "ve", "ve")
            chain(ctx, ctx.ve.tensor_tensor(R_SG, R_ES, R_M2, AL.subtract), "ve", "ve")
            chain(ctx, ctx.ve.tensor_scalar(R_SG, R_SG, 1e-3, None, AL.max), "ve", "ve")
            ctx.wait(ctx.se, "ve")
            chain(ctx, ctx.se.activation(R_SG, R_SG, AF.Ln), "se", "se")
            chain(ctx, ctx.se.activation(R_SG, R_SG, AF.Exp, scale=0.5), "se", "se")
            ctx.wait(ctx.ve, "se")
            chain(ctx, ctx.ve.tensor_scalar(R_U, P["A"][0:1, 512:1024],
                                            mkh[:, h:h + 1], None, AL.mult), "ve", "ve")
            chain(ctx, ctx.ve.tensor_scalar(R_U, R_U, 1e-12, None, AL.max), "ve", "ve")
            ctx.wait(ctx.se, "ve")
            chain(ctx, ctx.se.activation(R_U, R_U, AF.Ln), "se", "se")
            chain(ctx, ctx.se.activation(R_U, R_U, AF.Exp, scale=0.5), "se", "se")
            ctx.wait(ctx.ve, "se")
            chain(ctx, ctx.ve.tensor_scalar(R_U, R_U, 0.125, -70.0, AL.mult, AL.add),
                  "ve", "ve")
            chain(ctx, ctx.ve.scalar_tensor_tensor(R_C, R_SG, 0.875, R_MU,
                                                   AL.mult, AL.add), "ve", "ve")
            chain(ctx, ctx.ve.tensor_tensor(R_C, R_C, R_U, AL.max), "ve", "ve")
            vv = ctx.step(ctx.ve.tensor_scalar(QB[64:65, :], R_C, -8.0, None, AL.mult), "ve")
            ctx.wait(ctx.ve, "ve", vv)
            ctx.wait(ctx.te, "ve", vv)
            ctx.wait(ctx.te, "xq", kv2)
            ysl = YT[:, h * 512:(h + 1) * 512]
            pv_done = [None, None]
            exp_done = None
            for kc2 in range(16):
                sl = kc2 % 2
                pslot = PT2[:, sl * 1024:(sl + 1) * 1024]
                if exp_done is not None:
                    ctx.wait(ctx.te, "se", exp_done)
                for j in range(2):
                    kc = kc2 * 2 + j
                    mm = ctx.te.matmul(
                        P["B"][:, j * 512:(j + 1) * 512],
                        KB[0:65, kc * 128:(kc + 1) * 128],
                        QB[0:65, :], start=True, stop=True)
                te_mm = ctx.step(mm, "te")
                ctx.wait(ctx.se, "te", te_mm)
                if pv_done[sl] is not None:
                    ctx.wait(ctx.se, "te", pv_done[sl])
                exp_done = ctx.step(ctx.se.activation(pslot, P["B"][:], AF.Exp,
                                                      scale=0.125), "se")
                ctx.wait(ctx.te, "se", exp_done)
                for j in range(2):
                    kc = kc2 * 2 + j
                    mm = ctx.te.matmul(
                        P["D"][0:65, :],
                        VB[:, kc * 65:(kc + 1) * 65],
                        pslot[:, j * 512:(j + 1) * 512],
                        start=(kc == 0), stop=(kc == 31))
                te_mm = ctx.step(mm, "te")
                pv_done[sl] = te_mm
            ctx.wait(ctx.ve, "te", te_mm)
            chain(ctx, ctx.ve.tensor_scalar(P["D"][64:65, :], P["D"][64:65, :], 1e-30,
                                            None, AL.add), "ve", "ve")
            chain(ctx, ctx.ve.reciprocal(recf, P["D"][64:65, :]), "ve", "ve")
            vv = ctx.step(ctx.ve.tensor_copy(rrow, recf), "ve")
            ctx.wait(ctx.ve, "ve", vv)
            ctx.wait(ctx.te, "ve", vv)
            mm = ctx.te.matmul(P["C"][0:64, :], oneb[64:65, :], rrow,
                               start=True, stop=True)
            te_mm = ctx.step(mm, "te")
            ctx.wait(ctx.se, "te", te_mm)
            sev = ctx.step(ctx.se.activation(RBC, P["C"][0:64, :], AF.Copy), "se")
            ctx.wait(ctx.ve, "se", sev)
            vv = ctx.step(ctx.ve.tensor_tensor(ysl, P["D"][0:64, :], RBC, AL.mult), "ve")
            ctx.wait(ctx.te, "ve", vv)
        ctx.wait(ctx.te, "ve")
        wu_r = A["l3_wu"].rearrange("(c p) d -> c p d", p=64)
        te_free = ctx.cnt["te"]
        for t2 in range(2):
            psl = [(P["A"], 0), (P["A"], 512), (P["B"], 0), (P["B"], 512)]
            for yc in range(16):
                ctx.wait(ctx.gp, "te", te_free)
                wsl = wbuf.slot(yc)[0:64, :]
                vw = ctx.dma("gp", wsl, wu_r[yc], "wq")
                ctx.wait(ctx.te, "wq", vw)
                for u in range(2):
                    t4 = t2 * 2 + u
                    for dh in range(2):
                        ps, off = psl[u * 2 + dh]
                        mm = ctx.te.matmul(
                            ps[:, off:off + 512],
                            YT[:, yc * 512 + t4 * 128: yc * 512 + (t4 + 1) * 128],
                            wsl[:, dh * 512:(dh + 1) * 512],
                            start=(yc == 0), stop=False)
                te_mm = ctx.step(mm, "te")
                te_free = te_mm
            for u in range(2):
                t4 = t2 * 2 + u
                tt = qq * 4 + t4
                for dh in range(2):
                    ps, off = psl[u * 2 + dh]
                    mm = ctx.te.matmul(ps[:, off:off + 512], S["ONE"][:],
                                       bur[:, dh * 512:(dh + 1) * 512],
                                       start=False, stop=True)
                    te_mm = ctx.step(mm, "te")
                    ctx.wait(ctx.ve, "te", te_mm)
                    xsl = X[:, tt * 1024 + dh * 512: tt * 1024 + (dh + 1) * 512]
                    ctx.step(ctx.ve.tensor_tensor(xsl, xsl, ps[:, off:off + 512],
                                                  AL.add), "ve")
            ctx.wait(ctx.te, "ve")
    barrier(ctx, engs=("te", "se", "gp", "sy"), sems=("ve",))


SELU_LA = SELU_L * SELU_A


def emit_attn_l1(ctx, A, S, P):
    X, AR, AFX = S["X"], S["AR"], S["AF"]
    XN = AR[:, 0:32 * 1024]
    kb_off = 32 * 1024
    qb_off = kb_off + 4096
    vb_off = qb_off + 4096
    es_off = vb_off + 2048
    rs_off = es_off + 2048
    tp_off = rs_off + 2048
    wb_off = tp_off + 2048
    r2_off = wb_off + 1024

    wbuf = WStream(ctx, AR[:, wb_off:wb_off + 1024], 512)
    KB = AR[:, kb_off:kb_off + 4096]
    QB = AR[:, qb_off:qb_off + 4096]
    VB = AR[:, vb_off:vb_off + 2048]
    EB = AR[:, es_off:es_off + 2048]
    RB = AR[:, rs_off:rs_off + 2048]
    TB = AR[:, tp_off:tp_off + 2048]
    kmb = AR[:, r2_off:r2_off + 1]
    corr = AFX[:, 176:177]
    vsum = AFX[:, 177:178]
    qbc = AFX[:, 178:194]
    kbc = AFX[:, 194:210]
    km_col = AFX[:, 210:211]
    ctx.dma("sy", qbc, A["l1_bqc"], "xq")
    ctx.dma("sy", kbc, A["l1_bkc"], "xq")
    vv = ctx.step(ctx.ve.memset(AR[64:65, kb_off:kb_off + 4096], 1.0), "ve")
    ctx.wait(ctx.se, "xq")
    barrier(ctx, engs=("ve", "te"), sems=("ve",))

    te_free = None
    for h in range(16):
        hb = 64 * (h % 2)
        for sel, wname, dst, bc in ((1, "l1_kA", KB, kbc), (0, "l1_qA", QB, qbc)):
            iw, vw = wbuf.load(A[wname][h], free_after=te_free)
            ctx.wait(ctx.te, "wq", vw)
            for tt in range(8):
                ts = slice(tt * 512, (tt + 1) * 512)
                psm = P["A"][:, (tt % 2) * 512:(tt % 2) * 512 + 512]
                if tt >= 2:
                    ctx.wait(ctx.te, "se", ctx.cnt["se"] - 1)
                for kc in range(8):
                    mm = ctx.te.matmul(psm[0:64, :], wbuf.slot(iw)[:, kc * 64:(kc + 1) * 64],
                                       XN[:, kc * 4096 + ts.start:kc * 4096 + ts.stop],
                                       start=(kc == 0), stop=(kc == 7))
                te_mm = ctx.step(mm, "te")
                te_free = te_mm
                ctx.wait(ctx.se, "te", te_mm)
                sev = ctx.step(ctx.se.activation(dst[0:64, ts], psm[0:64, :], AF.Identity,
                                                 bias=bc[0:64, h:h + 1]), "se")
        ctx.wait(ctx.ve, "se")
        chain(ctx, ctx.ve.reduce_sum(km_col[0:64, :], KB[0:64, :], axis=AX.X), "ve", "ve")
        vv = ctx.step(ctx.ve.tensor_copy(kmb[0:64, :], km_col[0:64, :]), "ve")
        ctx.wait(ctx.ve, "ve", vv)
        chain(ctx, ctx.ve.reduce_sum(vsum[0:64, :],
                                     XN[hb:hb + 64, (h // 2) * 4096:(h // 2 + 1) * 4096],
                                     axis=AX.X), "ve", "ve")
        vv = ctx.step(ctx.ve.tensor_scalar(corr[0:64, :], vsum[0:64, :], -SELU_LA, None,
                                           AL.mult), "ve")
        ctx.wait(ctx.ve, "ve", vv)
        ctx.wait(ctx.te, "ve", vv)
        for tt in range(8):
            ts = slice(tt * 512, (tt + 1) * 512)
            mm = ctx.te.matmul(P["C"][0:1, 0:512], kmb[0:64, :], QB[0:64, ts],
                               start=True, stop=True)
            te_mm = ctx.step(mm, "te")
            ctx.wait(ctx.ve, "te", te_mm)
            vv = ctx.step(ctx.ve.tensor_scalar(QB[64:65, ts], P["C"][0:1, 0:512],
                                               -1.0 / 4096.0, None, AL.mult), "ve")
            ctx.wait(ctx.ve, "ve", vv)
            ctx.wait(ctx.te, "ve", vv)
        for kc in range(32):
            bb = kc % 2
            if kc >= 2:
                ctx.wait(ctx.te, "ve", vb_done[kc - 2])
            tr = ctx.te.transpose(P["T"][0:128, bb * 1024:bb * 1024 + 64],
                                  XN[hb:hb + 64,
                                     (h // 2) * 4096 + kc * 128:(h // 2) * 4096 + (kc + 1) * 128],
                                  S["CID"][hb:hb + 64, hb:hb + 64])
            tr_v = ctx.step(tr, "te")
            ctx.wait(ctx.ve, "te", tr_v)
            if kc == 0:
                vb_done = {}
            vb_done[kc] = ctx.step(ctx.ve.tensor_copy(VB[:, kc * 64:(kc + 1) * 64],
                                                      P["T"][0:128, bb * 1024:bb * 1024 + 64]),
                                   "ve")
        ctx.wait(ctx.te, "ve")
        for qq in range(8):
            qs = slice(qq * 512, (qq + 1) * 512)
            ctx.wait(ctx.ve, "te")
            ctx.wait(ctx.se, "te")
            ctx.wait(ctx.gp, "te")
            pv_done = [None, None]
            e_done = None
            p_done = None
            for kc2 in range(16):
                sl = kc2 % 2
                ebuf = EB[:, sl * 1024:(sl + 1) * 1024]
                rbuf = RB[:, sl * 1024:(sl + 1) * 1024]
                tbuf = TB[:, sl * 1024:(sl + 1) * 1024]
                if e_done is not None:
                    ctx.wait(ctx.te, "se", e_done)
                    ctx.wait(ctx.te, "ve", r_done)
                for j in range(2):
                    kc = kc2 * 2 + j
                    mm = ctx.te.matmul(
                        P["B"][:, j * 512:(j + 1) * 512],
                        KB[0:65, kc * 128:(kc + 1) * 128],
                        QB[0:65, qs], start=True, stop=True)
                te_mm = ctx.step(mm, "te")
                ctx.wait(ctx.se, "te", te_mm)
                if pv_done[sl] is not None:
                    ctx.wait(ctx.se, "te", pv_done[sl])
                e_done = ctx.step(ctx.se.activation(ebuf, P["B"][:], AF.Exp,
                                                    scale=0.125), "se")
                ctx.wait(ctx.ve, "te", te_mm)
                ctx.wait(ctx.ve, "se", e_done)
                if pv_done[sl] is not None:
                    ctx.wait(ctx.ve, "te", pv_done[sl])
                r_done = chain(ctx, ctx.ve.tensor_scalar(rbuf, P["B"][:], 0.125 * SELU_L,
                                                         0.0, AL.mult, AL.max), "ve", "ve")
                ctx.wait(ctx.gp, "se", e_done)
                ctx.wait(ctx.gp, "ve", r_done)
                gv = ctx.step(ctx.gp.tensor_scalar(tbuf, ebuf, SELU_LA, SELU_LA,
                                                   AL.mult, AL.min), "gq")
                ctx.wait(ctx.ve, "gq", gv)
                p_done = chain(ctx, ctx.ve.tensor_tensor(tbuf, tbuf, rbuf, AL.add),
                               "ve", "ve")
                ctx.wait(ctx.te, "ve", p_done)
                for j in range(2):
                    kc = kc2 * 2 + j
                    mm = ctx.te.matmul(
                        P["D"][0:64, :],
                        VB[:, kc * 64:(kc + 1) * 64],
                        tbuf[:, j * 512:(j + 1) * 512],
                        start=(kc == 0), stop=(kc == 31))
                te_mm = ctx.step(mm, "te")
                pv_done[sl] = te_mm
            ctx.wait(ctx.ve, "te", te_mm)
            vv = ctx.step(ctx.ve.tensor_scalar(RB[0:64, 0:512], P["D"][0:64, :],
                                               corr[0:64, :], 0.015625, AL.add, AL.mult),
                          "ve")
            ctx.wait(ctx.ve, "ve", vv)
            ctx.wait(ctx.te, "ve", vv)
            for tb in range(4):
                tt = qq * 4 + tb
                tr = ctx.te.transpose(P["T"][0:128, tb * 128:tb * 128 + 64],
                                      RB[0:64, tb * 128:(tb + 1) * 128],
                                      S["CID"][0:64, 0:64])
                tr_v = ctx.step(tr, "te")
                ctx.wait(ctx.ve, "te", tr_v)
                xsl = X[:, tt * 1024 + h * 64: tt * 1024 + (h + 1) * 64]
                vv = ctx.step(ctx.ve.tensor_tensor(xsl, xsl,
                                                   P["T"][0:128, tb * 128:tb * 128 + 64],
                                                   AL.add), "ve")
            ctx.wait(ctx.te, "ve")
    barrier(ctx, engs=("te", "se", "gp", "sy"), sems=("ve", "gq"))


def emit_ffn_moe(ctx, A, S, P):
    X, AR, AFX = S["X"], S["AR"], S["AF"]
    XN = AR[:, 0:32 * 1024]
    w1_off = 32 * 1024
    w2_off = w1_off + 16384
    h_off = w2_off + 16384
    W1B = AR[:, w1_off:w1_off + 16384]
    W2B = AR[:, w2_off:w2_off + 16384]
    HB = AR[:, h_off:h_off + 4096]
    b2r = AR[0:1, h_off + 4096:h_off + 4096 + 1024]
    GTS = AFX[:, 256:512]
    LG = AFX[:, 212:220]
    m8 = AFX[:, 220:221]
    z8 = AFX[:, 221:222]
    b1c = AFX[:, 224:240]

    wr_s = AR[:, w1_off:w1_off + 64].rearrange("p (c e) -> p c e", e=8)
    brr = AR[0:1, w1_off + 64:w1_off + 72]
    jnk8 = HB[:, 0:8]
    vw = ctx.dma("gp", wr_s, A["l1_wr"].rearrange("(c p) e -> p c e", p=128), "wq")
    ctx.dma("gp", brr, A["l1_brr"], "wq")
    ctx.wait(ctx.te, "wq")
    barrier(ctx, engs=("te",), sems=("ve", "se"))
    for tt in range(32):
        psm = P["C"][:, 0:8] if tt % 2 == 0 else P["D"][:, 0:8]
        if tt >= 2:
            ctx.wait(ctx.te, "ve")
        for kc in range(8):
            mm = ctx.te.matmul(psm, XN[:, kc * 4096 + tt * 128:kc * 4096 + (tt + 1) * 128],
                               wr_s[:, kc, :], start=(kc == 0), stop=False)
        mm = ctx.te.matmul(psm, S["ONE"][:], brr, start=False, stop=True)
        te_mm = ctx.step(mm, "te")
        ctx.wait(ctx.ve, "te", te_mm)
        chain(ctx, ctx.ve.tensor_copy(LG, psm), "ve", "ve")
        chain(ctx, ctx.ve.reduce_max(m8, LG, axis=AX.X, negate=True), "ve", "ve")
        ctx.wait(ctx.se, "ve")
        sev = ctx.step(ctx.se.activation(jnk8, LG, AF.Exp,
                                         bias=m8, accum_out=z8), "se")
        ctx.wait(ctx.ve, "se", sev)
        chain(ctx, ctx.ve.reciprocal(z8, z8), "ve", "ve")
        chain(ctx, ctx.ve.tensor_scalar(m8, m8, -1.0, None, AL.mult), "ve", "ve")
        vv = ctx.step(ctx.ve.tensor_scalar(GTS[:, tt * 8:(tt + 1) * 8], LG, m8, z8,
                                           AL.is_ge, AL.mult), "ve")
        ctx.wait(ctx.ve, "ve", vv)
        ctx.wait(ctx.te, "ve", vv)

    w1r = A["l1_w1A"]
    w2r = A["l1_w2"].rearrange("e (c p) d -> e c p d", p=128)
    te_free = None
    ctx.wait(ctx.gp, "te")
    for e in range(8):
        if te_free is not None:
            ctx.wait(ctx.gp, "te", te_free)
        for hf in range(16):
            ctx.dma("gp", W1B[:, hf * 1024:(hf + 1) * 1024], w1r[e, hf], "wq")
        for kc in range(16):
            ctx.dma("gp", W2B[:, kc * 1024:(kc + 1) * 1024], w2r[e, kc], "wq")
        w_v = ctx.dma("gp", b2r, A["l1_b2r"][e], "wq")
        bv = ctx.dma("sy", b1c, A["l1_b1c"][e], "xq")
        ctx.wait(ctx.te, "wq", w_v)
        ctx.wait(ctx.se, "xq", bv)
        for t2 in range(16):
            for hf in range(16):
                psm = P["A"][:, (hf % 2) * 512:(hf % 2) * 512 + 256]
                if hf >= 2 or t2 > 0:
                    ctx.wait(ctx.te, "se", ctx.cnt["se"] - 1)
                for kc in range(8):
                    mm = ctx.te.matmul(psm,
                                       W1B[:, hf * 1024 + kc * 128: hf * 1024 + (kc + 1) * 128],
                                       XN[:, kc * 4096 + t2 * 256:kc * 4096 + (t2 + 1) * 256],
                                       start=(kc == 0), stop=(kc == 7))
                te_mm = ctx.step(mm, "te")
                te_free = te_mm
                ctx.wait(ctx.se, "te", te_mm)
                sev = ctx.step(ctx.se.activation(HB[:, hf * 256:(hf + 1) * 256], psm,
                                                 AF.Relu, bias=b1c[:, hf:hf + 1]), "se")
            ctx.wait(ctx.te, "se")
            for u in range(2):
                t128 = t2 * 2 + u
                for dh in range(2):
                    psm = P["B"][:, dh * 512:(dh + 1) * 512]
                    for kc in range(16):
                        mm = ctx.te.matmul(psm,
                                           HB[:, kc * 256 + u * 128: kc * 256 + u * 128 + 128],
                                           W2B[:, kc * 1024 + dh * 512: kc * 1024 + (dh + 1) * 512],
                                           start=(kc == 0), stop=False)
                    mm = ctx.te.matmul(psm, S["ONE"][:], b2r[:, dh * 512:(dh + 1) * 512],
                                       start=False, stop=True)
                    te_mm = ctx.step(mm, "te")
                    ctx.wait(ctx.ve, "te", te_mm)
                    xsl = X[:, t128 * 1024 + dh * 512: t128 * 1024 + (dh + 1) * 512]
                    vv = ctx.step(ctx.ve.scalar_tensor_tensor(
                        xsl, psm, GTS[:, t128 * 8 + e: t128 * 8 + e + 1], xsl,
                        AL.mult, AL.add), "ve")
                    ctx.wait(ctx.ve, "ve", vv)
                ctx.wait(ctx.te, "ve")
        te_free = ctx.cnt["te"]
    barrier(ctx, engs=("te", "se", "gp", "sy"), sems=("ve",))


def emit_attn_l0(ctx, A, S, P):
    X, AR, AFX = S["X"], S["AR"], S["AF"]
    XN = AR[:, 0:32 * 1024]
    lt_off = 32 * 1024
    kb_off = lt_off + 16384
    vb_off = kb_off + 4096
    qb_off = vb_off + 2080
    yt_off = qb_off + 1024
    pt_off = yt_off + 8192
    r2_off = pt_off + 2048
    wb_off = r2_off + 1408

    wbuf = WStream(ctx, AR[:, wb_off:wb_off + 2048], 1024)
    LT = AR[:, lt_off:lt_off + 16384]
    KB = AR[:, kb_off:kb_off + 4096]
    VB = AR[:, vb_off:vb_off + 2080]
    QT = AR[:, qb_off:qb_off + 512]
    QTs = AR[:, qb_off + 512:qb_off + 1024]
    YT = AR[0:64, yt_off:yt_off + 8192]
    PT2 = AR[:, pt_off:pt_off + 2048]
    RBC = AR[0:64, r2_off:r2_off + 512]
    rrow = AR[64:65, r2_off + 512:r2_off + 1024]
    kmb = AR[:, r2_off + 1024:r2_off + 1025]
    qmb = AR[:, r2_off + 1026:r2_off + 1027]
    br2 = AR[0:1, r2_off + 1028:r2_off + 1030]
    qrow_t = AR[0:1, r2_off + 1030:r2_off + 1158]
    kr1 = AR[0:1, r2_off + 1158:r2_off + 1286]
    oneb = AR[:, r2_off + 1288:r2_off + 1352]
    kmaxf = AFX[:, 170:171]
    qmaxf = AFX[:, 172:173]
    negb = AFX[:, 184:186]
    recf = AFX[64:65, 192:192 + 512]
    bbf = AFX[0:1, 704:706]
    vv = ctx.step(ctx.ve.memset(oneb, 1.0), "ve")

    te_free = None
    for oc in range(4):
        iw, vw = wbuf.load(A["l0_cdA"][oc], free_after=te_free)
        ctx.wait(ctx.te, "wq", vw)
        for tt in range(8):
            ts = slice(tt * 512, (tt + 1) * 512)
            psm = P["A"][:, (tt % 2) * 512:(tt % 2) * 512 + 512]
            if tt >= 2:
                ctx.wait(ctx.te, "se", ctx.cnt["se"] - 1)
            for kc in range(8):
                sl = wbuf.slot(iw)
                mm = ctx.te.matmul(psm, sl[:, kc * 128:kc * 128 + 128],
                                   XN[:, kc * 4096 + ts.start:kc * 4096 + ts.stop],
                                   start=(kc == 0), stop=(kc == 7))
            te_mm = ctx.step(mm, "te")
            te_free = te_mm
            ctx.wait(ctx.se, "te", te_mm)
            ctx.step(ctx.se.activation(LT[:, oc * 4096 + ts.start:oc * 4096 + ts.stop],
                                       psm, AF.Copy), "se")
    barrier(ctx, engs=("te", "ve"), sems=("se",))

    if te_free is not None:
        ctx.wait(ctx.gp, "te", te_free)
    kwsl = AR[:, wb_off:wb_off + 256]
    vw = ctx.dma("gp", kwsl, A["l0_kA"][0][:, 0:256], "wq")
    vwsl = AR[:, wb_off + 256:wb_off + 512]
    vvw = ctx.dma("gp", vwsl.rearrange("p (c d) -> p c d", d=64),
                  A["l0_v"].rearrange("(c p) d -> p c d", p=128), "wq")
    ov = VB.rearrange("p (g u) -> p g u", u=65)
    vv_one = ctx.step(ctx.ve.memset(ov[:, :, 64:65], 1.0), "ve")
    ctx.wait(ctx.te, "wq", vvw)
    ctx.wait(ctx.te, "ve", vv_one)
    for tt in range(8):
        ts = slice(tt * 512, (tt + 1) * 512)
        psm = P["A"] if tt % 2 == 0 else P["B"]
        if tt >= 2:
            ctx.wait(ctx.te, "se", ctx.cnt["se"] - 2)
        for kc in range(4):
            mm = ctx.te.matmul(psm[0:64, 0:512], kwsl[:, kc * 64:(kc + 1) * 64],
                               LT[:, kc * 4096 + ts.start:kc * 4096 + ts.stop],
                               start=(kc == 0), stop=(kc == 3))
        ctx.step(mm, "te")
        for kc in range(4):
            mm = ctx.te.matmul(psm[64:128, 0:512], kwsl[:, kc * 64:(kc + 1) * 64],
                               LT[:, kc * 4096 + ts.start:kc * 4096 + ts.stop],
                               start=(kc == 0), stop=(kc == 3))
        te_mm = ctx.step(mm, "te")
        te_free = te_mm
        ctx.wait(ctx.se, "te", te_mm)
        ctx.step(ctx.se.activation(KB[0:64, ts], psm[0:64, 0:512], AF.Copy), "se")
        ctx.step(ctx.se.activation(KB[64:128, ts], psm[64:128, 0:512], AF.Copy), "se")
    for tt2 in range(8):
        psm = P["C"] if tt2 % 2 == 0 else P["D"]
        if tt2 >= 2:
            ctx.wait(ctx.te, "se", ctx.cnt["se"] - 4)
        for u in range(4):
            t128 = tt2 * 4 + u
            us = slice(u * 64, (u + 1) * 64)
            for kc in range(4):
                mm = ctx.te.matmul(psm[:, us],
                                   LT[:, kc * 4096 + t128 * 128:kc * 4096 + (t128 + 1) * 128],
                                   vwsl[:, kc * 64:(kc + 1) * 64],
                                   start=(kc == 0), stop=(kc == 3))
        te_mm = ctx.step(mm, "te")
        te_free = te_mm
        ctx.wait(ctx.se, "te", te_mm)
        for u in range(4):
            t128 = tt2 * 4 + u
            us = slice(u * 64, (u + 1) * 64)
            ctx.step(ctx.se.activation(VB[:, t128 * 65:t128 * 65 + 64], psm[:, us],
                                       AF.Copy), "se")
    ctx.wait(ctx.ve, "se")
    chain(ctx, ctx.ve.reduce_max(kmaxf, KB[:, :], axis=AX.X,
                                 apply_absolute_value=True), "ve", "ve")
    vv = ctx.step(ctx.ve.tensor_copy(kmb, kmaxf), "ve")
    ctx.wait(ctx.ve, "ve", vv)
    ctx.wait(ctx.te, "ve", vv)
    tev = ctx.step(ctx.te.matmul(P["C"][0:1, 0:128], kmb, S["CID"][:],
                                 start=True, stop=True), "te")
    ctx.wait(ctx.ve, "te", tev)
    vv = ctx.step(ctx.ve.tensor_copy(kr1, P["C"][0:1, 0:128]), "ve")
    ctx.wait(ctx.ve, "ve", vv)
    ctx.wait(ctx.te, "ve", vv)
    barrier(ctx, engs=("te", "ve", "se"), sems=("ve", "se", "te"))

    for qq in range(8):
        qs = slice(qq * 512, (qq + 1) * 512)
        nkc = 4 * qq + 4
        for pr in range(8):
            iw, vw = wbuf.load(A["l0_quA"][pr], free_after=te_free)
            ctx.wait(ctx.te, "wq", vw)
            ctx.wait(ctx.te, "ve")
            for kc in range(4):
                mm = ctx.te.matmul(P["D"][:],
                                   wbuf.slot(iw)[:, kc * 128:(kc + 1) * 128],
                                   LT[:, kc * 4096 + qs.start:kc * 4096 + qs.stop],
                                   start=(kc == 0), stop=(kc == 3))
            te_mm = ctx.step(mm, "te")
            te_free = te_mm
            ctx.wait(ctx.se, "te", te_mm)
            sev = ctx.step(ctx.se.activation(QT, P["D"][:], AF.Copy), "se")
            ctx.wait(ctx.ve, "se", sev)
            chain(ctx, ctx.ve.reduce_max(qmaxf, QT, axis=AX.X,
                                         apply_absolute_value=True), "ve", "ve")
            vv = ctx.step(ctx.ve.tensor_copy(qmb, qmaxf), "ve")
            ctx.wait(ctx.ve, "ve", vv)
            ctx.wait(ctx.te, "ve", vv)
            tev = ctx.step(ctx.te.matmul(P["C"][0:1, 128:256], qmb, S["CID"][:],
                                         start=True, stop=True), "te")
            ctx.wait(ctx.ve, "te", tev)
            chain(ctx, ctx.ve.tensor_copy(qrow_t, P["C"][0:1, 128:256]), "ve", "ve")
            for hh in range(2):
                chain(ctx, ctx.ve.tensor_tensor(
                    RBC[0:1, hh * 64:hh * 64 + 64],
                    qrow_t[:, hh * 64:hh * 64 + 64],
                    kr1[:, 0:64], AL.mult), "ve", "ve")
                chain(ctx, ctx.ve.reduce_sum(bbf[:, hh:hh + 1],
                                             RBC[0:1, hh * 64:hh * 64 + 64],
                                             axis=AX.X), "ve", "ve")
            vv = ctx.step(ctx.ve.tensor_scalar(br2, bbf, -1.0 / 16.0, None, AL.mult), "ve")
            ctx.wait(ctx.ve, "ve", vv)
            ctx.wait(ctx.te, "ve", vv)
            mm = ctx.te.matmul(P["C"][:, 0:2], S["ONE"][:], br2, start=True, stop=True)
            te_mm = ctx.step(mm, "te")
            ctx.wait(ctx.ve, "te", te_mm)
            vv_negb = ctx.step(ctx.ve.tensor_copy(negb, P["C"][:, 0:2]), "ve")
            for hh in range(2):
                h = 2 * pr + hh
                qrow = QT[64 * hh:64 * hh + 64, :]
                ysl = YT[:, h * 512:(h + 1) * 512]
                pv_done = [None, None]
                exp_done = None
                gp_done = None
                for kc2 in range((nkc + 1) // 2):
                    sl = kc2 % 2
                    pslot = PT2[:, sl * 1024:(sl + 1) * 1024]
                    njj = min(2, nkc - kc2 * 2)
                    if exp_done is not None:
                        ctx.wait(ctx.te, "se", exp_done)
                    for j in range(njj):
                        kc = kc2 * 2 + j
                        mm = ctx.te.matmul(
                            P["A"][:, j * 512:(j + 1) * 512],
                            KB[64 * hh:64 * hh + 64, kc * 128:(kc + 1) * 128],
                            qrow, start=True, stop=True)
                    te_mm = ctx.step(mm, "te")
                    ctx.wait(ctx.se, "te", te_mm)
                    if pv_done[sl] is not None:
                        ctx.wait(ctx.se, "te", pv_done[sl])
                    if gp_done is not None:
                        ctx.wait(ctx.se, "gq", gp_done)
                    if kc2 == 0:
                        ctx.wait(ctx.se, "ve", vv_negb)
                    exp_done = ctx.step(ctx.se.activation(
                        pslot[:, 0:512 * njj], P["A"][:, 0:512 * njj], AF.Exp,
                        scale=0.125, bias=negb[:, hh:hh + 1]), "se")
                    need_mask = [j for j in range(njj) if (kc2 * 2 + j) >= 4 * qq]
                    if need_mask:
                        ctx.wait(ctx.gp, "se", exp_done)
                        for j in need_mask:
                            kc = kc2 * 2 + j
                            gp_done = ctx.step(ctx.gp.affine_select(
                                pslot[:, j * 512:(j + 1) * 512],
                                pslot[:, j * 512:(j + 1) * 512],
                                pattern=[[1, 512]], compare_op=AL.is_ge,
                                fill=0.0, base=qq * 512 - kc * 128,
                                channel_multiplier=-1), "gq")
                        ctx.wait(ctx.te, "gq", gp_done)
                    ctx.wait(ctx.te, "se", exp_done)
                    for j in range(njj):
                        kc = kc2 * 2 + j
                        mm = ctx.te.matmul(
                            P["D"][0:65, :],
                            VB[:, kc * 65:(kc + 1) * 65],
                            pslot[:, j * 512:(j + 1) * 512],
                            start=(kc == 0), stop=(kc == nkc - 1))
                    te_mm = ctx.step(mm, "te")
                    pv_done[sl] = te_mm
                ctx.wait(ctx.ve, "te", te_mm)
                chain(ctx, ctx.ve.tensor_scalar(P["D"][64:65, :], P["D"][64:65, :],
                                                1e-30, None, AL.add), "ve", "ve")
                chain(ctx, ctx.ve.reciprocal(recf, P["D"][64:65, :]), "ve", "ve")
                vv = ctx.step(ctx.ve.tensor_copy(rrow, recf), "ve")
                ctx.wait(ctx.ve, "ve", vv)
                ctx.wait(ctx.te, "ve", vv)
                mm = ctx.te.matmul(P["C"][0:64, :], oneb[64:65, :], rrow,
                                   start=True, stop=True)
                te_mm = ctx.step(mm, "te")
                ctx.wait(ctx.se, "te", te_mm)
                sev = ctx.step(ctx.se.activation(RBC, P["C"][0:64, :], AF.Copy), "se")
                ctx.wait(ctx.ve, "se", sev)
                vv = ctx.step(ctx.ve.tensor_tensor(ysl, P["D"][0:64, :], RBC, AL.mult),
                              "ve")
                ctx.wait(ctx.te, "ve", vv)
        ctx.wait(ctx.te, "ve")
        cp_r = A["l0_cp"].rearrange("(c p) d -> c p d", p=64)
        te_free = ctx.cnt["te"]
        for t2 in range(2):
            psl = [(P["A"], 0), (P["A"], 512), (P["B"], 0), (P["B"], 512)]
            for yc in range(16):
                ctx.wait(ctx.gp, "te", te_free)
                wsl = wbuf.slot(yc)[0:64, :]
                vw = ctx.dma("gp", wsl, cp_r[yc], "wq")
                ctx.wait(ctx.te, "wq", vw)
                for u in range(2):
                    t4 = t2 * 2 + u
                    for dh in range(2):
                        ps, off = psl[u * 2 + dh]
                        mm = ctx.te.matmul(
                            ps[:, off:off + 512],
                            YT[:, yc * 512 + t4 * 128: yc * 512 + (t4 + 1) * 128],
                            wsl[:, dh * 512:(dh + 1) * 512],
                            start=(yc == 0), stop=(yc == 15))
                te_mm = ctx.step(mm, "te")
                te_free = te_mm
            for u in range(2):
                t4 = t2 * 2 + u
                tt = qq * 4 + t4
                for dh in range(2):
                    ps, off = psl[u * 2 + dh]
                    ctx.wait(ctx.ve, "te", te_free)
                    xsl = X[:, tt * 1024 + dh * 512: tt * 1024 + (dh + 1) * 512]
                    ctx.step(ctx.ve.tensor_tensor(xsl, xsl, ps[:, off:off + 512],
                                                  AL.add), "ve")
            ctx.wait(ctx.te, "ve")
    barrier(ctx, engs=("te", "se", "gp", "sy"), sems=("ve", "gq"))


def emit_l0_down(ctx, A, S, P, q4, UP, wbuf, te_free):
    X, AR = S["X"], S["AR"]
    dn_r = A["l0_dn"].rearrange("(c p) d -> c p d", p=128)
    wb_off2 = 64 * 1024
    slab_use = {}
    for s in range(4):
        slab_use[s] = (None, ctx.cnt["te"])
    for dh in range(2):
        for grp in range(2):
            for yc in range(32):
                s4 = yc % 4
                ctx.wait(ctx.gp, "te", slab_use[s4][1])
                wsl = AR[:, wb_off2 + s4 * 512: wb_off2 + (s4 + 1) * 512]
                vw = ctx.dma("gp", wsl, dn_r[yc][:, dh * 512:(dh + 1) * 512], f"w{s4}")
                ctx.wait(ctx.te, f"w{s4}", vw)
                for u in range(4):
                    tloc = grp * 4 + u
                    ps, off = [(P["A"], 0), (P["A"], 512), (P["B"], 0), (P["B"], 512)][u]
                    mm = ctx.te.matmul(
                        ps[:, off:off + 512],
                        UP[:, yc * 1024 + tloc * 128: yc * 1024 + (tloc + 1) * 128],
                        wsl, start=(yc == 0), stop=(yc == 31))
                te_mm = ctx.step(mm, "te")
                te_free = te_mm
                slab_use[s4] = (vw, te_mm)
            for u in range(4):
                tloc = grp * 4 + u
                tt = q4 * 8 + tloc
                ps, off = [(P["A"], 0), (P["A"], 512), (P["B"], 0), (P["B"], 512)][u]
                ctx.wait(ctx.ve, "te", te_free)
                xsl = X[:, tt * 1024 + dh * 512: tt * 1024 + (dh + 1) * 512]
                ctx.step(ctx.ve.tensor_tensor(xsl, xsl, ps[:, off:off + 512], AL.add),
                         "ve")
            ctx.wait(ctx.te, "ve")
    return te_free


def emit_ffn_peer(ctx, A, S, P):
    X, AR, AFX = S["X"], S["AR"], S["AF"]
    XN = AR[:, 0:32 * 1024]
    up_off = 32 * 1024
    wb_off = 66 * 1024
    wbuf = WStream(ctx, AR[:, wb_off:wb_off + 2048], 1024)
    UP = AR[:, up_off:up_off + 32768]

    barrier(ctx, engs=("te", "ve", "se", "gp"), sems=("ve", "se", "te"))
    te_free = None
    for q4 in range(4 if L0_DENSE else 0):
        tq = q4 * 1024
        for oc in range(32):
            iw, vw = wbuf.load(A["l0_upA"][oc], free_after=te_free)
            ctx.wait(ctx.te, "wq", vw)
            for hh in range(2):
                hs = slice(hh * 512, (hh + 1) * 512)
                for kc in range(8):
                    mm = ctx.te.matmul(P["C"][:] if hh == 0 else P["D"][:],
                                       wbuf.slot(iw)[:, kc * 128:(kc + 1) * 128],
                                       XN[:, kc * 4096 + tq + hs.start:
                                          kc * 4096 + tq + hs.stop],
                                       start=(kc == 0), stop=(kc == 7))
            te_mm = ctx.step(mm, "te")
            te_free = te_mm
            ctx.wait(ctx.se, "te", te_mm)
            sev = ctx.step(ctx.se.activation(UP[:, oc * 1024:oc * 1024 + 512], P["C"][:],
                                             AF.Silu), "se")
            sev = ctx.step(ctx.se.activation(UP[:, oc * 1024 + 512:(oc + 1) * 1024],
                                             P["D"][:], AF.Silu), "se")
            ctx.wait(ctx.te, "se", sev)
        if DBG and q4 == 0:
            ctx.wait(ctx.gp, "se")
            ctx.dma("gp", A["dbg_xn"], UP, "so")
            ctx.wait(ctx.gp, "so")
        te_free = emit_l0_down(ctx, A, S, P, q4, UP, wbuf, te_free)
    barrier(ctx, engs=("te", "ve", "se", "gp", "sy"), sems=("ve", "se", "te", "wq"))

    dt_off = 32 * 1024
    gt_off = dt_off + 8192
    g_off = gt_off + 8192
    b1_off = g_off + 8192
    b2_off = b1_off + 1024
    b3_off = b2_off + 1024
    qp_off = b3_off + 1024
    ky_off = qp_off + 512
    ex_off = ky_off + 512
    DOTS = AR[:, dt_off:dt_off + 8192]
    GT = AR[:, gt_off:gt_off + 8192]
    G = AR[:, g_off:g_off + 4096]
    B1 = AR[:, b1_off:b1_off + 1024]
    B2 = AR[:, b2_off:b2_off + 1024]
    B3 = AR[:, b3_off:b3_off + 1024]
    QP = AR[:, qp_off:qp_off + 512]
    KEYS = AR[:, ky_off:ky_off + 512]
    EXH = AR[:, ex_off:ex_off + 64]
    SIM = AFX[:, 0:512]
    T8X = AFX[:, 512:520]
    T8Y = AFX[:, 520:528]
    T8G = AFX[:, 528:536]
    GRD = AFX[:, 544:608]
    E8 = AFX[:, 608:616]
    ZS = AFX[:, 616:617]
    negmx = AFX[:, 617:618]
    ctx.dma("sy", KEYS, A["l0_keys"], "xq")
    ctx.wait(ctx.te, "xq")
    deT_r = A["l0_deT"]
    ue_r = A["l0_ue"]
    qw_r = A["l0_qA"]

    te_free = None
    slab_use = {}
    for t128 in range(32 if L0_PEER else 0):
        tcol = t128 * 128
        for s in range(4):
            slab_use[s] = (None, ctx.cnt["te"])
        for ec in range(8):
            psm = P["C"] if ec % 2 == 0 else P["D"]
            for kc in range(8):
                i = ec * 8 + kc
                s4 = i % 4
                if s4 in slab_use:
                    ctx.wait(ctx.gp, "te", slab_use[s4][1])
                wsl = AR[:, wb_off + s4 * 512: wb_off + (s4 + 1) * 512]
                vw = ctx.dma("gp", wsl,
                             deT_r[kc * 128:(kc + 1) * 128, ec * 512:(ec + 1) * 512],
                             f"w{s4}")
                ctx.wait(ctx.te, f"w{s4}", vw)
                mm = ctx.te.matmul(psm[:, 0:512],
                                   XN[:, kc * 4096 + tcol: kc * 4096 + tcol + 128],
                                   wsl, start=(kc == 0), stop=(kc == 7))
                te_mm = ctx.step(mm, "te")
                te_free = te_mm
                slab_use[s4] = (vw, te_mm)
            ctx.wait(ctx.se, "te", te_free)
            sev = ctx.step(ctx.se.activation(DOTS[:, ec * 512:(ec + 1) * 512],
                                             psm[:, 0:512], AF.Copy), "se")
            ctx.wait(ctx.te, "se", sev)
        for oc in range(4 if L0_PSTG >= 2 else 0):
            iw, vw = wbuf.load(qw_r[oc], free_after=te_free)
            ctx.wait(ctx.te, "wq", vw)
            psm = P["A"][:, (oc % 2) * 512:(oc % 2) * 512 + 128]
            if oc >= 2:
                ctx.wait(ctx.te, "se", ctx.cnt["se"] - 1)
            for kc in range(8):
                mm = ctx.te.matmul(psm, wbuf.slot(iw)[:, kc * 128:(kc + 1) * 128],
                                   XN[:, kc * 4096 + tcol: kc * 4096 + tcol + 128],
                                   start=(kc == 0), stop=(kc == 7))
            te_mm = ctx.step(mm, "te")
            te_free = te_mm
            ctx.wait(ctx.se, "te", te_mm)
            sev = ctx.step(ctx.se.activation(QP[:, oc * 128:(oc + 1) * 128], psm,
                                             AF.Copy), "se")
        if L0_PSTG >= 2:
            ctx.wait(ctx.te, "se")
            slots = [(P["A"], 0), (P["A"], 512), (P["B"], 0), (P["B"], 512)]
            for w in range(2):
                for i in range(4):
                    blk = w * 4 + i
                    p, hq = blk // 4, blk % 4
                    hb = 64 * (hq % 2)
                    cc = 2 * p + hq // 2
                    ps, off = slots[i]
                    mm = ctx.te.matmul(ps[:, off:off + 64],
                                       QP[hb:hb + 64, cc * 128:(cc + 1) * 128],
                                       KEYS[hb:hb + 64, blk * 64:(blk + 1) * 64],
                                       start=True, stop=True)
                te_mm = ctx.step(mm, "te")
                ctx.wait(ctx.se, "te", te_mm)
                for i in range(4):
                    blk = w * 4 + i
                    ps, off = slots[i]
                    sev = ctx.se.activation(SIM[:, blk * 64:(blk + 1) * 64],
                                            ps[:, off:off + 64], AF.Copy)
                sev = ctx.step(sev, "se")
                ctx.wait(ctx.te, "se", sev)
            ctx.wait(ctx.ve, "se", sev)
        for hq in range(4 if L0_PSTG >= 3 else 0):
            xs = slice(hq * 64, (hq + 1) * 64)
            ys = slice(256 + hq * 64, 256 + (hq + 1) * 64)
            chain(ctx, ctx.ve.max(out=T8X, in_=SIM[:, xs]), "ve", "ve")
            chain(ctx, ctx.ve.max(out=T8Y, in_=SIM[:, ys]), "ve", "ve")
            g3 = GRD.rearrange("p (i j) -> p i j", i=8)
            chain(ctx, ctx.ve.tensor_copy(g3, T8Y.unsqueeze(1).to_broadcast([128, 8, 8])),
                  "ve", "ve")
            chain(ctx, ctx.ve.tensor_tensor(g3, g3,
                                            T8X.unsqueeze(2).to_broadcast([128, 8, 8]),
                                            AL.add), "ve", "ve")
            chain(ctx, ctx.ve.max(out=T8G, in_=GRD), "ve", "ve")
            vv = ctx.step(ctx.ve.tensor_scalar(negmx, T8G[:, 0:1], -1.0, None, AL.mult),
                          "ve")
            ctx.wait(ctx.ve, "ve", vv)
            ctx.wait(ctx.se, "ve", vv)
            chain(ctx, ctx.se.activation(E8, T8G, AF.Exp, bias=negmx, accum_out=ZS),
                  "se", "se")
            sev = ctx.step(ctx.se.activation(EXH, SIM[:, xs], AF.Exp, bias=negmx), "se")
            ctx.wait(ctx.ve, "se", sev)
            chain(ctx, ctx.ve.reciprocal(ZS, ZS), "ve", "ve")
            for ech in range(4):
                es = slice(ech * 1024, (ech + 1) * 1024)
                sev = ctx.step(ctx.se.activation(
                    B1.rearrange("p (i j) -> p i j", j=64),
                    SIM[:, ys].unsqueeze(1).to_broadcast([128, 16, 64]), AF.Exp), "se")
                ctx.wait(ctx.ve, "se", sev)
                exv = EXH[:, ech * 16:(ech + 1) * 16]
                chain(ctx, ctx.ve.tensor_tensor(
                    B2.rearrange("p (i j) -> p i j", j=64),
                    B1.rearrange("p (i j) -> p i j", j=64),
                    exv.unsqueeze(2).to_broadcast([128, 16, 64]), AL.mult), "ve", "ve")
                chain(ctx, ctx.ve.scalar_tensor_tensor(B3, B2, E8[:, 7:8], B2,
                                                       AL.is_ge, AL.mult), "ve", "ve")
                chain(ctx, ctx.ve.scalar_tensor_tensor(B1, DOTS[:, es], ZS, B3,
                                                       AL.mult, AL.mult), "ve", "ve")
                ctx.wait(ctx.se, "ve")
                sev = ctx.step(ctx.se.activation(B2, B1, AF.Tanh, scale=0.5), "se")
                ctx.wait(ctx.ve, "se", sev)
                chain(ctx, ctx.ve.scalar_tensor_tensor(B3, B2, 1.0, B1, AL.add, AL.mult),
                      "ve", "ve")
                if hq == 0:
                    vv = ctx.step(ctx.ve.tensor_scalar(G[:, es], B3, 0.5, None, AL.mult),
                                  "ve")
                else:
                    vv = ctx.step(ctx.ve.scalar_tensor_tensor(G[:, es], B3, 0.5,
                                                              G[:, es], AL.mult, AL.add),
                                  "ve")
                ctx.wait(ctx.ve, "ve", vv)
                ctx.wait(ctx.se, "ve", vv)
        ctx.wait(ctx.te, "ve")
        for half in range(4 if L0_PSTG >= 4 else 0):
            bb = half % 2
            if half >= 2:
                ctx.wait(ctx.te, "ve", gt_ve[half - 2])
            for blk in range(8):
                c = half * 8 + blk
                tr = ctx.te.transpose(P["T"][:, bb * 1024 + blk * 128: bb * 1024 + (blk + 1) * 128],
                                      G[:, c * 128:(c + 1) * 128], S["CID"][:])
            tr_v = ctx.step(tr, "te")
            ctx.wait(ctx.ve, "te", tr_v)
            for blk in range(8):
                c = half * 8 + blk
                cp = ctx.ve.tensor_copy(GT[:, c * 128:(c + 1) * 128],
                                        P["T"][:, bb * 1024 + blk * 128: bb * 1024 + (blk + 1) * 128])
            if half == 0:
                gt_ve = {}
            gt_ve[half] = ctx.step(cp, "ve")
        ctx.wait(ctx.te, "ve")
        for s in range(4):
            slab_use[s] = (None, ctx.cnt["te"])
        for dc in range(2 if L0_PSTG >= 5 else 0):
            psm = P["A"] if dc == 0 else P["B"]
            for kc in range(32):
                i = dc * 32 + kc
                s4 = i % 4
                if s4 in slab_use:
                    ctx.wait(ctx.gp, "te", slab_use[s4][1])
                wsl = AR[:, wb_off + s4 * 512: wb_off + (s4 + 1) * 512]
                vw = ctx.dma("gp", wsl,
                             ue_r[kc * 128:(kc + 1) * 128, dc * 512:(dc + 1) * 512], f"w{s4}")
                ctx.wait(ctx.te, f"w{s4}", vw)
                mm = ctx.te.matmul(psm[:, 0:512], GT[:, kc * 128:(kc + 1) * 128],
                                   wsl, start=(kc == 0), stop=(kc == 31))
                te_mm = ctx.step(mm, "te")
                te_free = te_mm
                slab_use[s4] = (vw, te_mm)
            ctx.wait(ctx.ve, "te", te_free)
            xsl = X[:, t128 * 1024 + dc * 512: t128 * 1024 + (dc + 1) * 512]
            vv = ctx.step(ctx.ve.tensor_tensor(xsl, xsl, psm[:, 0:512], AL.add), "ve")
            ctx.wait(ctx.te, "ve", vv)
    barrier(ctx, engs=("te", "se", "gp", "sy"), sems=("ve", "se"))


ATTN = {0: emit_attn_l0, 1: emit_attn_l1, 2: emit_attn_l2, 3: emit_attn_l3}
FFN = {0: emit_ffn_peer, 1: emit_ffn_moe, 2: make_ffn_ternary(2), 3: make_ffn_ternary(3)}



def kernel(**inputs):
    shared = host_prepare(inputs)
    x = np.ascontiguousarray(np.asarray(inputs["L0_attn_x"], dtype=np.float32))
    B = x.shape[0]
    cur = [np.ascontiguousarray(x[b]) for b in range(B)]
    if FUSED:
        groups = [list(BLOCKS)]
    else:
        groups = [[li] for li in BLOCKS]
    for gblocks in groups:
        nc = build_program(gblocks, shared)
        in_maps = [dict(shared, x_in=cur[b]) for b in range(B)]
        res = run_bass_kernel_spmd(nc, in_maps, core_ids=list(range(B)))
        cur = [np.asarray(res.results[b]["x_out"], dtype=np.float32) for b in range(B)]
    return np.stack(cur).reshape(B, T, D)


# version 04
# speedup vs baseline: 1.0290x; 1.0273x over previous
import numpy as np
import ml_dtypes
from contextlib import ExitStack

import concourse.bass as bass
import concourse.mybir as mybir
from concourse.bass_utils import run_bass_kernel_spmd

F32 = mybir.dt.float32
BF16 = mybir.dt.bfloat16
AL = mybir.AluOpType
AF = mybir.ActivationFunctionType
AX = mybir.AxisListType

T = 4096
D = 1024
NTT = T // 128
BF = np.dtype(ml_dtypes.bfloat16)

SELU_L = 1.0507009873554804934193349852946
SELU_A = 1.6732632423543772848170429916717

BLOCKS = [0, 1, 2, 3]
FUSED = True
DO_ATTN = True
DO_FFN = True
L0_DENSE = True
L0_PEER = True
L0_PSTG = 99
L0_CONSV = True
LN_LIMIT = 2
DBG = False


def _packA(W, of_tile=128):
    IF, OF = W.shape
    return np.ascontiguousarray(
        W.reshape(IF // 128, 128, OF // of_tile, of_tile)
         .transpose(2, 1, 0, 3)
         .reshape(OF // of_tile, 128, (IF // 128) * of_tile))


def _col128(v):
    v = np.asarray(v, np.float32).reshape(-1)
    return np.ascontiguousarray(v.reshape(v.shape[0] // 128, 128).T)


def _row(v):
    return np.ascontiguousarray(np.asarray(v, np.float32).reshape(1, -1))


def host_prepare(inputs):
    g = {}
    f32 = lambda a: np.ascontiguousarray(np.asarray(a, dtype=np.float32))

    g["c_ident"] = np.eye(128, dtype=BF)
    g["c_ones1"] = np.ones((1, 128), dtype=BF)

    g["l0_cdA"] = _packA(f32(inputs["L0_attn_Wc_down"]))
    g["l0_quA"] = _packA(f32(inputs["L0_attn_Wq_up"]))
    g["l0_kA"] = _packA(np.ascontiguousarray(f32(inputs["L0_attn_Wkv_up"])[:, :64]), of_tile=64)
    g["l0_v"] = np.ascontiguousarray(f32(inputs["L0_attn_Wkv_up"])[:, 64:])
    g["l0_cp"] = f32(inputs["L0_attn_Wc_proj"])

    qswp = np.arange(1024).reshape(8, 2, 64)[:, ::-1].reshape(-1)
    g["l0_quAs"] = _packA(np.ascontiguousarray(f32(inputs["L0_attn_Wq_up"])[:, qswp]))
    g["l0_upA"] = _packA(f32(inputs["L0_ffn_W_up"]).T)
    g["l0_dn"] = np.ascontiguousarray(f32(inputs["L0_ffn_W_down"]).T)
    g["l0_qA"] = _packA(f32(inputs["L0_ffn_W_q"]).T)
    keys = f32(inputs["L0_ffn_keys"])
    karr = keys.transpose(2, 0, 3, 1).reshape(8, 64, 64)
    kdup = np.concatenate([karr, karr], axis=1)
    g["l0_keys"] = np.ascontiguousarray(
        kdup.transpose(1, 0, 2).reshape(128, 512).astype(BF))
    g["l0_deT"] = np.ascontiguousarray(f32(inputs["L0_ffn_down_embed"]).T)
    g["l0_ue"] = f32(inputs["L0_ffn_up_embed"])

    Wqk = f32(inputs["L1_attn_W_qk"])
    bqk = f32(inputs["L1_attn_b_qk"])
    qr = np.empty(1024, np.int64); kr = np.empty(1024, np.int64)
    for h in range(16):
        for d in range(64):
            qr[h * 64 + d] = h * 128 + 2 * d
            kr[h * 64 + d] = h * 128 + 2 * d + 1
    g["l1_qA"] = _packA(np.ascontiguousarray(Wqk[qr, :].T), of_tile=64)
    g["l1_kA"] = _packA(np.ascontiguousarray(Wqk[kr, :].T), of_tile=64)
    g["l1_bqc"] = _col128(np.pad(bqk[qr].reshape(16, 64), ((0, 0), (0, 64))).reshape(-1))
    g["l1_bkc"] = _col128(np.pad(bqk[kr].reshape(16, 64), ((0, 0), (0, 64))).reshape(-1))

    g["l1_wr"] = f32(inputs["L1_ffn_Wr"])
    g["l1_brr"] = _row(inputs["L1_ffn_br"])
    W1 = f32(inputs["L1_ffn_W1"]); W2 = f32(inputs["L1_ffn_W2"])
    g["l1_w1A"] = np.ascontiguousarray(np.stack([_packA(W1[e]) for e in range(8)]))
    g["l1_w2"] = W2
    b1 = f32(inputs["L1_ffn_b1"]); b2 = f32(inputs["L1_ffn_b2"])
    g["l1_b1c"] = np.ascontiguousarray(np.stack([_col128(b1[e]) for e in range(8)]))
    g["l1_b2r"] = np.ascontiguousarray(np.stack([_row(b2[e]) for e in range(8)]))

    g["l2_qA"] = _packA(f32(inputs["L2_attn_Wq"]))
    g["l2_kA"] = _packA(f32(inputs["L2_attn_Wk"]))
    qswp = np.arange(1024).reshape(8, 2, 64)[:, ::-1].reshape(-1)
    g["l2_qAs"] = _packA(np.ascontiguousarray(f32(inputs["L2_attn_Wq"])[:, qswp]))
    g["l2_bqcs"] = _col128(np.asarray(inputs["L2_attn_bq"], np.float32)[qswp])
    g["l2_v"] = f32(inputs["L2_attn_Wv"])
    perm = np.empty(1024, np.int64)
    for h in range(16):
        for d in range(64):
            perm[h * 64 + d] = (h % 4) * 256 + (h // 4) * 64 + d
    g["l2_wo"] = np.ascontiguousarray(f32(inputs["L2_attn_Wo"])[perm, :])
    g["l2_bqc"] = _col128(inputs["L2_attn_bq"])
    g["l2_bkc"] = _col128(inputs["L2_attn_bk"])
    g["l2_bvr"] = _row(inputs["L2_attn_bv"])
    g["l2_bor"] = _row(inputs["L2_attn_bo"])

    g["l3_qA"] = _packA(np.ascontiguousarray(f32(inputs["L3_attn_Wq"]).T))
    g["l3_kA"] = _packA(np.ascontiguousarray(f32(inputs["L3_attn_Wk"]).T))
    g["l3_v"] = np.ascontiguousarray(f32(inputs["L3_attn_Wv"]).T)
    g["l3_wu"] = np.ascontiguousarray(f32(inputs["L3_attn_Wu"]).T)
    g["l3_bur"] = _row(inputs["L3_attn_bu"])
    half = 512
    theta = 1.0 / (10000.0 ** (np.arange(half, dtype=np.float64) / half))
    ang = np.arange(T, dtype=np.float64)[:, None] * theta[None, :]
    cosT = np.cos(ang).astype(np.float32).T
    sinT = np.sin(ang).astype(np.float32).T
    ctab = np.empty((1024, T), np.float32); stab = np.empty((1024, T), np.float32)
    ctab[0::2] = cosT; ctab[1::2] = cosT
    stab[0::2] = -sinT; stab[1::2] = sinT
    g["l3_cos"] = ctab; g["l3_sin2"] = stab
    psw = np.zeros((128, 128), dtype=BF)
    for p in range(128):
        psw[p ^ 1, p] = 1.0
    g["c_psw"] = psw

    for L in (2, 3):
        p, o = f"L{L}_ffn_", f"t{L}_"
        g[o + "w1A"] = _packA(f32(inputs[p + "w1_w"]).T)
        g[o + "w2A"] = _packA(f32(inputs[p + "w2_w"]).T)
        g[o + "h1A"] = _packA(f32(inputs[p + "h1_w"]).T)
        g[o + "h2A"] = _packA(f32(inputs[p + "h2_w"]).T)
        g[o + "w3A"] = _packA(f32(inputs[p + "w3_w"]).T)
        g[o + "s1c"] = _col128(inputs[p + "w1_s"])
        g[o + "s2c"] = _col128(inputs[p + "w2_s"])
        g[o + "h1sc"] = _col128(inputs[p + "h1_s"])
        g[o + "h2sc"] = _col128(inputs[p + "h2_s"])
        g[o + "s3c"] = _col128(inputs[p + "w3_s"])
    for nm in ("t2_w1A", "t2_w2A", "t2_h1A", "t2_h2A", "t2_w3A",
               "t3_w1A", "t3_w2A", "t3_h1A", "t3_h2A", "t3_w3A",
               "l0_cdA", "l0_quA", "l0_quAs", "l0_kA", "l0_v", "l0_cp",
               "l0_upA", "l0_dn", "l0_qA", "l0_deT", "l0_ue",
               "l1_qA", "l1_kA", "l1_w1A", "l1_w2", "l1_wr",
               "l2_qA", "l2_qAs", "l2_kA", "l2_v", "l2_wo",
               "l3_qA", "l3_kA", "l3_v", "l3_wu", "l3_cos", "l3_sin2"):
        g[nm] = np.ascontiguousarray(g[nm].astype(BF))
    return g



class _DummyInst:
    def then_inc(self, *a, **k):
        return self


_DI = _DummyInst()


class _DummyEng:
    def __getattr__(self, name):
        def f(*a, **k):
            return _DI
        return f


_DE = _DummyEng()

ROLE_ATTR = {"tensor": "te", "vector": "ve", "scalar": "se",
             "gpsimd": "gp", "sync": "sy"}


class Ctx:
    def __init__(self, nc, sems):
        self.nc = nc
        self.sems = sems

    def begin(self, role, eng):
        self.role = role
        self.cnt = dict((s, 0) for s in self.sems)
        for r, at in ROLE_ATTR.items():
            setattr(self, at, eng if r == role else _DE)

    def step(self, instr, sem, n=1):
        self.cnt[sem] += n
        if instr is not _DI and instr is not None:
            instr.then_inc(self.sems[sem], n)
        return self.cnt[sem]

    def wait(self, eng, sem, val=None):
        v = self.cnt[sem] if val is None else val
        if v > 0:
            eng.wait_ge(self.sems[sem], v)

    def dma(self, q, out, in_, sem):
        eng = self.gp if q == "gp" else self.sy
        ins = eng.dma_start(out=out, in_=in_)
        return self.step(ins, sem, 16)


def barrier(ctx, engs, sems):
    for en in engs:
        e = getattr(ctx, en)
        for sm in sems:
            ctx.wait(e, sm)


def chain(ctx, instr, sem, eng_at):
    v = ctx.step(instr, sem)
    ctx.wait(getattr(ctx, eng_at), sem, v)
    return v


class WStream:

    def __init__(self, ctx, buf, cols):
        self.ctx, self.buf, self.cols = ctx, buf, cols
        self.n = 0

    def load(self, src_ap, free_after=None):
        ctx = self.ctx
        if free_after is not None:
            ctx.wait(ctx.gp, "te", free_after)
        i = self.n
        b = (i % 2) * self.cols
        w = src_ap.shape[-1] if len(src_ap.shape) == 2 else self.cols
        wq = ctx.dma("gp", self.buf[:, b:b + w], src_ap, "wq")
        self.n += 1
        return i, wq

    def slot(self, i):
        b = (i % 2) * self.cols
        return self.buf[:, b:b + self.cols]


AR_COLS = 69 * 1024


def build_program(blocks, shared):
    nc = bass.Bass("TRN2", target_bir_lowering=False)
    A = {}
    A["x_in"] = nc.dram_tensor("x_in", [T, D], F32, kind="ExternalInput").ap()
    A["x_out"] = nc.dram_tensor("x_out", [T, D], F32, kind="ExternalOutput").ap()
    for name, arr in shared.items():
        A[name] = nc.dram_tensor(name, list(arr.shape),
                                 BF16 if arr.dtype == BF else F32,
                                 kind="ExternalInput").ap()
    A["xnsp"] = nc.dram_tensor("xnsp", [128, 8 * T], BF16, kind="Internal").ap()
    if 3 in blocks:
        A["l3qk"] = nc.dram_tensor("l3qk", [2, 128, 8 * T], BF16, kind="Internal").ap()
        A["l3v"] = nc.dram_tensor("l3v", [16, 32, 128, 65], BF16, kind="Internal").ap()
    if DBG:
        A["dbg"] = nc.dram_tensor("dbg", [128, 512], F32, kind="ExternalOutput").ap()
        A["dbg_xn"] = nc.dram_tensor("dbg_xn", [128, 32768], BF16, kind="ExternalOutput").ap()
        A["dbg_kt"] = nc.dram_tensor("dbg_kt", [128, 8192], BF16, kind="ExternalOutput").ap()

    with ExitStack() as st:
        S = {}
        S["X"] = st.enter_context(nc.sbuf_tensor("X", [128, NTT * 1024], BF16))
        S["AR"] = st.enter_context(nc.sbuf_tensor("AR", [128, AR_COLS - (1024 if DBG else 0)], BF16))
        S["AF"] = st.enter_context(nc.sbuf_tensor("AFX", [128, 1024], F32))
        S["CID"] = st.enter_context(nc.sbuf_tensor("CID", [128, 128], BF16))
        S["PSW"] = st.enter_context(nc.sbuf_tensor("PSW", [128, 128], BF16))
        S["ONE"] = st.enter_context(nc.sbuf_tensor("ONEX", [1, 128], BF16))
        S["STAT"] = st.enter_context(nc.sbuf_tensor("STAT", [128, 128], F32))
        if DBG:
            S["DBG"] = st.enter_context(nc.sbuf_tensor("DBGT", [128, 512], F32))
        P = {}
        P["A"] = st.enter_context(nc.psum_tensor("PSA", [128, 1024], F32))
        P["B"] = st.enter_context(nc.psum_tensor("PSB", [128, 1024], F32))
        P["C"] = st.enter_context(nc.psum_tensor("PSC", [128, 512], F32))
        P["D"] = st.enter_context(nc.psum_tensor("PSD", [128, 512], F32))
        P["T"] = st.enter_context(nc.psum_tensor("PST", [128, 2048], BF16))

        sems = dict((n, st.enter_context(nc.semaphore(n)))
                    for n in ("wq", "xq", "te", "ve", "se", "gq", "st", "so",
                              "w0", "w1", "w2", "w3"))
        block = st.enter_context(nc.Block())
        ctx = Ctx(nc, sems)

        def program(ctx):
            emit_prologue(ctx, A, S, P)
            for li in blocks:
                emit_ln_xn(ctx, A, S, P)
                if DO_ATTN:
                    ATTN[li](ctx, A, S, P)
                if DO_FFN:
                    FFN[li](ctx, A, S, P)
            emit_epilogue(ctx, A, S, P)

        @block.tensor
        def _(e):
            ctx.begin("tensor", e); program(ctx)

        @block.vector
        def _(e):
            ctx.begin("vector", e); program(ctx)

        @block.scalar
        def _(e):
            ctx.begin("scalar", e); program(ctx)

        @block.gpsimd
        def _(e):
            ctx.begin("gpsimd", e); program(ctx)

        @block.sync
        def _(e):
            ctx.begin("sync", e); program(ctx)

    return nc


def emit_prologue(ctx, A, S, P):
    if DBG:
        ctx.step(ctx.ve.memset(S["DBG"][:], 0.0), "ve")
    ctx.dma("sy", S["CID"][:], A["c_ident"], "xq")
    ctx.dma("sy", S["PSW"][:], A["c_psw"], "xq")
    ctx.dma("sy", S["ONE"][:], A["c_ones1"], "xq")
    src = A["x_in"].rearrange("(t p) d -> p t d", p=128)
    dst = S["X"][:].rearrange("p (t d) -> p t d", d=1024)
    ctx.dma("gp", dst, src, "wq")
    for en in ("te", "ve", "se"):
        e = getattr(ctx, en)
        ctx.wait(e, "xq")
        ctx.wait(e, "wq")


def emit_epilogue(ctx, A, S, P):
    dst = A["x_out"].rearrange("(t p) d -> p t d", p=128)
    src = S["X"][:].rearrange("p (t d) -> p t d", d=1024)
    ctx.wait(ctx.gp, "ve")
    ctx.wait(ctx.gp, "se")
    ctx.dma("gp", dst, src, "so")
    if DBG:
        ctx.dma("gp", A["dbg"], S["DBG"][:], "so")
    ctx.wait(ctx.gp, "so")



def emit_ln_xn(ctx, A, S, P):
    X, ST, AR = S["X"], S["STAT"], S["AR"]
    XN = AR[:, 0:NTT * 1024]
    jnk = AR[:, 64 * 1024:65 * 1024]
    barrier(ctx, engs=("ve", "se"), sems=("ve", "se", "te"))
    for t in range(NTT):
        xt = X[:, t * 1024:(t + 1) * 1024]
        ctx.step(ctx.ve.reduce_sum(ST[:, t:t + 1], xt, axis=AX.X), "ve")
        ctx.step(ctx.se.activation(jnk, xt, AF.Square,
                                   accum_out=ST[:, 32 + t:33 + t]), "se")
    if LN_LIMIT < 1:
        barrier(ctx, engs=("te", "gp", "sy"), sems=("ve", "se"))
        return
    ctx.wait(ctx.ve, "se")
    ctx.wait(ctx.ve, "ve")
    chain(ctx, ctx.ve.tensor_scalar(ST[:, 0:32], ST[:, 0:32], 1.0 / 1024.0, None, AL.mult), "ve", "ve")
    chain(ctx, ctx.ve.tensor_tensor(ST[:, 96:128], ST[:, 0:32], ST[:, 0:32], AL.mult), "ve", "ve")
    chain(ctx, ctx.ve.scalar_tensor_tensor(ST[:, 32:64], ST[:, 32:64], 1.0 / 1024.0,
                                           ST[:, 96:128], AL.mult, AL.subtract), "ve", "ve")
    chain(ctx, ctx.ve.tensor_scalar(ST[:, 32:64], ST[:, 32:64], 1e-5, None, AL.add), "ve", "ve")
    ctx.wait(ctx.se, "ve")
    chain(ctx, ctx.se.activation(ST[:, 64:96], ST[:, 32:64], AF.Sqrt), "se", "se")
    ctx.wait(ctx.ve, "se")
    chain(ctx, ctx.ve.reciprocal(ST[:, 64:96], ST[:, 64:96]), "ve", "ve")
    chain(ctx, ctx.ve.scalar_tensor_tensor(ST[:, 96:128], ST[:, 0:32], -1.0,
                                           ST[:, 64:96], AL.mult, AL.mult), "ve", "ve")
    ctx.wait(ctx.se, "ve")
    if LN_LIMIT < 2:
        barrier(ctx, engs=("te", "se", "gp", "sy"), sems=("ve",))
        return
    xp = [AR[:, 65 * 1024:66 * 1024], AR[:, 66 * 1024:67 * 1024]]
    te_tile = {}
    ve_tile = {}
    for t in range(NTT):
        b = t % 2
        xt = X[:, t * 1024:(t + 1) * 1024]
        if t >= 2:
            ctx.wait(ctx.se, "te", te_tile[t - 2])
        sev = ctx.step(ctx.se.activation(xp[b], xt, AF.Identity,
                                         bias=ST[:, 96 + t:97 + t],
                                         scale=ST[:, 64 + t:65 + t]), "se")
        ctx.wait(ctx.te, "se", sev)
        if t >= 2:
            ctx.wait(ctx.te, "ve", ve_tile[t - 2])
        for c in range(8):
            tr = ctx.te.transpose(P["T"][:, b * 1024 + c * 128: b * 1024 + (c + 1) * 128],
                                  xp[b][:, c * 128:(c + 1) * 128], S["CID"][:])
        te_tile[t] = ctx.step(tr, "te")
        ctx.wait(ctx.ve, "te", te_tile[t])
        for c in range(8):
            cp = ctx.ve.tensor_copy(
                XN[:, c * 4096 + t * 128: c * 4096 + (t + 1) * 128],
                P["T"][:, b * 1024 + c * 128: b * 1024 + (c + 1) * 128])
        ve_tile[t] = ctx.step(cp, "ve")
    barrier(ctx, engs=("te", "se", "gp", "sy"), sems=("ve",))



def make_ffn_ternary(li):
    o = f"t{li}_"

    def emit(ctx, A, S, P):
        X, AR, AFX = S["X"], S["AR"], S["AF"]
        XN = AR[:, 0:32 * 1024]
        aoff, boff = 8192, 40960
        woff = 65536
        joff = 65536 + 4096

        def a_chunk(c):
            return AR[:, aoff + c * 1024: aoff + (c + 1) * 1024]

        def b_chunk(c):
            if c < 24:
                return AR[:, boff + c * 1024: boff + (c + 1) * 1024]
            return AR[:, (c - 24) * 1024: (c - 23) * 1024]

        jnk = AR[:, joff:joff + 1024]
        wbuf1 = WStream(ctx, AR[:, woff:woff + 2048], 1024)
        wbuf = WStream(ctx, AR[:, woff:woff + 4096], 2048)

        barrier(ctx, engs=("sy",), sems=("te", "ve", "se"))
        ctx.dma("sy", A["xnsp"][:, :], XN, "st")
        spilled = ctx.cnt["st"]
        for en in ("te", "ve", "se", "gp"):
            ctx.wait(getattr(ctx, en), "st", spilled)

        s1c, s2c = AFX[:, 0:32], AFX[:, 32:64]
        h1c, h2c = AFX[:, 64:96], AFX[:, 96:128]
        s3c = AFX[:, 128:136]
        for nm, dst in ((o + "s1c", s1c), (o + "s2c", s2c), (o + "h1sc", h1c),
                        (o + "h2sc", h2c), (o + "s3c", s3c)):
            ctx.dma("sy", dst, A[nm], "xq")
        for en in ("se", "ve"):
            ctx.wait(getattr(ctx, en), "xq")

        te_free = None
        for q in range(4):
            xnq_src = A["xnsp"].rearrange("p (c t) -> p c t", t=T)[:, :, q * 1024:(q + 1) * 1024]
            xnq_dst = AR[:, 0:8192].rearrange("p (c t) -> p c t", t=1024)
            if q > 0:
                ctx.wait(ctx.sy, "te")
            xq_v = ctx.dma("sy", xnq_dst, xnq_src, "xq")
            ctx.wait(ctx.te, "xq", xq_v)

            def XNq(c, ts):
                return AR[:, c * 1024 + ts.start: c * 1024 + ts.stop]

            for oc in range(32):
                i1, v1 = wbuf1.load(A[o + "w1A"][oc], free_after=te_free)
                i2, v2 = wbuf1.load(A[o + "w2A"][oc], free_after=te_free)
                ctx.wait(ctx.te, "wq", v1)
                for hh in range(2):
                    ts = slice(hh * 512, (hh + 1) * 512)
                    for kc in range(8):
                        mm = ctx.te.matmul(P["A"][:, ts],
                                           wbuf1.slot(i1)[:, kc * 128:(kc + 1) * 128],
                                           XNq(kc, ts), start=(kc == 0), stop=(kc == 7))
                ctx.step(mm, "te")
                ctx.wait(ctx.te, "wq", v2)
                for hh in range(2):
                    ts = slice(hh * 512, (hh + 1) * 512)
                    for kc in range(8):
                        mm = ctx.te.matmul(P["B"][:, ts],
                                           wbuf1.slot(i2)[:, kc * 128:(kc + 1) * 128],
                                           XNq(kc, ts), start=(kc == 0), stop=(kc == 7))
                te_mm = ctx.step(mm, "te")
                te_free = te_mm
                ctx.wait(ctx.se, "te", te_mm)
                sev = ctx.step(ctx.se.activation(jnk, P["A"][:], AF.Silu,
                                                 scale=s1c[:, oc:oc + 1]), "se")
                ctx.wait(ctx.ve, "se", sev)
                vev = ctx.step(ctx.ve.scalar_tensor_tensor(
                    a_chunk(oc), P["B"][:], s2c[:, oc:oc + 1], jnk, AL.mult, AL.mult), "ve")
                ctx.wait(ctx.te, "ve", vev)

            for wname, scol, srcf, dstf in ((o + "h1A", h1c, a_chunk, b_chunk),
                                            (o + "h2A", h2c, b_chunk, a_chunk)):
                for oc in range(32):
                    ih, vh = wbuf.load(A[wname][oc][:, 0:2048], free_after=te_free)
                    ih2, vh2 = wbuf.load(A[wname][oc][:, 2048:4096], free_after=None)
                    ctx.wait(ctx.te, "wq", vh)
                    for hh in range(2):
                        ts = slice(hh * 512, (hh + 1) * 512)
                        for kc in range(16):
                            mm = ctx.te.matmul(P["A"][:, ts],
                                               wbuf.slot(ih)[:, kc * 128:(kc + 1) * 128],
                                               srcf(kc)[:, ts], start=(kc == 0), stop=False)
                    ctx.step(mm, "te")
                    ctx.wait(ctx.te, "wq", vh2)
                    for hh in range(2):
                        ts = slice(hh * 512, (hh + 1) * 512)
                        for kc in range(16):
                            mm = ctx.te.matmul(P["A"][:, ts],
                                               wbuf.slot(ih2)[:, kc * 128:(kc + 1) * 128],
                                               srcf(16 + kc)[:, ts], start=False, stop=(kc == 15))
                    te_mm = ctx.step(mm, "te")
                    te_free = te_mm
                    ctx.wait(ctx.se, "te", te_mm)
                    sev = ctx.step(ctx.se.activation(dstf(oc), P["A"][:], AF.Gelu,
                                                     scale=scol[:, oc:oc + 1]), "se")
                    ctx.wait(ctx.te, "se", sev)

            s4_ve = {}
            for oc in range(8):
                iw, vw = wbuf.load(A[o + "w3A"][oc][:, 0:2048], free_after=te_free)
                iw2, vw2 = wbuf.load(A[o + "w3A"][oc][:, 2048:4096], free_after=None)
                ctx.wait(ctx.te, "wq", vw)
                for hh in range(2):
                    ts = slice(hh * 512, (hh + 1) * 512)
                    for kc in range(16):
                        mm = ctx.te.matmul(P["B"][:, ts],
                                           wbuf.slot(iw)[:, kc * 128:(kc + 1) * 128],
                                           a_chunk(kc)[:, ts], start=(kc == 0), stop=False)
                ctx.step(mm, "te")
                ctx.wait(ctx.te, "wq", vw2)
                for hh in range(2):
                    ts = slice(hh * 512, (hh + 1) * 512)
                    for kc in range(16):
                        mm = ctx.te.matmul(P["B"][:, ts],
                                           wbuf.slot(iw2)[:, kc * 128:(kc + 1) * 128],
                                           a_chunk(16 + kc)[:, ts], start=False, stop=(kc == 15))
                te_mm = ctx.step(mm, "te")
                te_free = te_mm
                ctx.wait(ctx.se, "te", te_mm)
                sev = ctx.step(ctx.se.activation(jnk, P["B"][:], AF.Identity,
                                                 scale=s3c[:, oc:oc + 1]), "se")
                ctx.wait(ctx.te, "se", sev)
                bb = oc % 2
                if oc >= 2:
                    ctx.wait(ctx.te, "ve", s4_ve[oc - 2])
                for tb in range(8):
                    tr = ctx.te.transpose(
                        P["T"][:, bb * 1024 + tb * 128: bb * 1024 + (tb + 1) * 128],
                        jnk[:, tb * 128:(tb + 1) * 128], S["CID"][:])
                s4_te = ctx.step(tr, "te")
                ctx.wait(ctx.ve, "te", s4_te)
                for tb in range(8):
                    tt = q * 8 + tb
                    xsl = X[:, tt * 1024 + oc * 128: tt * 1024 + (oc + 1) * 128]
                    ad = ctx.ve.tensor_tensor(
                        xsl, xsl,
                        P["T"][:, bb * 1024 + tb * 128: bb * 1024 + (tb + 1) * 128],
                        AL.add)
                s4_ve[oc] = ctx.step(ad, "ve")
                ctx.wait(ctx.se, "ve", s4_ve[oc])
        barrier(ctx, engs=("te", "se", "gp", "sy"), sems=("ve",))

    return emit



def emit_attn_l2(ctx, A, S, P):
    X, AR, AFX = S["X"], S["AR"], S["AF"]
    XN = AR[:, 0:32 * 1024]
    kt_off = 32 * 1024
    vt_off = kt_off + 8192
    q_off = vt_off + 8320
    y_off = q_off + 1024
    p_off = y_off + 8192
    r_off = p_off + 2048
    wb_off = r_off + 1024
    rb_off = wb_off + 2048
    xb_off = rb_off + 512

    KT = AR[:, kt_off:kt_off + 8192]
    QT = AR[:, q_off:q_off + 512]
    QTs = AR[:, q_off + 512:q_off + 1024]
    YT = AR[0:64, y_off:y_off + 8192]
    PT2 = AR[:, p_off:p_off + 2048]
    RBC = AR[0:64, rb_off:rb_off + 512]
    wbuf = WStream(ctx, AR[:, wb_off:wb_off + 2048], 1024)

    kbc = AFX[:, 160:162]
    qbc = AFX[:, 162:170]
    qbcs = AFX[:, 146:154]
    km2 = AFX[:, 170:172]
    qmaxf = AFX[:, 172:173]
    negb = AFX[:, 184:186]
    recf = AFX[64:65, 192:192 + 512]
    bbf = AFX[0:1, 704:706]
    ctx.dma("sy", kbc, A["l2_bkc"], "xq")
    ctx.dma("sy", qbc, A["l2_bqc"], "xq")
    ctx.dma("sy", qbcs, A["l2_bqcs"], "xq")
    bvr = AR[0:1, r_off:r_off + 256]
    rrow = AR[64:65, r_off + 256:r_off + 768]
    km2b = AR[:, r_off + 768:r_off + 770]
    qmb = AR[:, r_off + 770:r_off + 771]
    br2 = AR[0:1, r_off + 772:r_off + 774]
    qrow_t = AR[0:1, r_off + 774:r_off + 774 + 128]
    kr2 = AR[0:1, xb_off:xb_off + 256]
    oneb = AR[:, xb_off + 256:xb_off + 256 + 64]
    bor = AR[0:1, xb_off + 512:xb_off + 512 + 1024]
    vv = ctx.step(ctx.ve.memset(oneb, 1.0), "ve")
    ctx.dma("gp", bvr, A["l2_bvr"], "wq")
    ctx.dma("gp", bor, A["l2_bor"], "wq")
    ctx.wait(ctx.se, "xq")

    te_free = None
    for oc in range(2):
        iw, vw = wbuf.load(A["l2_kA"][oc], free_after=te_free)
        ctx.wait(ctx.te, "wq", vw)
        for tt in range(8):
            ts = slice(tt * 512, (tt + 1) * 512)
            ps = P["C"] if tt % 2 == 0 else P["D"]
            if tt >= 2:
                ctx.wait(ctx.te, "se", ctx.cnt["se"] - 1)
            for kc in range(8):
                mm = ctx.te.matmul(ps[:], wbuf.slot(iw)[:, kc * 128:(kc + 1) * 128],
                                   XN[:, kc * 4096 + ts.start: kc * 4096 + ts.stop],
                                   start=(kc == 0), stop=(kc == 7))
            te_mm = ctx.step(mm, "te")
            te_free = te_mm
            ctx.wait(ctx.se, "te", te_mm)
            ctx.step(ctx.se.activation(KT[:, oc * 4096 + ts.start: oc * 4096 + ts.stop],
                                       ps[:], AF.Identity, bias=kbc[:, oc:oc + 1]), "se")
    ctx.wait(ctx.ve, "se")
    for oc in range(2):
        chain(ctx, ctx.ve.reduce_max(km2[:, oc:oc + 1], KT[:, oc * 4096:(oc + 1) * 4096],
                                     axis=AX.X, apply_absolute_value=True), "ve", "ve")
    vv = ctx.step(ctx.ve.tensor_copy(km2b, km2), "ve")
    ctx.wait(ctx.ve, "ve", vv)
    ctx.wait(ctx.te, "ve", vv)
    for oc in range(2):
        tev = ctx.step(ctx.te.matmul(P["C"][0:1, 0:128], km2b[:, oc:oc + 1],
                                     S["CID"][:], start=True, stop=True), "te")
        ctx.wait(ctx.ve, "te", tev)
        vv = ctx.step(ctx.ve.tensor_copy(kr2[:, oc * 128:oc * 128 + 128],
                                         P["C"][0:1, 0:128]), "ve")
        ctx.wait(ctx.te, "ve", vv)

    ov = AR[:, vt_off:vt_off + 32 * 260].rearrange("p (t g u) -> p t g u", g=4, u=65)
    vev = ctx.step(ctx.ve.memset(ov[:, :, :, 64:65], 1.0), "ve")
    vsrc = A["l2_v"].rearrange("(c p) d -> p c d", p=128)
    vwbuf = PT2
    vw = ctx.dma("gp", vwbuf.rearrange("p (c d) -> p c d", d=256), vsrc, "wq")
    ctx.wait(ctx.te, "wq", vw)
    ctx.wait(ctx.te, "ve", vev)
    for tt2 in range(16):
        ps = P["C"] if tt2 % 2 == 0 else P["D"]
        if tt2 >= 2:
            ctx.wait(ctx.te, "se", ctx.cnt["se"] - 2)
        for half in range(2):
            tt = tt2 * 2 + half
            hs = slice(half * 256, (half + 1) * 256)
            for kc in range(8):
                mm = ctx.te.matmul(ps[:, hs],
                                   XN[:, kc * 4096 + tt * 128: kc * 4096 + (tt + 1) * 128],
                                   vwbuf[:, kc * 256:(kc + 1) * 256],
                                   start=(kc == 0), stop=False)
            mm = ctx.te.matmul(ps[:, hs], S["ONE"][:], bvr, start=False, stop=True)
        te_mm = ctx.step(mm, "te")
        te_free = te_mm
        ctx.wait(ctx.se, "te", te_mm)
        for half in range(2):
            tt = tt2 * 2 + half
            hs = slice(half * 256, (half + 1) * 256)
            dst = AR[:, vt_off + tt * 260: vt_off + (tt + 1) * 260] \
                .rearrange("p (g u) -> p g u", u=65)[:, :, 0:64]
            srcv = ps[:, hs].rearrange("p (g u) -> p g u", u=64)
            ctx.step(ctx.se.activation(dst, srcv, AF.Copy), "se")
    barrier(ctx, engs=("te", "ve", "se"), sems=("ve", "se", "te"))
    if DBG:
        ctx.wait(ctx.sy, "se")
        ctx.wait(ctx.sy, "ve")
        ctx.dma("sy", A["dbg_xn"], AR[:, 0:32768], "xq")
        ctx.dma("sy", A["dbg_kt"], KT, "xq")

    for qq in range(8):
        qs = slice(qq * 512, (qq + 1) * 512)
        for pr in range(8):
            g = pr // 2
            gb = 64 * (g % 2)
            iw, vw = wbuf.load(A["l2_qA"][pr], free_after=te_free)
            iws, vws = wbuf.load(A["l2_qAs"][pr], free_after=te_free)
            ctx.wait(ctx.te, "wq", vws)
            ctx.wait(ctx.te, "ve")
            for sw, (qt, bc) in enumerate(((QT, qbc), (QTs, qbcs))):
                for kc in range(8):
                    mm = ctx.te.matmul(P["D"][:],
                                       wbuf.slot(iw if sw == 0 else iws)[:, kc * 128:(kc + 1) * 128],
                                       XN[:, kc * 4096 + qs.start: kc * 4096 + qs.stop],
                                       start=(kc == 0), stop=(kc == 7))
                te_mm = ctx.step(mm, "te")
                te_free = te_mm
                ctx.wait(ctx.se, "te", te_mm)
                sev = ctx.step(ctx.se.activation(qt, P["D"][:], AF.Identity,
                                                 bias=bc[:, pr:pr + 1]), "se")
                if sw == 0:
                    ctx.wait(ctx.te, "se", sev)
            ctx.wait(ctx.ve, "se", sev)
            chain(ctx, ctx.ve.reduce_max(qmaxf, QT, axis=AX.X,
                                         apply_absolute_value=True), "ve", "ve")
            vv = ctx.step(ctx.ve.tensor_copy(qmb, qmaxf), "ve")
            ctx.wait(ctx.ve, "ve", vv)
            ctx.wait(ctx.te, "ve", vv)
            tev = ctx.step(ctx.te.matmul(P["C"][0:1, 128:256], qmb, S["CID"][:],
                                         start=True, stop=True), "te")
            ctx.wait(ctx.ve, "te", tev)
            chain(ctx, ctx.ve.tensor_copy(qrow_t, P["C"][0:1, 128:256]), "ve", "ve")
            for hh in range(2):
                chain(ctx, ctx.ve.tensor_tensor(
                    RBC[0:1, hh * 64:hh * 64 + 64],
                    qrow_t[:, hh * 64:hh * 64 + 64],
                    kr2[:, (g // 2) * 128 + gb: (g // 2) * 128 + gb + 64], AL.mult), "ve", "ve")
                chain(ctx, ctx.ve.reduce_sum(bbf[:, hh:hh + 1],
                                             RBC[0:1, hh * 64:hh * 64 + 64],
                                             axis=AX.X), "ve", "ve")
            vv = ctx.step(ctx.ve.tensor_scalar(br2, bbf, -1.0 / 16.0, None, AL.mult), "ve")
            ctx.wait(ctx.ve, "ve", vv)
            ctx.wait(ctx.te, "ve", vv)
            mm = ctx.te.matmul(P["C"][:, 0:2], S["ONE"][:], br2, start=True, stop=True)
            te_mm = ctx.step(mm, "te")
            ctx.wait(ctx.ve, "te", te_mm)
            vv_negb = ctx.step(ctx.ve.tensor_copy(negb, P["C"][:, 0:2]), "ve")
            if DBG and qq == 0:
                dv = S["DBG"]
                if pr == 0:
                    ctx.step(ctx.ve.tensor_copy(dv[:, 440:450],
                                                AR[:, 0:10]), "ve")
                    ctx.step(ctx.ve.tensor_copy(dv[:, 450:460],
                                                S["STAT"][:, 64:74]), "ve")
                    ctx.step(ctx.ve.tensor_copy(dv[:, 460:470],
                                                S["X"][:, 0:10]), "ve")
                ctx.step(ctx.ve.tensor_copy(dv[:, pr * 2:pr * 2 + 2], negb), "ve")
                ctx.step(ctx.ve.tensor_copy(dv[0:1, 16 + pr * 2:16 + pr * 2 + 2], bbf), "ve")
                ctx.step(ctx.ve.tensor_copy(dv[:, 32 + pr:33 + pr], qmaxf), "ve")
                if pr == 0:
                    ctx.step(ctx.ve.tensor_copy(dv[0:1, 40:40 + 256], kr2), "ve")
                    ctx.step(ctx.ve.tensor_copy(dv[:, 296:298], km2), "ve")
                    ctx.step(ctx.ve.tensor_copy(dv[0:1, 300:428], qrow_t), "ve")
            for hh in range(2):
                h = 2 * pr + hh
                qt = QT if (h % 2) == (g % 2) else QTs
                qrow = qt[gb:gb + 64, :]
                ysl = YT[:, h * 512:(h + 1) * 512]
                pv_done = [None, None]
                exp_done = None
                for kc2 in range(16):
                    sl = kc2 % 2
                    pslot = PT2[:, sl * 1024:(sl + 1) * 1024]
                    if exp_done is not None:
                        ctx.wait(ctx.te, "se", exp_done)
                    for j in range(2):
                        kc = kc2 * 2 + j
                        mm = ctx.te.matmul(
                            P["A"][:, j * 512:(j + 1) * 512],
                            KT[gb:gb + 64,
                               (g // 2) * 4096 + kc * 128: (g // 2) * 4096 + (kc + 1) * 128],
                            qrow, start=True, stop=True)
                    te_mm = ctx.step(mm, "te")
                    ctx.wait(ctx.se, "te", te_mm)
                    if pv_done[sl] is not None:
                        ctx.wait(ctx.se, "te", pv_done[sl])
                    if kc2 == 0:
                        ctx.wait(ctx.se, "ve", vv_negb)
                    exp_done = ctx.step(ctx.se.activation(pslot, P["A"][:], AF.Exp,
                                                          scale=0.125,
                                                          bias=negb[:, hh:hh + 1]), "se")
                    ctx.wait(ctx.te, "se", exp_done)
                    for j in range(2):
                        kc = kc2 * 2 + j
                        mm = ctx.te.matmul(
                            P["D"][0:65, :],
                            AR[:, vt_off + kc * 260 + g * 65: vt_off + kc * 260 + g * 65 + 65],
                            pslot[:, j * 512:(j + 1) * 512],
                            start=(kc == 0), stop=(kc == 31))
                    te_mm = ctx.step(mm, "te")
                    pv_done[sl] = te_mm
                ctx.wait(ctx.ve, "te", te_mm)
                chain(ctx, ctx.ve.reciprocal(recf, P["D"][64:65, :]), "ve", "ve")
                vv = ctx.step(ctx.ve.tensor_copy(rrow, recf), "ve")
                ctx.wait(ctx.ve, "ve", vv)
                ctx.wait(ctx.te, "ve", vv)
                mm = ctx.te.matmul(P["C"][0:64, :], oneb[64:65, :], rrow,
                                   start=True, stop=True)
                te_mm = ctx.step(mm, "te")
                ctx.wait(ctx.se, "te", te_mm)
                sev = ctx.step(ctx.se.activation(RBC, P["C"][0:64, :], AF.Copy), "se")
                ctx.wait(ctx.ve, "se", sev)
                vv = ctx.step(ctx.ve.tensor_tensor(ysl, P["D"][0:64, :], RBC,
                                                   AL.mult), "ve")
                ctx.wait(ctx.te, "ve", vv)
        ctx.wait(ctx.te, "ve")
        wo_r = A["l2_wo"].rearrange("(c p) d -> c p d", p=64)
        for t2 in range(2):
            psl = [(P["A"], 0), (P["A"], 512), (P["B"], 0), (P["B"], 512)]
            for yc in range(16):
                if te_free is not None:
                    ctx.wait(ctx.gp, "te", te_free)
                wsl = wbuf.slot(yc)[0:64, :]
                vw = ctx.dma("gp", wsl, wo_r[yc], "wq")
                ctx.wait(ctx.te, "wq", vw)
                for u in range(2):
                    t4 = t2 * 2 + u
                    for dh in range(2):
                        ps, off = psl[u * 2 + dh]
                        mm = ctx.te.matmul(
                            ps[:, off:off + 512],
                            YT[:, yc * 512 + t4 * 128: yc * 512 + (t4 + 1) * 128],
                            wsl[:, dh * 512:(dh + 1) * 512],
                            start=(yc == 0), stop=False)
                te_mm = ctx.step(mm, "te")
                te_free = te_mm
            for u in range(2):
                t4 = t2 * 2 + u
                tt = qq * 4 + t4
                for dh in range(2):
                    ps, off = psl[u * 2 + dh]
                    mm = ctx.te.matmul(ps[:, off:off + 512], S["ONE"][:],
                                       bor[:, dh * 512:(dh + 1) * 512],
                                       start=False, stop=True)
                    te_mm = ctx.step(mm, "te")
                    ctx.wait(ctx.ve, "te", te_mm)
                    xsl = X[:, tt * 1024 + dh * 512: tt * 1024 + (dh + 1) * 512]
                    ctx.step(ctx.ve.tensor_tensor(xsl, xsl, ps[:, off:off + 512],
                                                  AL.add), "ve")
            ctx.wait(ctx.te, "ve")
    barrier(ctx, engs=("te", "se", "gp", "sy"), sems=("ve",))


def emit_attn_l3(ctx, A, S, P):
    X, AR, AFX = S["X"], S["AR"], S["AF"]
    XN = AR[:, 0:32 * 1024]
    csb_off = 32 * 1024
    qsb_off = csb_off + 8192
    t1_off = qsb_off + 1024
    ro_off = t1_off + 1024
    ktm_off = ro_off + 1024
    ksq_off = ktm_off + 1024
    gsb_off = ksq_off + 1024
    vs_off = gsb_off + 1024
    wb_off = vs_off + 2080
    kb_off = csb_off
    vb_off = kb_off + 4096
    qb_off = vb_off + 2080
    ub_off = qb_off + 512
    yt_off = ub_off + 1024
    pt_off = yt_off + 8192
    r2_off = pt_off + 2048

    wbuf = WStream(ctx, AR[:, wb_off:wb_off + 2048], 1024)
    CSB = AR[:, csb_off:csb_off + 8192]
    PSW = S["PSW"]

    recf = AFX[64:65, 192:192 + 512]
    trh = AFX[0:1, 140:156]
    mkh = AFX[0:1, 156:172]
    sc1 = AFX[0:1, 172:174]
    km_col = AFX[:, 176:177]

    QSB = AR[:, qsb_off:qsb_off + 1024]
    T1 = AR[:, t1_off:t1_off + 1024]
    RO = AR[:, ro_off:ro_off + 1024]
    KTM = AR[:, ktm_off:ktm_off + 1024]
    KSQ = AR[:, ksq_off:ksq_off + 1024]
    GSB = AR[0:64, gsb_off:gsb_off + 1024]
    onec = AR[:, r2_off + 3072:r2_off + 3073]
    vv = ctx.step(ctx.ve.memset(onec, 1.0), "ve")
    vv0 = ctx.step(ctx.ve.memset(trh, 0.0), "ve")
    vv0 = ctx.step(ctx.ve.memset(mkh, 0.0), "ve")
    ctx.wait(ctx.ve, "ve", vv0)
    KSQS = AFX[0:1, 180:181]
    qsb_used = {}
    ro_used = {}
    te_free = None
    for sel, wname in ((0, "l3_qA"), (1, "l3_kA")):
        for oc in range(8):
            iw, vw = wbuf.load(A[wname][oc], free_after=te_free)
            ctx.wait(ctx.gp, "ve")
            cw = ctx.dma("gp", CSB[:, 0:4096], A["l3_cos"][oc * 128:(oc + 1) * 128, :], "wq")
            cw = ctx.dma("gp", CSB[:, 4096:8192], A["l3_sin2"][oc * 128:(oc + 1) * 128, :], "wq")
            ctx.wait(ctx.te, "wq", cw)
            ctx.wait(ctx.ve, "wq", cw)
            for tt in range(8):
                b = tt % 2
                ts = slice(tt * 512, (tt + 1) * 512)
                psm = P["A"][:, b * 512:(b + 1) * 512]
                psw_out = P["B"][:, b * 512:(b + 1) * 512]
                qsb = QSB[:, b * 512:(b + 1) * 512]
                t1 = T1[:, b * 512:(b + 1) * 512]
                ro = RO[:, b * 512:(b + 1) * 512]
                if qsb_used.get(b) is not None:
                    ctx.wait(ctx.se, "ve", qsb_used[b])
                for kc in range(8):
                    mm = ctx.te.matmul(psm, wbuf.slot(iw)[:, kc * 128:(kc + 1) * 128],
                                       XN[:, kc * 4096 + ts.start:kc * 4096 + ts.stop],
                                       start=(kc == 0), stop=(kc == 7))
                te_mm = ctx.step(mm, "te")
                te_free = te_mm
                ctx.wait(ctx.se, "te", te_mm)
                sev = ctx.step(ctx.se.activation(qsb, psm, AF.Copy), "se")
                ctx.wait(ctx.te, "se", sev)
                mm = ctx.te.matmul(psw_out, PSW[:], qsb, start=True, stop=True)
                te_sw = ctx.step(mm, "te")
                ctx.wait(ctx.ve, "se", sev)
                if ro_used.get(b) is not None:
                    ctx.wait(ctx.ve, "xq", ro_used[b])
                qsb_used[b] = chain(ctx, ctx.ve.tensor_tensor(t1, qsb, CSB[:, ts], AL.mult), "ve", "ve")
                ctx.wait(ctx.ve, "te", te_sw)
                chain(ctx, ctx.ve.tensor_tensor(ro, psw_out,
                                                CSB[:, 4096 + ts.start:4096 + ts.stop],
                                                AL.mult), "ve", "ve")
                vv = ctx.step(ctx.ve.tensor_tensor(ro, ro, t1, AL.add), "ve")
                ctx.wait(ctx.ve, "ve", vv)
                ctx.wait(ctx.te, "ve", vv)
                ctx.wait(ctx.sy, "ve", vv)
                dst = A["l3qk"][sel, :, oc * 4096 + ts.start: oc * 4096 + ts.stop]
                ro_used[b] = ctx.dma("sy", dst, ro, "xq")
                if sel == 1:
                    ktm = KTM[:, b * 512:(b + 1) * 512]
                    for blk in range(4):
                        tr = ctx.te.transpose(
                            P["T"][:, b * 1024 + blk * 128: b * 1024 + (blk + 1) * 128],
                            ro[:, blk * 128:(blk + 1) * 128], S["CID"][:])
                    tr_v = ctx.step(tr, "te")
                    ctx.wait(ctx.ve, "te", tr_v)
                    for blk in range(4):
                        cp = ctx.ve.tensor_copy(
                            ktm[:, blk * 128:(blk + 1) * 128],
                            P["T"][:, b * 1024 + blk * 128: b * 1024 + (blk + 1) * 128])
                    ktm_v = ctx.step(cp, "ve")
                    ctx.wait(ctx.te, "ve", ktm_v)
                    for blk in range(4):
                        mm = ctx.te.matmul(P["C"][0:64, 0:64],
                                           ktm[:, blk * 128:blk * 128 + 64],
                                           ktm[:, blk * 128:blk * 128 + 64],
                                           start=(tt == 0 and blk == 0), stop=(tt == 7 and blk == 3))
                        mm = ctx.te.matmul(P["D"][0:64, 0:64],
                                           ktm[:, blk * 128 + 64:blk * 128 + 128],
                                           ktm[:, blk * 128 + 64:blk * 128 + 128],
                                           start=(tt == 0 and blk == 0), stop=(tt == 7 and blk == 3))
                    gm_v = ctx.step(mm, "te")
                    ksq = KSQ[:, b * 512:(b + 1) * 512]
                    sev = ctx.step(ctx.se.activation(ksq, ro, AF.Square), "se")
                    ctx.wait(ctx.te, "se", sev)
                    for hh in range(2):
                        mm = ctx.te.matmul(psm[0:1, 0:512],
                                           onec[64 * hh:64 * hh + 64, :],
                                           ksq[64 * hh:64 * hh + 64, :],
                                           start=True, stop=True)
                        knv = ctx.step(mm, "te")
                        ctx.wait(ctx.ve, "te", knv)
                        h = oc * 2 + hh
                        chain(ctx, ctx.ve.reduce_max(sc1[:, 1:2], psm[0:1, 0:512],
                                                     axis=AX.X), "ve", "ve")
                        vv = ctx.step(ctx.ve.tensor_tensor(mkh[:, h:h + 1], mkh[:, h:h + 1],
                                                           sc1[:, 1:2], AL.max), "ve")
                        ctx.wait(ctx.ve, "ve", vv)
                        ctx.wait(ctx.te, "ve", vv)
            if sel == 1:
                ctx.wait(ctx.se, "te", gm_v)
                sev = ctx.step(ctx.se.activation(GSB[:, (2 * oc) * 64:(2 * oc) * 64 + 64],
                                                 P["C"][0:64, 0:64], AF.Copy), "se")
                sev = ctx.step(ctx.se.activation(GSB[:, (2 * oc + 1) * 64:(2 * oc + 1) * 64 + 64],
                                                 P["D"][0:64, 0:64], AF.Copy), "se")
                ctx.wait(ctx.te, "se", sev)
    VW = CSB
    ctx.wait(ctx.gp, "ve")
    vw = ctx.dma("gp", VW.rearrange("p (c d) -> p c d", d=1024),
                 A["l3_v"].rearrange("(c p) d -> p c d", p=128), "wq")
    VS = AR[:, vs_off:vs_off + 2080]
    ov = VS.rearrange("p (b g u) -> p b g u", b=2, u=65)
    vv_one = ctx.step(ctx.ve.memset(ov[:, :, :, 64:65], 1.0), "ve")
    ctx.wait(ctx.te, "wq", vw)
    vdma = {}
    for tt in range(32):
        b = tt % 2
        psm = P["A"] if b == 0 else P["B"]
        if tt >= 2:
            ctx.wait(ctx.te, "se", ctx.cnt["se"] - 1)
        for dh in range(2):
            hs = slice(dh * 512, (dh + 1) * 512)
            for kc in range(8):
                mm = ctx.te.matmul(psm[:, hs],
                                   XN[:, kc * 4096 + tt * 128: kc * 4096 + (tt + 1) * 128],
                                   VW[:, kc * 1024 + hs.start: kc * 1024 + hs.stop],
                                   start=(kc == 0), stop=(kc == 7))
        te_mm = ctx.step(mm, "te")
        ctx.wait(ctx.se, "te", te_mm)
        if tt >= 2:
            ctx.wait(ctx.se, "xq", vdma[tt - 2])
        if tt == 0:
            ctx.wait(ctx.se, "ve", vv_one)
        dstv = VS[:, b * 1040:(b + 1) * 1040].rearrange("p (g u) -> p g u", u=65)[:, :, 0:64]
        sev = ctx.step(ctx.se.activation(dstv, psm[:].rearrange("p (g u) -> p g u", u=64),
                                         AF.Copy), "se")
        ctx.wait(ctx.sy, "se", sev)
        for h in range(16):
            last = ctx.dma("sy", A["l3v"][h, tt, :, :],
                           VS[:, b * 1040 + h * 65: b * 1040 + (h + 1) * 65], "xq")
        vdma[tt] = last
    barrier(ctx, engs=("te", "ve", "se", "gp", "sy"), sems=("ve", "se", "te", "xq", "wq"))

    KB = AR[:, kb_off:kb_off + 4096]
    VB = AR[:, vb_off:vb_off + 2080]
    QB = AR[:, qb_off:qb_off + 512]
    UB = AR[0:64, ub_off:ub_off + 512]
    UB2 = AR[0:64, ub_off + 512:ub_off + 1024]
    YT = AR[0:64, yt_off:yt_off + 8192]
    PT2 = AR[:, pt_off:pt_off + 2048]
    RBC = AR[0:64, r2_off:r2_off + 512]
    rrow = AR[64:65, r2_off + 512:r2_off + 1024]
    bur = AR[0:1, r2_off + 1024:r2_off + 2048]
    kmb = AR[:, r2_off + 2048:r2_off + 2049]
    oneb = AR[:, r2_off + 2052:r2_off + 2052 + 64]
    R_MU = AR[0:1, r2_off + 3200:r2_off + 3712]
    R_ES = AR[0:1, r2_off + 3712:r2_off + 4224]
    R_M2 = AR[0:1, r2_off + 4224:r2_off + 4736]
    R_SG = AR[0:1, r2_off + 4736:r2_off + 5248]
    R_U = AR[0:1, r2_off + 5248:r2_off + 5760]
    R_C = AR[0:1, r2_off + 5760:r2_off + 6272]
    vv = ctx.step(ctx.ve.memset(oneb, 1.0), "ve")
    vv = ctx.step(ctx.ve.memset(AR[64:65, kb_off:kb_off + 4096], 1.0), "ve")
    ctx.dma("gp", bur, A["l3_bur"], "wq")
    barrier(ctx, engs=("ve",), sems=("ve",))

    for qq in range(8):
        for h in range(16):
            ctx.wait(ctx.sy, "te")
            ctx.wait(ctx.sy, "ve")
            ctx.wait(ctx.sy, "se")
            kv1 = ctx.dma("sy", KB[0:64, :], A["l3qk"][1, 64 * (h % 2):64 * (h % 2) + 64,
                                                       (h // 2) * 4096:(h // 2 + 1) * 4096], "xq")
            kv2 = ctx.dma("sy", VB.rearrange("p (c u) -> p c u", u=65),
                          A["l3v"][h].rearrange("c p u -> p c u"), "xq")
            kv3 = ctx.dma("sy", QB[0:64, :],
                          A["l3qk"][0, 64 * (h % 2):64 * (h % 2) + 64,
                                    (h // 2) * 4096 + qq * 512: (h // 2) * 4096 + (qq + 1) * 512], "xq")
            ctx.wait(ctx.ve, "xq", kv3)
            chain(ctx, ctx.ve.reduce_sum(km_col[0:64, :], KB[0:64, :], axis=AX.X), "ve", "ve")
            chain(ctx, ctx.ve.tensor_scalar(kmb[0:64, :], km_col[0:64, :],
                                            1.0 / 4096.0, None, AL.mult), "ve", "ve")
            ctx.wait(ctx.te, "ve")
            mm = ctx.te.matmul(P["C"][0:1, 0:512], kmb[0:64, :], QB[0:64, :],
                               start=True, stop=True)
            mu_v = ctx.step(mm, "te")
            mm = ctx.te.matmul(P["D"][0:64, :], GSB[:, h * 64:(h + 1) * 64],
                               QB[0:64, :], start=True, stop=True)
            gq_v = ctx.step(mm, "te")
            ctx.wait(ctx.ve, "te", gq_v)
            chain(ctx, ctx.ve.tensor_tensor(UB, QB[0:64, :], P["D"][0:64, :], AL.mult),
                  "ve", "ve")
            chain(ctx, ctx.ve.tensor_tensor(UB2, QB[0:64, :], QB[0:64, :], AL.mult),
                  "ve", "ve")
            ctx.wait(ctx.te, "ve")
            mm = ctx.te.matmul(P["A"][0:1, 0:512], onec[0:64, :], UB,
                               start=True, stop=True)
            ss_v = ctx.step(mm, "te")
            mm = ctx.te.matmul(P["A"][0:1, 512:1024], onec[0:64, :], UB2,
                               start=True, stop=True)
            qn_v = ctx.step(mm, "te")
            ctx.wait(ctx.ve, "te", qn_v)
            chain(ctx, ctx.ve.tensor_scalar(R_MU, P["C"][0:1, 0:512], 0.125, None,
                                            AL.mult), "ve", "ve")
            chain(ctx, ctx.ve.tensor_scalar(R_ES, P["A"][0:1, 0:512],
                                            1.0 / (4096.0 * 64.0), None,
                                            AL.mult), "ve", "ve")
            chain(ctx, ctx.ve.tensor_tensor(R_M2, R_MU, R_MU, AL.mult), "ve", "ve")
            chain(ctx, ctx.ve.tensor_tensor(R_SG, R_ES, R_M2, AL.subtract), "ve", "ve")
            chain(ctx, ctx.ve.tensor_scalar(R_SG, R_SG, 1e-3, None, AL.max), "ve", "ve")
            ctx.wait(ctx.se, "ve")
            chain(ctx, ctx.se.activation(R_SG, R_SG, AF.Ln), "se", "se")
            chain(ctx, ctx.se.activation(R_SG, R_SG, AF.Exp, scale=0.5), "se", "se")
            ctx.wait(ctx.ve, "se")
            chain(ctx, ctx.ve.tensor_scalar(R_U, P["A"][0:1, 512:1024],
                                            mkh[:, h:h + 1], None, AL.mult), "ve", "ve")
            chain(ctx, ctx.ve.tensor_scalar(R_U, R_U, 1e-12, None, AL.max), "ve", "ve")
            ctx.wait(ctx.se, "ve")
            chain(ctx, ctx.se.activation(R_U, R_U, AF.Ln), "se", "se")
            chain(ctx, ctx.se.activation(R_U, R_U, AF.Exp, scale=0.5), "se", "se")
            ctx.wait(ctx.ve, "se")
            chain(ctx, ctx.ve.tensor_scalar(R_U, R_U, 0.125, -70.0, AL.mult, AL.add),
                  "ve", "ve")
            chain(ctx, ctx.ve.scalar_tensor_tensor(R_C, R_SG, 0.875, R_MU,
                                                   AL.mult, AL.add), "ve", "ve")
            chain(ctx, ctx.ve.tensor_tensor(R_C, R_C, R_U, AL.max), "ve", "ve")
            vv = ctx.step(ctx.ve.tensor_scalar(QB[64:65, :], R_C, -8.0, None, AL.mult), "ve")
            ctx.wait(ctx.ve, "ve", vv)
            ctx.wait(ctx.te, "ve", vv)
            ctx.wait(ctx.te, "xq", kv2)
            ysl = YT[:, h * 512:(h + 1) * 512]
            pv_done = [None, None]
            exp_done = None
            for kc2 in range(16):
                sl = kc2 % 2
                pslot = PT2[:, sl * 1024:(sl + 1) * 1024]
                if exp_done is not None:
                    ctx.wait(ctx.te, "se", exp_done)
                for j in range(2):
                    kc = kc2 * 2 + j
                    mm = ctx.te.matmul(
                        P["B"][:, j * 512:(j + 1) * 512],
                        KB[0:65, kc * 128:(kc + 1) * 128],
                        QB[0:65, :], start=True, stop=True)
                te_mm = ctx.step(mm, "te")
                ctx.wait(ctx.se, "te", te_mm)
                if pv_done[sl] is not None:
                    ctx.wait(ctx.se, "te", pv_done[sl])
                exp_done = ctx.step(ctx.se.activation(pslot, P["B"][:], AF.Exp,
                                                      scale=0.125), "se")
                ctx.wait(ctx.te, "se", exp_done)
                for j in range(2):
                    kc = kc2 * 2 + j
                    mm = ctx.te.matmul(
                        P["D"][0:65, :],
                        VB[:, kc * 65:(kc + 1) * 65],
                        pslot[:, j * 512:(j + 1) * 512],
                        start=(kc == 0), stop=(kc == 31))
                te_mm = ctx.step(mm, "te")
                pv_done[sl] = te_mm
            ctx.wait(ctx.ve, "te", te_mm)
            chain(ctx, ctx.ve.tensor_scalar(P["D"][64:65, :], P["D"][64:65, :], 1e-30,
                                            None, AL.add), "ve", "ve")
            chain(ctx, ctx.ve.reciprocal(recf, P["D"][64:65, :]), "ve", "ve")
            vv = ctx.step(ctx.ve.tensor_copy(rrow, recf), "ve")
            ctx.wait(ctx.ve, "ve", vv)
            ctx.wait(ctx.te, "ve", vv)
            mm = ctx.te.matmul(P["C"][0:64, :], oneb[64:65, :], rrow,
                               start=True, stop=True)
            te_mm = ctx.step(mm, "te")
            ctx.wait(ctx.se, "te", te_mm)
            sev = ctx.step(ctx.se.activation(RBC, P["C"][0:64, :], AF.Copy), "se")
            ctx.wait(ctx.ve, "se", sev)
            vv = ctx.step(ctx.ve.tensor_tensor(ysl, P["D"][0:64, :], RBC, AL.mult), "ve")
            ctx.wait(ctx.te, "ve", vv)
        ctx.wait(ctx.te, "ve")
        wu_r = A["l3_wu"].rearrange("(c p) d -> c p d", p=64)
        te_free = ctx.cnt["te"]
        for t2 in range(2):
            psl = [(P["A"], 0), (P["A"], 512), (P["B"], 0), (P["B"], 512)]
            for yc in range(16):
                ctx.wait(ctx.gp, "te", te_free)
                wsl = wbuf.slot(yc)[0:64, :]
                vw = ctx.dma("gp", wsl, wu_r[yc], "wq")
                ctx.wait(ctx.te, "wq", vw)
                for u in range(2):
                    t4 = t2 * 2 + u
                    for dh in range(2):
                        ps, off = psl[u * 2 + dh]
                        mm = ctx.te.matmul(
                            ps[:, off:off + 512],
                            YT[:, yc * 512 + t4 * 128: yc * 512 + (t4 + 1) * 128],
                            wsl[:, dh * 512:(dh + 1) * 512],
                            start=(yc == 0), stop=False)
                te_mm = ctx.step(mm, "te")
                te_free = te_mm
            for u in range(2):
                t4 = t2 * 2 + u
                tt = qq * 4 + t4
                for dh in range(2):
                    ps, off = psl[u * 2 + dh]
                    mm = ctx.te.matmul(ps[:, off:off + 512], S["ONE"][:],
                                       bur[:, dh * 512:(dh + 1) * 512],
                                       start=False, stop=True)
                    te_mm = ctx.step(mm, "te")
                    ctx.wait(ctx.ve, "te", te_mm)
                    xsl = X[:, tt * 1024 + dh * 512: tt * 1024 + (dh + 1) * 512]
                    ctx.step(ctx.ve.tensor_tensor(xsl, xsl, ps[:, off:off + 512],
                                                  AL.add), "ve")
            ctx.wait(ctx.te, "ve")
    barrier(ctx, engs=("te", "se", "gp", "sy"), sems=("ve",))


SELU_LA = SELU_L * SELU_A


def emit_attn_l1(ctx, A, S, P):
    X, AR, AFX = S["X"], S["AR"], S["AF"]
    XN = AR[:, 0:32 * 1024]
    kb_off = 32 * 1024
    qb_off = kb_off + 4096
    vb_off = qb_off + 4096
    es_off = vb_off + 2048
    rs_off = es_off + 2048
    tp_off = rs_off + 2048
    wb_off = tp_off + 2048
    r2_off = wb_off + 1024

    wbuf = WStream(ctx, AR[:, wb_off:wb_off + 1024], 512)
    KB = AR[:, kb_off:kb_off + 4096]
    QB = AR[:, qb_off:qb_off + 4096]
    VB = AR[:, vb_off:vb_off + 2048]
    EB = AR[:, es_off:es_off + 2048]
    RB = AR[:, rs_off:rs_off + 2048]
    TB = AR[:, tp_off:tp_off + 2048]
    kmb = AR[:, r2_off:r2_off + 1]
    corr = AFX[:, 176:177]
    vsum = AFX[:, 177:178]
    qbc = AFX[:, 178:194]
    kbc = AFX[:, 194:210]
    km_col = AFX[:, 210:211]
    ctx.dma("sy", qbc, A["l1_bqc"], "xq")
    ctx.dma("sy", kbc, A["l1_bkc"], "xq")
    vv = ctx.step(ctx.ve.memset(AR[64:65, kb_off:kb_off + 4096], 1.0), "ve")
    ctx.wait(ctx.se, "xq")
    barrier(ctx, engs=("ve", "te"), sems=("ve",))

    te_free = None
    for h in range(16):
        hb = 64 * (h % 2)
        for sel, wname, dst, bc in ((1, "l1_kA", KB, kbc), (0, "l1_qA", QB, qbc)):
            iw, vw = wbuf.load(A[wname][h], free_after=te_free)
            ctx.wait(ctx.te, "wq", vw)
            for tt in range(8):
                ts = slice(tt * 512, (tt + 1) * 512)
                psm = P["A"][:, (tt % 2) * 512:(tt % 2) * 512 + 512]
                if tt >= 2:
                    ctx.wait(ctx.te, "se", ctx.cnt["se"] - 1)
                for kc in range(8):
                    mm = ctx.te.matmul(psm[0:64, :], wbuf.slot(iw)[:, kc * 64:(kc + 1) * 64],
                                       XN[:, kc * 4096 + ts.start:kc * 4096 + ts.stop],
                                       start=(kc == 0), stop=(kc == 7))
                te_mm = ctx.step(mm, "te")
                te_free = te_mm
                ctx.wait(ctx.se, "te", te_mm)
                sev = ctx.step(ctx.se.activation(dst[0:64, ts], psm[0:64, :], AF.Identity,
                                                 bias=bc[0:64, h:h + 1]), "se")
        ctx.wait(ctx.ve, "se")
        chain(ctx, ctx.ve.reduce_sum(km_col[0:64, :], KB[0:64, :], axis=AX.X), "ve", "ve")
        vv = ctx.step(ctx.ve.tensor_copy(kmb[0:64, :], km_col[0:64, :]), "ve")
        ctx.wait(ctx.ve, "ve", vv)
        chain(ctx, ctx.ve.reduce_sum(vsum[0:64, :],
                                     XN[hb:hb + 64, (h // 2) * 4096:(h // 2 + 1) * 4096],
                                     axis=AX.X), "ve", "ve")
        vv = ctx.step(ctx.ve.tensor_scalar(corr[0:64, :], vsum[0:64, :], -SELU_LA, None,
                                           AL.mult), "ve")
        ctx.wait(ctx.ve, "ve", vv)
        ctx.wait(ctx.te, "ve", vv)
        for tt in range(8):
            ts = slice(tt * 512, (tt + 1) * 512)
            mm = ctx.te.matmul(P["C"][0:1, 0:512], kmb[0:64, :], QB[0:64, ts],
                               start=True, stop=True)
            te_mm = ctx.step(mm, "te")
            ctx.wait(ctx.ve, "te", te_mm)
            vv = ctx.step(ctx.ve.tensor_scalar(QB[64:65, ts], P["C"][0:1, 0:512],
                                               -1.0 / 4096.0, None, AL.mult), "ve")
            ctx.wait(ctx.ve, "ve", vv)
            ctx.wait(ctx.te, "ve", vv)
        for kc in range(32):
            bb = kc % 2
            if kc >= 2:
                ctx.wait(ctx.te, "ve", vb_done[kc - 2])
            tr = ctx.te.transpose(P["T"][0:128, bb * 1024:bb * 1024 + 64],
                                  XN[hb:hb + 64,
                                     (h // 2) * 4096 + kc * 128:(h // 2) * 4096 + (kc + 1) * 128],
                                  S["CID"][hb:hb + 64, hb:hb + 64])
            tr_v = ctx.step(tr, "te")
            ctx.wait(ctx.ve, "te", tr_v)
            if kc == 0:
                vb_done = {}
            vb_done[kc] = ctx.step(ctx.ve.tensor_copy(VB[:, kc * 64:(kc + 1) * 64],
                                                      P["T"][0:128, bb * 1024:bb * 1024 + 64]),
                                   "ve")
        ctx.wait(ctx.te, "ve")
        for qq in range(8):
            qs = slice(qq * 512, (qq + 1) * 512)
            ctx.wait(ctx.ve, "te")
            ctx.wait(ctx.se, "te")
            ctx.wait(ctx.gp, "te")
            pv_done = [None, None]
            e_done = None
            p_done = None
            for kc2 in range(16):
                sl = kc2 % 2
                ebuf = EB[:, sl * 1024:(sl + 1) * 1024]
                rbuf = RB[:, sl * 1024:(sl + 1) * 1024]
                tbuf = TB[:, sl * 1024:(sl + 1) * 1024]
                if e_done is not None:
                    ctx.wait(ctx.te, "se", e_done)
                    ctx.wait(ctx.te, "ve", r_done)
                for j in range(2):
                    kc = kc2 * 2 + j
                    mm = ctx.te.matmul(
                        P["B"][:, j * 512:(j + 1) * 512],
                        KB[0:65, kc * 128:(kc + 1) * 128],
                        QB[0:65, qs], start=True, stop=True)
                te_mm = ctx.step(mm, "te")
                ctx.wait(ctx.se, "te", te_mm)
                if pv_done[sl] is not None:
                    ctx.wait(ctx.se, "te", pv_done[sl])
                e_done = ctx.step(ctx.se.activation(ebuf, P["B"][:], AF.Exp,
                                                    scale=0.125), "se")
                ctx.wait(ctx.ve, "te", te_mm)
                ctx.wait(ctx.ve, "se", e_done)
                if pv_done[sl] is not None:
                    ctx.wait(ctx.ve, "te", pv_done[sl])
                r_done = chain(ctx, ctx.ve.tensor_scalar(rbuf, P["B"][:], 0.125 * SELU_L,
                                                         0.0, AL.mult, AL.max), "ve", "ve")
                ctx.wait(ctx.gp, "se", e_done)
                ctx.wait(ctx.gp, "ve", r_done)
                gv = ctx.step(ctx.gp.tensor_scalar(tbuf, ebuf, SELU_LA, SELU_LA,
                                                   AL.mult, AL.min), "gq")
                ctx.wait(ctx.ve, "gq", gv)
                p_done = chain(ctx, ctx.ve.tensor_tensor(tbuf, tbuf, rbuf, AL.add),
                               "ve", "ve")
                ctx.wait(ctx.te, "ve", p_done)
                for j in range(2):
                    kc = kc2 * 2 + j
                    mm = ctx.te.matmul(
                        P["D"][0:64, :],
                        VB[:, kc * 64:(kc + 1) * 64],
                        tbuf[:, j * 512:(j + 1) * 512],
                        start=(kc == 0), stop=(kc == 31))
                te_mm = ctx.step(mm, "te")
                pv_done[sl] = te_mm
            ctx.wait(ctx.ve, "te", te_mm)
            vv = ctx.step(ctx.ve.tensor_scalar(RB[0:64, 0:512], P["D"][0:64, :],
                                               corr[0:64, :], 0.015625, AL.add, AL.mult),
                          "ve")
            ctx.wait(ctx.ve, "ve", vv)
            ctx.wait(ctx.te, "ve", vv)
            for tb in range(4):
                tt = qq * 4 + tb
                tr = ctx.te.transpose(P["T"][0:128, tb * 128:tb * 128 + 64],
                                      RB[0:64, tb * 128:(tb + 1) * 128],
                                      S["CID"][0:64, 0:64])
                tr_v = ctx.step(tr, "te")
                ctx.wait(ctx.ve, "te", tr_v)
                xsl = X[:, tt * 1024 + h * 64: tt * 1024 + (h + 1) * 64]
                vv = ctx.step(ctx.ve.tensor_tensor(xsl, xsl,
                                                   P["T"][0:128, tb * 128:tb * 128 + 64],
                                                   AL.add), "ve")
            ctx.wait(ctx.te, "ve")
    barrier(ctx, engs=("te", "se", "gp", "sy"), sems=("ve", "gq"))


def emit_ffn_moe(ctx, A, S, P):
    X, AR, AFX = S["X"], S["AR"], S["AF"]
    XN = AR[:, 0:32 * 1024]
    w1_off = 32 * 1024
    w2_off = w1_off + 16384
    h_off = w2_off + 16384
    W1B = AR[:, w1_off:w1_off + 16384]
    W2B = AR[:, w2_off:w2_off + 16384]
    HB = AR[:, h_off:h_off + 4096]
    b2r = AR[0:1, h_off + 4096:h_off + 4096 + 1024]
    GTS = AFX[:, 256:512]
    LG = AFX[:, 212:220]
    m8 = AFX[:, 220:221]
    z8 = AFX[:, 221:222]
    b1c = AFX[:, 224:240]

    wr_s = AR[:, w1_off:w1_off + 64].rearrange("p (c e) -> p c e", e=8)
    brr = AR[0:1, w1_off + 64:w1_off + 72]
    jnk8 = HB[:, 0:8]
    vw = ctx.dma("gp", wr_s, A["l1_wr"].rearrange("(c p) e -> p c e", p=128), "wq")
    ctx.dma("gp", brr, A["l1_brr"], "wq")
    ctx.wait(ctx.te, "wq")
    barrier(ctx, engs=("te",), sems=("ve", "se"))
    for tt in range(32):
        psm = P["C"][:, 0:8] if tt % 2 == 0 else P["D"][:, 0:8]
        if tt >= 2:
            ctx.wait(ctx.te, "ve")
        for kc in range(8):
            mm = ctx.te.matmul(psm, XN[:, kc * 4096 + tt * 128:kc * 4096 + (tt + 1) * 128],
                               wr_s[:, kc, :], start=(kc == 0), stop=False)
        mm = ctx.te.matmul(psm, S["ONE"][:], brr, start=False, stop=True)
        te_mm = ctx.step(mm, "te")
        ctx.wait(ctx.ve, "te", te_mm)
        chain(ctx, ctx.ve.tensor_copy(LG, psm), "ve", "ve")
        chain(ctx, ctx.ve.reduce_max(m8, LG, axis=AX.X, negate=True), "ve", "ve")
        ctx.wait(ctx.se, "ve")
        sev = ctx.step(ctx.se.activation(jnk8, LG, AF.Exp,
                                         bias=m8, accum_out=z8), "se")
        ctx.wait(ctx.ve, "se", sev)
        chain(ctx, ctx.ve.reciprocal(z8, z8), "ve", "ve")
        chain(ctx, ctx.ve.tensor_scalar(m8, m8, -1.0, None, AL.mult), "ve", "ve")
        vv = ctx.step(ctx.ve.tensor_scalar(GTS[:, tt * 8:(tt + 1) * 8], LG, m8, z8,
                                           AL.is_ge, AL.mult), "ve")
        ctx.wait(ctx.ve, "ve", vv)
        ctx.wait(ctx.te, "ve", vv)

    w1r = A["l1_w1A"]
    w2r = A["l1_w2"].rearrange("e (c p) d -> e c p d", p=128)
    te_free = None
    ctx.wait(ctx.gp, "te")
    for e in range(8):
        if te_free is not None:
            ctx.wait(ctx.gp, "te", te_free)
        for hf in range(16):
            ctx.dma("gp", W1B[:, hf * 1024:(hf + 1) * 1024], w1r[e, hf], "wq")
        for kc in range(16):
            ctx.dma("gp", W2B[:, kc * 1024:(kc + 1) * 1024], w2r[e, kc], "wq")
        w_v = ctx.dma("gp", b2r, A["l1_b2r"][e], "wq")
        bv = ctx.dma("sy", b1c, A["l1_b1c"][e], "xq")
        ctx.wait(ctx.te, "wq", w_v)
        ctx.wait(ctx.se, "xq", bv)
        for t2 in range(16):
            for hf in range(16):
                psm = P["A"][:, (hf % 2) * 512:(hf % 2) * 512 + 256]
                if hf >= 2 or t2 > 0:
                    ctx.wait(ctx.te, "se", ctx.cnt["se"] - 1)
                for kc in range(8):
                    mm = ctx.te.matmul(psm,
                                       W1B[:, hf * 1024 + kc * 128: hf * 1024 + (kc + 1) * 128],
                                       XN[:, kc * 4096 + t2 * 256:kc * 4096 + (t2 + 1) * 256],
                                       start=(kc == 0), stop=(kc == 7))
                te_mm = ctx.step(mm, "te")
                te_free = te_mm
                ctx.wait(ctx.se, "te", te_mm)
                sev = ctx.step(ctx.se.activation(HB[:, hf * 256:(hf + 1) * 256], psm,
                                                 AF.Relu, bias=b1c[:, hf:hf + 1]), "se")
            ctx.wait(ctx.te, "se")
            for u in range(2):
                t128 = t2 * 2 + u
                for dh in range(2):
                    psm = P["B"][:, dh * 512:(dh + 1) * 512]
                    for kc in range(16):
                        mm = ctx.te.matmul(psm,
                                           HB[:, kc * 256 + u * 128: kc * 256 + u * 128 + 128],
                                           W2B[:, kc * 1024 + dh * 512: kc * 1024 + (dh + 1) * 512],
                                           start=(kc == 0), stop=False)
                    mm = ctx.te.matmul(psm, S["ONE"][:], b2r[:, dh * 512:(dh + 1) * 512],
                                       start=False, stop=True)
                    te_mm = ctx.step(mm, "te")
                    ctx.wait(ctx.ve, "te", te_mm)
                    xsl = X[:, t128 * 1024 + dh * 512: t128 * 1024 + (dh + 1) * 512]
                    vv = ctx.step(ctx.ve.scalar_tensor_tensor(
                        xsl, psm, GTS[:, t128 * 8 + e: t128 * 8 + e + 1], xsl,
                        AL.mult, AL.add), "ve")
                    ctx.wait(ctx.ve, "ve", vv)
                ctx.wait(ctx.te, "ve")
        te_free = ctx.cnt["te"]
    barrier(ctx, engs=("te", "se", "gp", "sy"), sems=("ve",))


def emit_attn_l0(ctx, A, S, P):
    X, AR, AFX = S["X"], S["AR"], S["AF"]
    XN = AR[:, 0:32 * 1024]
    lt_off = 32 * 1024
    kb_off = lt_off + 16384
    vb_off = kb_off + 4096
    qb_off = vb_off + 2080
    yt_off = qb_off + 1024
    pt_off = yt_off + 8192
    r2_off = pt_off + 2048
    wb_off = r2_off + 1408

    wbuf = WStream(ctx, AR[:, wb_off:wb_off + 2048], 1024)
    LT = AR[:, lt_off:lt_off + 16384]
    KB = AR[:, kb_off:kb_off + 4096]
    VB = AR[:, vb_off:vb_off + 2080]
    QT = AR[:, qb_off:qb_off + 512]
    QTs = AR[:, qb_off + 512:qb_off + 1024]
    YT = AR[0:64, yt_off:yt_off + 8192]
    PT2 = AR[:, pt_off:pt_off + 2048]
    RBC = AR[0:64, r2_off:r2_off + 512]
    rrow = AR[64:65, r2_off + 512:r2_off + 1024]
    kmb = AR[:, r2_off + 1024:r2_off + 1025]
    qmb = AR[:, r2_off + 1026:r2_off + 1027]
    br2 = AR[0:1, r2_off + 1028:r2_off + 1030]
    qrow_t = AR[0:1, r2_off + 1030:r2_off + 1158]
    kr1 = AR[0:1, r2_off + 1158:r2_off + 1286]
    oneb = AR[:, r2_off + 1288:r2_off + 1352]
    kmaxf = AFX[:, 170:171]
    qmaxf = AFX[:, 172:173]
    negb = AFX[:, 184:186]
    recf = AFX[64:65, 192:192 + 512]
    bbf = AFX[0:1, 704:706]
    vv = ctx.step(ctx.ve.memset(oneb, 1.0), "ve")

    te_free = None
    for oc in range(4):
        iw, vw = wbuf.load(A["l0_cdA"][oc], free_after=te_free)
        ctx.wait(ctx.te, "wq", vw)
        for tt in range(8):
            ts = slice(tt * 512, (tt + 1) * 512)
            psm = P["A"][:, (tt % 2) * 512:(tt % 2) * 512 + 512]
            if tt >= 2:
                ctx.wait(ctx.te, "se", ctx.cnt["se"] - 1)
            for kc in range(8):
                sl = wbuf.slot(iw)
                mm = ctx.te.matmul(psm, sl[:, kc * 128:kc * 128 + 128],
                                   XN[:, kc * 4096 + ts.start:kc * 4096 + ts.stop],
                                   start=(kc == 0), stop=(kc == 7))
            te_mm = ctx.step(mm, "te")
            te_free = te_mm
            ctx.wait(ctx.se, "te", te_mm)
            ctx.step(ctx.se.activation(LT[:, oc * 4096 + ts.start:oc * 4096 + ts.stop],
                                       psm, AF.Copy), "se")
    barrier(ctx, engs=("te", "ve"), sems=("se",))

    if te_free is not None:
        ctx.wait(ctx.gp, "te", te_free)
    kwsl = AR[:, wb_off:wb_off + 256]
    vw = ctx.dma("gp", kwsl, A["l0_kA"][0][:, 0:256], "wq")
    vwsl = AR[:, wb_off + 256:wb_off + 512]
    vvw = ctx.dma("gp", vwsl.rearrange("p (c d) -> p c d", d=64),
                  A["l0_v"].rearrange("(c p) d -> p c d", p=128), "wq")
    ov = VB.rearrange("p (g u) -> p g u", u=65)
    vv_one = ctx.step(ctx.ve.memset(ov[:, :, 64:65], 1.0), "ve")
    ctx.wait(ctx.te, "wq", vvw)
    ctx.wait(ctx.te, "ve", vv_one)
    for tt in range(8):
        ts = slice(tt * 512, (tt + 1) * 512)
        psm = P["A"] if tt % 2 == 0 else P["B"]
        if tt >= 2:
            ctx.wait(ctx.te, "se", ctx.cnt["se"] - 2)
        for kc in range(4):
            mm = ctx.te.matmul(psm[0:64, 0:512], kwsl[:, kc * 64:(kc + 1) * 64],
                               LT[:, kc * 4096 + ts.start:kc * 4096 + ts.stop],
                               start=(kc == 0), stop=(kc == 3))
        ctx.step(mm, "te")
        for kc in range(4):
            mm = ctx.te.matmul(psm[64:128, 0:512], kwsl[:, kc * 64:(kc + 1) * 64],
                               LT[:, kc * 4096 + ts.start:kc * 4096 + ts.stop],
                               start=(kc == 0), stop=(kc == 3))
        te_mm = ctx.step(mm, "te")
        te_free = te_mm
        ctx.wait(ctx.se, "te", te_mm)
        ctx.step(ctx.se.activation(KB[0:64, ts], psm[0:64, 0:512], AF.Copy), "se")
        ctx.step(ctx.se.activation(KB[64:128, ts], psm[64:128, 0:512], AF.Copy), "se")
    for tt2 in range(8):
        psm = P["C"] if tt2 % 2 == 0 else P["D"]
        if tt2 >= 2:
            ctx.wait(ctx.te, "se", ctx.cnt["se"] - 4)
        for u in range(4):
            t128 = tt2 * 4 + u
            us = slice(u * 64, (u + 1) * 64)
            for kc in range(4):
                mm = ctx.te.matmul(psm[:, us],
                                   LT[:, kc * 4096 + t128 * 128:kc * 4096 + (t128 + 1) * 128],
                                   vwsl[:, kc * 64:(kc + 1) * 64],
                                   start=(kc == 0), stop=(kc == 3))
        te_mm = ctx.step(mm, "te")
        te_free = te_mm
        ctx.wait(ctx.se, "te", te_mm)
        for u in range(4):
            t128 = tt2 * 4 + u
            us = slice(u * 64, (u + 1) * 64)
            ctx.step(ctx.se.activation(VB[:, t128 * 65:t128 * 65 + 64], psm[:, us],
                                       AF.Copy), "se")
    ctx.wait(ctx.ve, "se")
    chain(ctx, ctx.ve.reduce_max(kmaxf, KB[:, :], axis=AX.X,
                                 apply_absolute_value=True), "ve", "ve")
    vv = ctx.step(ctx.ve.tensor_copy(kmb, kmaxf), "ve")
    ctx.wait(ctx.ve, "ve", vv)
    ctx.wait(ctx.te, "ve", vv)
    tev = ctx.step(ctx.te.matmul(P["C"][0:1, 0:128], kmb, S["CID"][:],
                                 start=True, stop=True), "te")
    ctx.wait(ctx.ve, "te", tev)
    vv = ctx.step(ctx.ve.tensor_copy(kr1, P["C"][0:1, 0:128]), "ve")
    ctx.wait(ctx.ve, "ve", vv)
    ctx.wait(ctx.te, "ve", vv)
    barrier(ctx, engs=("te", "ve", "se"), sems=("ve", "se", "te"))

    for qq in range(8):
        qs = slice(qq * 512, (qq + 1) * 512)
        nkc = 4 * qq + 4
        for pr in range(8):
            iw, vw = wbuf.load(A["l0_quA"][pr], free_after=te_free)
            ctx.wait(ctx.te, "wq", vw)
            ctx.wait(ctx.te, "ve")
            for kc in range(4):
                mm = ctx.te.matmul(P["D"][:],
                                   wbuf.slot(iw)[:, kc * 128:(kc + 1) * 128],
                                   LT[:, kc * 4096 + qs.start:kc * 4096 + qs.stop],
                                   start=(kc == 0), stop=(kc == 3))
            te_mm = ctx.step(mm, "te")
            te_free = te_mm
            ctx.wait(ctx.se, "te", te_mm)
            sev = ctx.step(ctx.se.activation(QT, P["D"][:], AF.Copy), "se")
            ctx.wait(ctx.ve, "se", sev)
            chain(ctx, ctx.ve.reduce_max(qmaxf, QT, axis=AX.X,
                                         apply_absolute_value=True), "ve", "ve")
            vv = ctx.step(ctx.ve.tensor_copy(qmb, qmaxf), "ve")
            ctx.wait(ctx.ve, "ve", vv)
            ctx.wait(ctx.te, "ve", vv)
            tev = ctx.step(ctx.te.matmul(P["C"][0:1, 128:256], qmb, S["CID"][:],
                                         start=True, stop=True), "te")
            ctx.wait(ctx.ve, "te", tev)
            chain(ctx, ctx.ve.tensor_copy(qrow_t, P["C"][0:1, 128:256]), "ve", "ve")
            for hh in range(2):
                chain(ctx, ctx.ve.tensor_tensor(
                    RBC[0:1, hh * 64:hh * 64 + 64],
                    qrow_t[:, hh * 64:hh * 64 + 64],
                    kr1[:, 0:64], AL.mult), "ve", "ve")
                chain(ctx, ctx.ve.reduce_sum(bbf[:, hh:hh + 1],
                                             RBC[0:1, hh * 64:hh * 64 + 64],
                                             axis=AX.X), "ve", "ve")
            vv = ctx.step(ctx.ve.tensor_scalar(br2, bbf, -1.0 / 16.0, None, AL.mult), "ve")
            ctx.wait(ctx.ve, "ve", vv)
            ctx.wait(ctx.te, "ve", vv)
            mm = ctx.te.matmul(P["C"][:, 0:2], S["ONE"][:], br2, start=True, stop=True)
            te_mm = ctx.step(mm, "te")
            ctx.wait(ctx.ve, "te", te_mm)
            vv_negb = ctx.step(ctx.ve.tensor_copy(negb, P["C"][:, 0:2]), "ve")
            for hh in range(2):
                h = 2 * pr + hh
                qrow = QT[64 * hh:64 * hh + 64, :]
                ysl = YT[:, h * 512:(h + 1) * 512]
                pv_done = [None, None]
                exp_done = None
                gp_done = None
                for kc2 in range((nkc + 1) // 2):
                    sl = kc2 % 2
                    pslot = PT2[:, sl * 1024:(sl + 1) * 1024]
                    njj = min(2, nkc - kc2 * 2)
                    if exp_done is not None:
                        ctx.wait(ctx.te, "se", exp_done)
                    for j in range(njj):
                        kc = kc2 * 2 + j
                        mm = ctx.te.matmul(
                            P["A"][:, j * 512:(j + 1) * 512],
                            KB[64 * hh:64 * hh + 64, kc * 128:(kc + 1) * 128],
                            qrow, start=True, stop=True)
                    te_mm = ctx.step(mm, "te")
                    ctx.wait(ctx.se, "te", te_mm)
                    if pv_done[sl] is not None:
                        ctx.wait(ctx.se, "te", pv_done[sl])
                    if gp_done is not None:
                        ctx.wait(ctx.se, "gq", gp_done)
                    if kc2 == 0:
                        ctx.wait(ctx.se, "ve", vv_negb)
                    exp_done = ctx.step(ctx.se.activation(
                        pslot[:, 0:512 * njj], P["A"][:, 0:512 * njj], AF.Exp,
                        scale=0.125, bias=negb[:, hh:hh + 1]), "se")
                    need_mask = [j for j in range(njj) if (kc2 * 2 + j) >= 4 * qq]
                    if need_mask:
                        ctx.wait(ctx.gp, "se", exp_done)
                        for j in need_mask:
                            kc = kc2 * 2 + j
                            gp_done = ctx.step(ctx.gp.affine_select(
                                pslot[:, j * 512:(j + 1) * 512],
                                pslot[:, j * 512:(j + 1) * 512],
                                pattern=[[1, 512]], compare_op=AL.is_ge,
                                fill=0.0, base=qq * 512 - kc * 128,
                                channel_multiplier=-1), "gq")
                        ctx.wait(ctx.te, "gq", gp_done)
                    ctx.wait(ctx.te, "se", exp_done)
                    for j in range(njj):
                        kc = kc2 * 2 + j
                        mm = ctx.te.matmul(
                            P["D"][0:65, :],
                            VB[:, kc * 65:(kc + 1) * 65],
                            pslot[:, j * 512:(j + 1) * 512],
                            start=(kc == 0), stop=(kc == nkc - 1))
                    te_mm = ctx.step(mm, "te")
                    pv_done[sl] = te_mm
                ctx.wait(ctx.ve, "te", te_mm)
                chain(ctx, ctx.ve.tensor_scalar(P["D"][64:65, :], P["D"][64:65, :],
                                                1e-30, None, AL.add), "ve", "ve")
                chain(ctx, ctx.ve.reciprocal(recf, P["D"][64:65, :]), "ve", "ve")
                vv = ctx.step(ctx.ve.tensor_copy(rrow, recf), "ve")
                ctx.wait(ctx.ve, "ve", vv)
                ctx.wait(ctx.te, "ve", vv)
                mm = ctx.te.matmul(P["C"][0:64, :], oneb[64:65, :], rrow,
                                   start=True, stop=True)
                te_mm = ctx.step(mm, "te")
                ctx.wait(ctx.se, "te", te_mm)
                sev = ctx.step(ctx.se.activation(RBC, P["C"][0:64, :], AF.Copy), "se")
                ctx.wait(ctx.ve, "se", sev)
                vv = ctx.step(ctx.ve.tensor_tensor(ysl, P["D"][0:64, :], RBC, AL.mult),
                              "ve")
                ctx.wait(ctx.te, "ve", vv)
        ctx.wait(ctx.te, "ve")
        cp_r = A["l0_cp"].rearrange("(c p) d -> c p d", p=64)
        te_free = ctx.cnt["te"]
        for t2 in range(2):
            psl = [(P["A"], 0), (P["A"], 512), (P["B"], 0), (P["B"], 512)]
            for yc in range(16):
                ctx.wait(ctx.gp, "te", te_free)
                wsl = wbuf.slot(yc)[0:64, :]
                vw = ctx.dma("gp", wsl, cp_r[yc], "wq")
                ctx.wait(ctx.te, "wq", vw)
                for u in range(2):
                    t4 = t2 * 2 + u
                    for dh in range(2):
                        ps, off = psl[u * 2 + dh]
                        mm = ctx.te.matmul(
                            ps[:, off:off + 512],
                            YT[:, yc * 512 + t4 * 128: yc * 512 + (t4 + 1) * 128],
                            wsl[:, dh * 512:(dh + 1) * 512],
                            start=(yc == 0), stop=(yc == 15))
                te_mm = ctx.step(mm, "te")
                te_free = te_mm
            for u in range(2):
                t4 = t2 * 2 + u
                tt = qq * 4 + t4
                for dh in range(2):
                    ps, off = psl[u * 2 + dh]
                    ctx.wait(ctx.ve, "te", te_free)
                    xsl = X[:, tt * 1024 + dh * 512: tt * 1024 + (dh + 1) * 512]
                    ctx.step(ctx.ve.tensor_tensor(xsl, xsl, ps[:, off:off + 512],
                                                  AL.add), "ve")
            ctx.wait(ctx.te, "ve")
    barrier(ctx, engs=("te", "se", "gp", "sy"), sems=("ve", "gq"))


def emit_l0_down(ctx, A, S, P, q4, UP, wbuf, te_free):
    X, AR = S["X"], S["AR"]
    dn_r = A["l0_dn"].rearrange("(c p) d -> c p d", p=128)
    wb_off2 = 64 * 1024
    slab_use = {}
    for s in range(4):
        slab_use[s] = (None, ctx.cnt["te"])
    for dh in range(2):
        for grp in range(2):
            for yc in range(32):
                s4 = yc % 4
                ctx.wait(ctx.gp, "te", slab_use[s4][1])
                wsl = AR[:, wb_off2 + s4 * 512: wb_off2 + (s4 + 1) * 512]
                vw = ctx.dma("gp", wsl, dn_r[yc][:, dh * 512:(dh + 1) * 512], f"w{s4}")
                ctx.wait(ctx.te, f"w{s4}", vw)
                for u in range(4):
                    tloc = grp * 4 + u
                    ps, off = [(P["A"], 0), (P["A"], 512), (P["B"], 0), (P["B"], 512)][u]
                    mm = ctx.te.matmul(
                        ps[:, off:off + 512],
                        UP[:, yc * 1024 + tloc * 128: yc * 1024 + (tloc + 1) * 128],
                        wsl, start=(yc == 0), stop=(yc == 31))
                te_mm = ctx.step(mm, "te")
                te_free = te_mm
                slab_use[s4] = (vw, te_mm)
            for u in range(4):
                tloc = grp * 4 + u
                tt = q4 * 8 + tloc
                ps, off = [(P["A"], 0), (P["A"], 512), (P["B"], 0), (P["B"], 512)][u]
                ctx.wait(ctx.ve, "te", te_free)
                xsl = X[:, tt * 1024 + dh * 512: tt * 1024 + (dh + 1) * 512]
                ctx.step(ctx.ve.tensor_tensor(xsl, xsl, ps[:, off:off + 512], AL.add),
                         "ve")
            ctx.wait(ctx.te, "ve")
    return te_free


def emit_ffn_peer(ctx, A, S, P):
    X, AR, AFX = S["X"], S["AR"], S["AF"]
    XN = AR[:, 0:32 * 1024]
    up_off = 32 * 1024
    wb_off = 66 * 1024
    wbuf = WStream(ctx, AR[:, wb_off:wb_off + 2048], 1024)
    UP = AR[:, up_off:up_off + 32768]

    barrier(ctx, engs=("te", "ve", "se", "gp"), sems=("ve", "se", "te"))
    te_free = None
    for q4 in range(4 if L0_DENSE else 0):
        tq = q4 * 1024
        for oc in range(32):
            iw, vw = wbuf.load(A["l0_upA"][oc], free_after=te_free)
            ctx.wait(ctx.te, "wq", vw)
            for hh in range(2):
                hs = slice(hh * 512, (hh + 1) * 512)
                for kc in range(8):
                    mm = ctx.te.matmul(P["C"][:] if hh == 0 else P["D"][:],
                                       wbuf.slot(iw)[:, kc * 128:(kc + 1) * 128],
                                       XN[:, kc * 4096 + tq + hs.start:
                                          kc * 4096 + tq + hs.stop],
                                       start=(kc == 0), stop=(kc == 7))
            te_mm = ctx.step(mm, "te")
            te_free = te_mm
            ctx.wait(ctx.se, "te", te_mm)
            sev = ctx.step(ctx.se.activation(UP[:, oc * 1024:oc * 1024 + 512], P["C"][:],
                                             AF.Silu), "se")
            sev = ctx.step(ctx.se.activation(UP[:, oc * 1024 + 512:(oc + 1) * 1024],
                                             P["D"][:], AF.Silu), "se")
            ctx.wait(ctx.te, "se", sev)
        if DBG and q4 == 0:
            ctx.wait(ctx.gp, "se")
            ctx.dma("gp", A["dbg_xn"], UP, "so")
            ctx.wait(ctx.gp, "so")
        te_free = emit_l0_down(ctx, A, S, P, q4, UP, wbuf, te_free)
    barrier(ctx, engs=("te", "ve", "se", "gp", "sy"), sems=("ve", "se", "te", "wq"))

    dt_off = 32 * 1024
    gt_off = dt_off + 8192
    g_off = gt_off + 8192
    b1_off = g_off + 8192
    b2_off = b1_off + 1024
    b3_off = b2_off + 1024
    qp_off = b3_off + 1024
    ky_off = qp_off + 512
    ex_off = ky_off + 512
    DOTS = AR[:, dt_off:dt_off + 8192]
    GT = AR[:, gt_off:gt_off + 8192]
    G = AR[:, g_off:g_off + 4096]
    B1 = AR[:, b1_off:b1_off + 1024]
    B2 = AR[:, b2_off:b2_off + 1024]
    B3 = AR[:, b3_off:b3_off + 1024]
    QP = AR[:, qp_off:qp_off + 512]
    KEYS = AR[:, ky_off:ky_off + 512]
    EXH = AR[:, ex_off:ex_off + 64]
    SIM = AFX[:, 0:512]
    T8X = AFX[:, 512:520]
    T8Y = AFX[:, 520:528]
    T8G = AFX[:, 528:536]
    GRD = AFX[:, 544:608]
    E8 = AFX[:, 608:616]
    ZS = AFX[:, 616:617]
    negmx = AFX[:, 617:618]
    ctx.dma("sy", KEYS, A["l0_keys"], "xq")
    ctx.wait(ctx.te, "xq")
    deT_r = A["l0_deT"]
    ue_r = A["l0_ue"]
    qw_r = A["l0_qA"]

    te_free = None
    slab_use = {}
    for t128 in range(32 if L0_PEER else 0):
        tcol = t128 * 128
        for s in range(4):
            slab_use[s] = (None, ctx.cnt["te"])
        for ec in range(8):
            psm = P["C"] if ec % 2 == 0 else P["D"]
            for kc in range(8):
                i = ec * 8 + kc
                s4 = i % 4
                if s4 in slab_use:
                    ctx.wait(ctx.gp, "te", slab_use[s4][1])
                wsl = AR[:, wb_off + s4 * 512: wb_off + (s4 + 1) * 512]
                vw = ctx.dma("gp", wsl,
                             deT_r[kc * 128:(kc + 1) * 128, ec * 512:(ec + 1) * 512],
                             f"w{s4}")
                ctx.wait(ctx.te, f"w{s4}", vw)
                mm = ctx.te.matmul(psm[:, 0:512],
                                   XN[:, kc * 4096 + tcol: kc * 4096 + tcol + 128],
                                   wsl, start=(kc == 0), stop=(kc == 7))
                te_mm = ctx.step(mm, "te")
                te_free = te_mm
                slab_use[s4] = (vw, te_mm)
            ctx.wait(ctx.se, "te", te_free)
            sev = ctx.step(ctx.se.activation(DOTS[:, ec * 512:(ec + 1) * 512],
                                             psm[:, 0:512], AF.Copy), "se")
            ctx.wait(ctx.te, "se", sev)
        for oc in range(4 if L0_PSTG >= 2 else 0):
            iw, vw = wbuf.load(qw_r[oc], free_after=te_free)
            ctx.wait(ctx.te, "wq", vw)
            psm = P["A"][:, (oc % 2) * 512:(oc % 2) * 512 + 128]
            if oc >= 2:
                ctx.wait(ctx.te, "se", ctx.cnt["se"] - 1)
            for kc in range(8):
                mm = ctx.te.matmul(psm, wbuf.slot(iw)[:, kc * 128:(kc + 1) * 128],
                                   XN[:, kc * 4096 + tcol: kc * 4096 + tcol + 128],
                                   start=(kc == 0), stop=(kc == 7))
            te_mm = ctx.step(mm, "te")
            te_free = te_mm
            ctx.wait(ctx.se, "te", te_mm)
            sev = ctx.step(ctx.se.activation(QP[:, oc * 128:(oc + 1) * 128], psm,
                                             AF.Copy), "se")
        if L0_PSTG >= 2:
            ctx.wait(ctx.te, "se")
            slots = [(P["A"], 0), (P["A"], 512), (P["B"], 0), (P["B"], 512)]
            for w in range(2):
                for i in range(4):
                    blk = w * 4 + i
                    p, hq = blk // 4, blk % 4
                    hb = 64 * (hq % 2)
                    cc = 2 * p + hq // 2
                    ps, off = slots[i]
                    mm = ctx.te.matmul(ps[:, off:off + 64],
                                       QP[hb:hb + 64, cc * 128:(cc + 1) * 128],
                                       KEYS[hb:hb + 64, blk * 64:(blk + 1) * 64],
                                       start=True, stop=True)
                te_mm = ctx.step(mm, "te")
                ctx.wait(ctx.se, "te", te_mm)
                for i in range(4):
                    blk = w * 4 + i
                    ps, off = slots[i]
                    sev = ctx.se.activation(SIM[:, blk * 64:(blk + 1) * 64],
                                            ps[:, off:off + 64], AF.Copy)
                sev = ctx.step(sev, "se")
                ctx.wait(ctx.te, "se", sev)
            ctx.wait(ctx.ve, "se", sev)
        for hq in range(4 if L0_PSTG >= 3 else 0):
            xs = slice(hq * 64, (hq + 1) * 64)
            ys = slice(256 + hq * 64, 256 + (hq + 1) * 64)
            chain(ctx, ctx.ve.max(out=T8X, in_=SIM[:, xs]), "ve", "ve")
            chain(ctx, ctx.ve.max(out=T8Y, in_=SIM[:, ys]), "ve", "ve")
            g3 = GRD.rearrange("p (i j) -> p i j", i=8)
            chain(ctx, ctx.ve.tensor_copy(g3, T8Y.unsqueeze(1).to_broadcast([128, 8, 8])),
                  "ve", "ve")
            chain(ctx, ctx.ve.tensor_tensor(g3, g3,
                                            T8X.unsqueeze(2).to_broadcast([128, 8, 8]),
                                            AL.add), "ve", "ve")
            chain(ctx, ctx.ve.max(out=T8G, in_=GRD), "ve", "ve")
            vv = ctx.step(ctx.ve.tensor_scalar(negmx, T8G[:, 0:1], -1.0, None, AL.mult),
                          "ve")
            ctx.wait(ctx.ve, "ve", vv)
            ctx.wait(ctx.se, "ve", vv)
            chain(ctx, ctx.se.activation(E8, T8G, AF.Exp, bias=negmx, accum_out=ZS),
                  "se", "se")
            sev = ctx.step(ctx.se.activation(EXH, SIM[:, xs], AF.Exp, bias=negmx), "se")
            ctx.wait(ctx.ve, "se", sev)
            chain(ctx, ctx.ve.reciprocal(ZS, ZS), "ve", "ve")
            for ech in range(4):
                es = slice(ech * 1024, (ech + 1) * 1024)
                sev = ctx.step(ctx.se.activation(
                    B1.rearrange("p (i j) -> p i j", j=64),
                    SIM[:, ys].unsqueeze(1).to_broadcast([128, 16, 64]), AF.Exp), "se")
                ctx.wait(ctx.ve, "se", sev)
                exv = EXH[:, ech * 16:(ech + 1) * 16]
                chain(ctx, ctx.ve.tensor_tensor(
                    B2.rearrange("p (i j) -> p i j", j=64),
                    B1.rearrange("p (i j) -> p i j", j=64),
                    exv.unsqueeze(2).to_broadcast([128, 16, 64]), AL.mult), "ve", "ve")
                chain(ctx, ctx.ve.scalar_tensor_tensor(B3, B2, E8[:, 7:8], B2,
                                                       AL.is_ge, AL.mult), "ve", "ve")
                chain(ctx, ctx.ve.scalar_tensor_tensor(B1, DOTS[:, es], ZS, B3,
                                                       AL.mult, AL.mult), "ve", "ve")
                ctx.wait(ctx.se, "ve")
                sev = ctx.step(ctx.se.activation(B2, B1, AF.Tanh, scale=0.5), "se")
                ctx.wait(ctx.ve, "se", sev)
                chain(ctx, ctx.ve.scalar_tensor_tensor(B3, B2, 1.0, B1, AL.add, AL.mult),
                      "ve", "ve")
                if hq == 0:
                    vv = ctx.step(ctx.ve.tensor_scalar(G[:, es], B3, 0.5, None, AL.mult),
                                  "ve")
                else:
                    vv = ctx.step(ctx.ve.scalar_tensor_tensor(G[:, es], B3, 0.5,
                                                              G[:, es], AL.mult, AL.add),
                                  "ve")
                ctx.wait(ctx.ve, "ve", vv)
                ctx.wait(ctx.se, "ve", vv)
        ctx.wait(ctx.te, "ve")
        for half in range(4 if L0_PSTG >= 4 else 0):
            bb = half % 2
            if half >= 2:
                ctx.wait(ctx.te, "ve", gt_ve[half - 2])
            for blk in range(8):
                c = half * 8 + blk
                tr = ctx.te.transpose(P["T"][:, bb * 1024 + blk * 128: bb * 1024 + (blk + 1) * 128],
                                      G[:, c * 128:(c + 1) * 128], S["CID"][:])
            tr_v = ctx.step(tr, "te")
            ctx.wait(ctx.ve, "te", tr_v)
            for blk in range(8):
                c = half * 8 + blk
                cp = ctx.ve.tensor_copy(GT[:, c * 128:(c + 1) * 128],
                                        P["T"][:, bb * 1024 + blk * 128: bb * 1024 + (blk + 1) * 128])
            if half == 0:
                gt_ve = {}
            gt_ve[half] = ctx.step(cp, "ve")
        ctx.wait(ctx.te, "ve")
        for s in range(4):
            slab_use[s] = (None, ctx.cnt["te"])
        for dc in range(2 if L0_PSTG >= 5 else 0):
            psm = P["A"] if dc == 0 else P["B"]
            for kc in range(32):
                i = dc * 32 + kc
                s4 = i % 4
                if s4 in slab_use:
                    ctx.wait(ctx.gp, "te", slab_use[s4][1])
                wsl = AR[:, wb_off + s4 * 512: wb_off + (s4 + 1) * 512]
                vw = ctx.dma("gp", wsl,
                             ue_r[kc * 128:(kc + 1) * 128, dc * 512:(dc + 1) * 512], f"w{s4}")
                ctx.wait(ctx.te, f"w{s4}", vw)
                mm = ctx.te.matmul(psm[:, 0:512], GT[:, kc * 128:(kc + 1) * 128],
                                   wsl, start=(kc == 0), stop=(kc == 31))
                te_mm = ctx.step(mm, "te")
                te_free = te_mm
                slab_use[s4] = (vw, te_mm)
            ctx.wait(ctx.ve, "te", te_free)
            xsl = X[:, t128 * 1024 + dc * 512: t128 * 1024 + (dc + 1) * 512]
            vv = ctx.step(ctx.ve.tensor_tensor(xsl, xsl, psm[:, 0:512], AL.add), "ve")
            ctx.wait(ctx.te, "ve", vv)
    barrier(ctx, engs=("te", "se", "gp", "sy"), sems=("ve", "se"))


ATTN = {0: emit_attn_l0, 1: emit_attn_l1, 2: emit_attn_l2, 3: emit_attn_l3}
FFN = {0: emit_ffn_peer, 1: emit_ffn_moe, 2: make_ffn_ternary(2), 3: make_ffn_ternary(3)}



def kernel(**inputs):
    shared = host_prepare(inputs)
    x = np.ascontiguousarray(np.asarray(inputs["L0_attn_x"], dtype=np.float32))
    B = x.shape[0]
    cur = [np.ascontiguousarray(x[b]) for b in range(B)]
    if FUSED:
        groups = [list(BLOCKS)]
    else:
        groups = [[li] for li in BLOCKS]
    for gblocks in groups:
        nc = build_program(gblocks, shared)
        in_maps = [dict(shared, x_in=cur[b]) for b in range(B)]
        res = run_bass_kernel_spmd(nc, in_maps, core_ids=list(range(B)))
        cur = [np.asarray(res.results[b]["x_out"], dtype=np.float32) for b in range(B)]
    return np.stack(cur).reshape(B, T, D)
